# version 13
# baseline (speedup 1.0000x reference)
import numpy as np
import ml_dtypes
from contextlib import ExitStack
import concourse.bass as bass
import concourse.mybir as mybir
from concourse.bass_utils import run_bass_kernel_spmd

F32 = mybir.dt.float32
BF16 = mybir.dt.bfloat16
AF = mybir.ActivationFunctionType
ALU = mybir.AluOpType
AX = mybir.AxisListType.X
BF = ml_dtypes.bfloat16

NCORES = 8
B = 2
S = 8192
D = 1024
TOK = 2048
NJ = 8
NTT = TOK // 128
NC4 = TOK // 512
NKT = S // 128
LN_EPS = 1e-5

CH2ENG = {"T": "tensor", "V": "vector", "S": "scalar", "G": "gpsimd",
          "Y": "sync", "Gq": "gpsimd", "Yq": "sync", "Sq": "scalar"}
DMACH = ("Gq", "Yq", "Sq")


class MK:

    def __init__(self):
        self.nc = bass.Bass("TRN2", target_bir_lowering=False)
        self.es = ExitStack()
        self.ops = {e: [] for e in ("tensor", "vector", "scalar", "gpsimd", "sync")}
        self.cnt = {c: 0 for c in CH2ENG}
        self.sems = {}
        self.regions = {}
        self.marks = {}

    def dram(self, name, shape, dtype, kind="Internal"):
        return self.nc.dram_tensor(name, shape, dtype, kind=kind)

    def sbuf(self, name, shape, dtype):
        return self.es.enter_context(self.nc.sbuf_tensor(name, shape, dtype))

    def psum_pool(self):
        self.pool = self.es.enter_context(self.nc.psum_tensor("pool", [128, 4096], F32))
        return self.pool

    def init_arena(self, cols_bf16=104448):
        self.arena = self.sbuf("ARENA", [128, cols_bf16], BF16)
        self.acols = cols_bf16
        self.aoff = 0

    def asbuf(self, name, shape, dtype):
        cols = shape[1] * (2 if dtype == F32 else 1)
        cols = (cols + 31) & ~31
        assert self.aoff + cols <= self.acols, (name, self.aoff, cols)
        ap = self.arena[:, self.aoff:self.aoff + cols]
        self.aoff += cols
        if dtype == F32:
            ap = ap.bitcast(F32)
        return ap[0:shape[0], 0:shape[1]]

    def arena_reset(self):
        self.barrier()
        self.aoff = 0

    def pb(self, bank, width=512, rows=128, off=0):
        return self.pool[0:rows, bank * 512 + off: bank * 512 + off + width]

    def _sem(self, ch):
        if ch not in self.sems:
            self.sems[ch] = self.es.enter_context(self.nc.semaphore("sem_" + ch))
        return self.sems[ch]

    def emit(self, ch, fn, reads=(), writes=(), ndma=1):
        eng = CH2ENG[ch]
        is_dma = ch in DMACH
        waits = {}

        def dep(d):
            if d is None:
                return
            dch, dval = d
            if dch == "T" and ch == "T":
                return
            if self.marks.get((eng, dch), 0) >= dval:
                return
            waits[dch] = max(waits.get(dch, 0), dval)

        for k in reads:
            r = self.regions.get(k)
            if r is not None:
                dep(r[0])
        for k in writes:
            r = self.regions.get(k)
            if r is not None:
                dep(r[0])
                for rd in r[1]:
                    dep(rd)
        for dch, dval in waits.items():
            self.marks[(eng, dch)] = dval
        inc_total = (16 * ndma) if is_dma else (0 if fn is None else 1)
        self.cnt[ch] += inc_total
        val = self.cnt[ch]
        for k in writes:
            self.regions[k] = [(ch, val), []]
        for k in reads:
            self.regions.setdefault(k, [None, []])[1].append((ch, val))
        self._sem(ch)
        self.ops[eng].append((ch, fn, sorted(waits.items()), is_dma))
        return (ch, val)

    def wait_only(self, ch, reads):
        self.emit(ch, None, reads=reads, ndma=0)

    def barrier(self):
        snap = [(ch, v) for ch, v in self.cnt.items() if v > 0]
        for eng in self.ops:
            if not self.ops[eng]:
                continue
            chs = [(ch, v) for (ch, v) in snap
                   if CH2ENG[ch] != eng and self.marks.get((eng, ch), 0) < v]
            for ch, v in chs:
                self.marks[(eng, ch)] = v
                self._sem(ch)
            mych = [c for c, e in CH2ENG.items() if e == eng and c not in DMACH]
            mych = mych[0] if mych else "Y"
            self.ops[eng].append((mych, None, sorted(chs), False))

    def finalize(self):
        nc = self.nc
        with nc.Block() as blk:
            for eng in self.ops:
                if not self.ops[eng]:
                    continue

                def mk_body(eng):
                    oplist = self.ops[eng]

                    def body(e):
                        for (ch, fn, waits, is_dma) in oplist:
                            for dch, dval in waits:
                                e.wait_ge(self.sems[dch], dval)
                            if fn is None:
                                continue
                            r = fn(e)
                            if is_dma:
                                insts = r if isinstance(r, (list, tuple)) else [r]
                                for i_ in insts:
                                    i_.then_inc(self.sems[ch], 16)
                            else:
                                r.then_inc(self.sems[ch], 1)
                    return body
                getattr(blk, eng)(mk_body(eng))
        return self.nc


PB = [("pb", i) for i in range(8)]


def emit_consts(mk):
    c = {}
    c["onesb"] = mk.sbuf("c_onesb", [128, 1], BF16)
    c["ones1f"] = mk.sbuf("c_ones1f", [1, 128], F32)
    c["epsrow"] = mk.sbuf("c_epsrow", [1, 1], F32)
    c["epscol"] = mk.sbuf("c_epscol", [128, 1], F32)
    c["ones1b"] = mk.sbuf("c_ones1b", [1, 128], BF16)
    c["ones8f"] = mk.sbuf("c_ones8f", [8, 1], F32)
    mk.emit("G", lambda e: e.memset(c["ones1b"][:], 1.0), writes=[("c", "ones1b")])
    mk.emit("G", lambda e: e.memset(c["ones8f"][:], 1.0), writes=[("c", "ones8f")])
    mk.emit("G", lambda e: e.memset(c["onesb"][:], 1.0), writes=[("c", "onesb")])
    mk.emit("G", lambda e: e.memset(c["ones1f"][:], 1.0), writes=[("c", "ones1f")])
    mk.emit("G", lambda e: e.memset(c["epsrow"][:], LN_EPS), writes=[("c", "eps")])
    mk.emit("G", lambda e: e.memset(c["epscol"][:], LN_EPS), writes=[("c", "epsc")])
    return c


def alloc_ln_tmp(mk, t_tok=TOK, with_xb=True):
    return {
        **({"xb": mk.asbuf("ln_xb", [128, 2 * t_tok], BF16)} if with_xb else {}),
        "xq": mk.asbuf("ln_xq", [128, 2 * t_tok], BF16),
        "arow": mk.asbuf("ln_arow", [1, t_tok], F32),
        "brow": mk.asbuf("ln_brow", [1, t_tok], F32),
        "trow": mk.asbuf("ln_trow", [1, t_tok], F32),
        "mrow": mk.asbuf("ln_mrow", [1, t_tok], F32),
        "absb": mk.asbuf("ln_absb", [128, 1024], F32),
        "t1a": mk.asbuf("ln_t1a", [128, 512], F32),
        "t1b": mk.asbuf("ln_t1b", [128, 512], F32),
    }


def emit_ln_fm(mk, consts, tmp, tag, src, src_key, dst_fn, res=None, res_key=None, src_bf16=False):
    for j in range(NJ):
        par = j % 2
        if not src_bf16:
            mk.emit("S", (lambda j=j, par=par: lambda e: e.activation(
                tmp["xb"][:, par * TOK:(par + 1) * TOK],
                src[:, j * TOK:(j + 1) * TOK], AF.Copy))(),
                reads=[src_key(j, c) for c in range(NC4)], writes=[(tag, "xb", par)])
        mk.emit("S", (lambda j=j, par=par: lambda e: e.activation(
            tmp["xq"][:, par * TOK:(par + 1) * TOK],
            src[:, j * TOK:(j + 1) * TOK], AF.Square))(),
            reads=[src_key(j, c) for c in range(NC4)], writes=[(tag, "xq", par)])

        def mmstats(e, j=j, par=par):
            last = None
            for c in range(NC4):
                xbap = (src[:, j * TOK + c * 512: j * TOK + c * 512 + 512] if src_bf16
                        else tmp["xb"][:, par * TOK + c * 512: par * TOK + c * 512 + 512])
                last = e.matmul(mk.pool[0:1, c * 512:c * 512 + 512], consts["onesb"][:],
                                xbap, start=(j == 0), stop=(j == NJ - 1))
                last = e.matmul(mk.pool[0:1, 2048 + c * 512:2048 + c * 512 + 512], consts["onesb"][:],
                                tmp["xq"][:, par * TOK + c * 512: par * TOK + c * 512 + 512],
                                start=(j == 0), stop=(j == NJ - 1))
            return last
        rd = [(tag, "xq", par), ("c", "onesb")]
        rd += [src_key(j, c) for c in range(NC4)] if src_bf16 else [(tag, "xb", par)]
        mk.emit("T", mmstats, reads=rd, writes=PB)
    mk.emit("V", lambda e: e.tensor_scalar_mul(tmp["mrow"][0:1, :], mk.pool[0:1, 0:TOK], 1.0 / D),
            reads=[("pb", i) for i in range(4)], writes=[(tag, "mrow")])
    mk.emit("V", lambda e: e.tensor_scalar_mul(tmp["trow"][0:1, :], mk.pool[0:1, 2048:2048 + TOK], 1.0 / D),
            reads=[("pb", i) for i in range(4, 8)], writes=[(tag, "trow")])
    mk.emit("V", lambda e: e.scalar_tensor_tensor(tmp["arow"][0:1, :], tmp["mrow"][0:1, :], -1.0,
                                                  tmp["mrow"][0:1, :], ALU.mult, ALU.mult),
            reads=[(tag, "mrow")], writes=[(tag, "arow")])
    mk.emit("V", lambda e: e.tensor_add(tmp["trow"][0:1, :], tmp["trow"][0:1, :], tmp["arow"][0:1, :]),
            reads=[(tag, "trow"), (tag, "arow")], writes=[(tag, "trow")])
    mk.emit("S", lambda e: e.activation(tmp["trow"][0:1, :], tmp["trow"][0:1, :], AF.Ln, bias=consts["epsrow"][0:1, 0:1]),
            reads=[(tag, "trow"), ("c", "eps")], writes=[(tag, "trow")])
    mk.emit("S", lambda e: e.activation(tmp["arow"][0:1, :], tmp["trow"][0:1, :], AF.Exp, scale=-0.5),
            reads=[(tag, "trow")], writes=[(tag, "arow")])
    mk.emit("V", lambda e: e.scalar_tensor_tensor(tmp["brow"][0:1, :], tmp["mrow"][0:1, :], -1.0,
                                                  tmp["arow"][0:1, :], ALU.mult, ALU.mult),
            reads=[(tag, "mrow"), (tag, "arow")], writes=[(tag, "brow")])
    for c in range(NC4):
        def bcast(e, c=c):
            e.matmul(mk.pb(0), consts["ones1f"][:], tmp["arow"][0:1, c * 512:c * 512 + 512],
                     start=True, stop=True)
            return e.matmul(mk.pb(1), consts["ones1f"][:], tmp["brow"][0:1, c * 512:c * 512 + 512],
                            start=True, stop=True)
        mk.emit("T", bcast, reads=[(tag, "arow"), (tag, "brow"), ("c", "ones1f")],
                writes=[("pb", 0), ("pb", 1)])
        mk.emit("V", lambda e: e.tensor_copy(tmp["absb"][:, 0:1024], mk.pool[:, 0:1024]),
                reads=[("pb", 0), ("pb", 1)], writes=[(tag, "absb")])
        for j in range(NJ):
            ch = "V" if j % 2 == 0 else "G"
            tk = (tag, "t1a") if ch == "V" else (tag, "t1b")
            tb = tmp["t1a"] if ch == "V" else tmp["t1b"]
            off = j * TOK + c * 512
            mk.emit(ch, (lambda off=off, tb=tb: lambda e: e.tensor_mul(
                tb[:, 0:512], src[:, off:off + 512], tmp["absb"][:, 0:512]))(),
                reads=[src_key(j, c), (tag, "absb")], writes=[tk])
            if res is None:
                dap, dkey = dst_fn(j, c)
                mk.emit(ch, (lambda dap=dap, tb=tb: lambda e: e.tensor_add(
                    dap, tb[:, 0:512], tmp["absb"][:, 512:1024]))(),
                    reads=[tk, (tag, "absb")], writes=[dkey])
            else:
                mk.emit(ch, (lambda tb=tb: lambda e: e.tensor_add(
                    tb[:, 0:512], tb[:, 0:512], tmp["absb"][:, 512:1024]))(),
                    reads=[tk, (tag, "absb")], writes=[tk])
                mk.emit(ch, (lambda off=off, tb=tb: lambda e: e.tensor_add(
                    res[:, off:off + 512], res[:, off:off + 512], tb[:, 0:512]))(),
                    reads=[tk, res_key(j, c)], writes=[res_key(j, c)])


def emit_loadx_fm(mk, x_dram, res, res_key, eyef, tag):
    xtm = mk.sbuf(tag + "_xtm", [128, 2 * 1024], F32)
    for tt in range(NTT):
        par = tt % 2
        mk.emit("Yq", (lambda tt=tt, par=par: lambda e: e.dma_start(
            out=xtm[:, par * 1024:(par + 1) * 1024],
            in_=x_dram[tt * 128:(tt + 1) * 128, :]))(),
            reads=[(tag, "xtm", par)], writes=[(tag, "xtmw", par)])

        def trans(e, par=par):
            last = None
            for j in range(NJ):
                last = e.transpose(mk.pool[:, 2048 + par * 1024 + j * 128: 2048 + par * 1024 + j * 128 + 128],
                                   xtm[:, par * 1024 + j * 128: par * 1024 + j * 128 + 128], eyef[:])
            return last
        mk.emit("T", trans, reads=[(tag, "xtmw", par), ("c", "eyef")],
                writes=[("pb", 4 + 2 * par), ("pb", 5 + 2 * par)])

        def evac(e, tt=tt, par=par):
            last = None
            for j in range(NJ):
                srcap = mk.pool[:, 2048 + par * 1024 + j * 128: 2048 + par * 1024 + j * 128 + 128]
                dstap = res[:, j * TOK + tt * 128: j * TOK + tt * 128 + 128]
                last = e.tensor_copy(dstap, srcap) if tt % 2 == 0 else e.copy(dstap, srcap)
            return last
        ch = "V" if tt % 2 == 0 else "S"
        mk.emit(ch, evac, reads=[("pb", 4 + 2 * par), ("pb", 5 + 2 * par)],
                writes=[res_key(j, tt // 4) for j in range(NJ)] + [(tag, "xtm", par)])



def build_A1():
    mk = MK()
    x_in = mk.dram("x", [TOK, D], F32, kind="ExternalInput").ap()
    vT_in = mk.dram("vT3", [3, NJ, 128, 32], F32, kind="ExternalInput").ap()
    uTb_in = mk.dram("uTb3", [3, NJ, 33, 128], F32, kind="ExternalInput").ap()
    eyef_in = mk.dram("eyef", [128, 128], F32, kind="ExternalInput").ap()
    qT_out = mk.dram("qT", [D, TOK], BF16, kind="ExternalOutput").ap()
    kT_out = mk.dram("kT", [D, TOK], BF16, kind="ExternalOutput").ap()
    v_out = mk.dram("v", [TOK, D], BF16, kind="ExternalOutput").ap()
    xT_out = mk.dram("xT", [D, TOK], F32, kind="ExternalOutput").ap()

    mk.psum_pool()
    consts = emit_consts(mk)
    mk.init_arena(32768)
    eyef = mk.sbuf("eyef_s", [128, 128], F32)
    mk.emit("Yq", lambda e: e.dma_start(out=eyef[:], in_=eyef_in[:, :]), writes=[("c", "eyef")])
    vTs = mk.sbuf("vTs", [128, 3 * NJ * 32], BF16)
    uTbs = mk.sbuf("uTbs", [33, 3 * NJ * 128], BF16)
    mk.emit("Gq", lambda e: e.dma_start(out=vTs[:, :].rearrange("p (x j r) -> p x j r", x=3, j=NJ),
                                        in_=vT_in.rearrange("x j p r -> p x j r")),
            writes=[("c", "vTs")])
    mk.emit("Gq", lambda e: e.dma_start(out=uTbs[:, :].rearrange("p (x j f) -> p x j f", x=3, j=NJ),
                                        in_=uTb_in.rearrange("x j p f -> p x j f")),
            writes=[("c", "uTbs")])

    res = mk.sbuf("res", [128, NJ * TOK], F32)
    xln = mk.sbuf("xln", [128, NJ * TOK], BF16)
    rk = lambda j, c: ("res", j, c)
    emit_loadx_fm(mk, x_in, res, rk, eyef, "lx")

    ln_tmp = alloc_ln_tmp(mk)
    emit_ln_fm(mk, consts, ln_tmp, "ln1", res, rk,
               dst_fn=lambda j, c: (xln[:, j * TOK + c * 512: j * TOK + c * 512 + 512], ("xln", j, c)))

    for j in range(NJ):
        mk.emit("Yq", (lambda j=j: lambda e: e.dma_start(
            out=xT_out[j * 128:(j + 1) * 128, :], in_=res[:, j * TOK:(j + 1) * TOK]))(),
            reads=[rk(j, c) for c in range(NC4)], writes=[("resst", j)])

    taug = mk.sbuf("taug", [33, 2 * 512], BF16)
    mk.emit("G", lambda e: e.memset(taug[:], 1.0), writes=[("taug", 0), ("taug", 1)])
    qbuf = mk.sbuf("qbuf", [128, 2 * TOK], BF16)
    kbuf = mk.sbuf("kbuf", [128, 2 * TOK], BF16)
    vbuf = mk.sbuf("vbuf", [128, 2 * TOK], BF16)
    bufs = {0: qbuf, 1: kbuf, 2: vbuf}
    outs = {0: qT_out, 1: kT_out}
    for h in range(NJ):
        hpar = h % 2
        for p in range(3):
            for c in range(NC4):
                cpar = c % 2
                xkeys = [("xln", h, c)]
                mk.emit("T", (lambda h=h, p=p, c=c, cpar=cpar: lambda e: e.matmul(
                    mk.pool[0:32, cpar * 512:cpar * 512 + 512],
                    vTs[:, (p * NJ + h) * 32:(p * NJ + h) * 32 + 32],
                    xln[:, h * TOK + c * 512: h * TOK + c * 512 + 512],
                    start=True, stop=True))(),
                    reads=xkeys + [("c", "vTs")], writes=[("pb", cpar)])
                mk.emit("V", (lambda cpar=cpar: lambda e: e.tensor_copy(
                    taug[0:32, cpar * 512:cpar * 512 + 512],
                    mk.pool[0:32, cpar * 512:cpar * 512 + 512]))(),
                    reads=[("pb", cpar)], writes=[("taug", cpar)])
                if p < 2:
                    mk.emit("T", (lambda h=h, p=p, cpar=cpar: lambda e: e.matmul(
                        mk.pb(2 + cpar),
                        uTbs[0:33, (p * NJ + h) * 128:(p * NJ + h) * 128 + 128],
                        taug[0:33, cpar * 512:cpar * 512 + 512],
                        start=True, stop=True))(),
                        reads=[("taug", cpar), ("c", "uTbs")], writes=[("pb", 2 + cpar)])
                    mk.emit("S", (lambda p=p, c=c, cpar=cpar, hpar=hpar: lambda e: e.copy(
                        bufs[p][:, hpar * TOK + c * 512: hpar * TOK + c * 512 + 512],
                        mk.pb(2 + cpar)))(),
                        reads=[("pb", 2 + cpar)], writes=[("ob", p, hpar, c)])
                else:
                    def vmm(e, h=h, cpar=cpar):
                        last = None
                        for i in range(4):
                            last = e.matmul(
                                mk.pool[:, 2048 + cpar * 1024 + i * 128: 2048 + cpar * 1024 + i * 128 + 128],
                                taug[0:33, cpar * 512 + i * 128: cpar * 512 + i * 128 + 128],
                                uTbs[0:33, (2 * NJ + h) * 128:(2 * NJ + h) * 128 + 128],
                                start=True, stop=True)
                        return last
                    mk.emit("T", vmm, reads=[("taug", cpar), ("c", "uTbs")],
                            writes=[("pb", 4 + 2 * cpar), ("pb", 5 + 2 * cpar)])

                    def vevac(e, c=c, cpar=cpar, hpar=hpar):
                        last = None
                        for i in range(4):
                            tt = c * 4 + i
                            last = e.copy(
                                bufs[2][:, hpar * TOK + tt * 128: hpar * TOK + tt * 128 + 128],
                                mk.pool[:, 2048 + cpar * 1024 + i * 128: 2048 + cpar * 1024 + i * 128 + 128])
                        return last
                    mk.emit("S", vevac, reads=[("pb", 4 + 2 * cpar), ("pb", 5 + 2 * cpar)],
                            writes=[("ob", 2, hpar, c)])
        for p in range(3):
            if p < 2:
                mk.emit("Yq", (lambda p=p, h=h, hpar=hpar: lambda e: e.dma_start(
                    out=outs[p][h * 128:(h + 1) * 128, :],
                    in_=bufs[p][:, hpar * TOK:(hpar + 1) * TOK]))(),
                    reads=[("ob", p, hpar, c) for c in range(NC4)], writes=[("obst", p, h)])
            else:
                mk.emit("Yq", (lambda h=h, hpar=hpar: lambda e: e.dma_start(
                    out=v_out.rearrange("(tt p) d -> p tt d", p=128)[:, :, h * 128:(h + 1) * 128],
                    in_=bufs[2][:, hpar * TOK:(hpar + 1) * TOK].rearrange("p (tt f) -> p tt f", f=128)))(),
                    reads=[("ob", 2, hpar, c) for c in range(NC4)], writes=[("obst", 2, h)])
    mk.wait_only("Y", [("obst", p, h) for p in range(3) for h in range(NJ)] +
                 [("resst", j) for j in range(NJ)])
    return mk



def emit_attention_l0(mk, consts, qT_src, kTf_in, vf_in, fsb, nheads=NJ, scale=1.0 / np.sqrt(128.0)):
    kh = mk.asbuf("kh", [128, S], BF16)
    vh = mk.asbuf("vh", [128, NKT * 130], BF16)
    qh = mk.asbuf("qh", [128, TOK], BF16)
    pT = mk.asbuf("pT", [128, 3 * 1024], BF16)
    denr = mk.asbuf("denr", [1, 512], F32)
    bsb = mk.asbuf("bsb", [128, 4 * 512], F32)
    mk.emit("G", lambda e: e.memset(vh[:, :], 1.0), writes=[("vhones",)])
    for h in range(nheads):
        mk.emit("Yq", (lambda h=h: lambda e: e.dma_start(
            out=kh[:], in_=kTf_in[h * 128:(h + 1) * 128, :]))(),
            reads=[("kh",)], writes=[("kh",)])
        for hh in range(2):
            mk.emit("Yq", (lambda h=h, hh=hh: lambda e: e.dma_start(
                out=vh[:, :].rearrange("p (kt g) -> p kt g", g=130)[:, :, hh * 65:hh * 65 + 64],
                in_=vf_in.rearrange("(kt p) d -> p kt d", p=128)[:, :, h * 128 + hh * 64: h * 128 + hh * 64 + 64]))(),
                reads=[("vh",), ("vhones",)], writes=[("vh",)])
        mk.emit("Yq", (lambda h=h: lambda e: e.dma_start(
            out=qh[:], in_=qT_src[h * 128:(h + 1) * 128, :]))(),
            reads=[("qh",)], writes=[("qh",)])
        for qc in range(NC4):
            qpar = qc % 2
            dbank = 4 + qpar
            obank = 6 + qpar
            def emit_scexp(ktp, qc=qc):
                spar = (ktp // 2) % 2
                sb0 = spar * 2
                ppar = (ktp // 2) % 3

                def scmm(e, qc=qc, ktp=ktp, sb0=sb0):
                    last = None
                    for i in range(2):
                        kt = ktp + i
                        last = e.matmul(mk.pb(sb0 + i),
                                        kh[:, kt * 128: kt * 128 + 128],
                                        qh[:, qc * 512: qc * 512 + 512],
                                        start=True, stop=True)
                    return last
                mk.emit("T", scmm, reads=[("kh",), ("qh",), ("pb", sb0), ("pb", sb0 + 1)],
                        writes=[("pb", sb0), ("pb", sb0 + 1)])

                def expop(e, sb0=sb0, ppar=ppar):
                    return e.activation(pT[:, ppar * 1024:(ppar + 1) * 1024],
                                        mk.pool[:, sb0 * 512:sb0 * 512 + 1024],
                                        AF.Exp, scale=scale)
                mk.emit("S", expop, reads=[("pb", sb0), ("pb", sb0 + 1)], writes=[("pT", ppar)])

            def emit_pv(ktp, qpar=qpar):
                ppar = (ktp // 2) % 3
                ob0 = 4 + qpar
                ob1 = 6 + qpar

                def pv(e, ktp=ktp, ppar=ppar, ob0=ob0, ob1=ob1):
                    last = None
                    for i in range(2):
                        kt = ktp + i
                        pslice = pT[:, ppar * 1024 + i * 512: ppar * 1024 + i * 512 + 512]
                        for hh, ob in ((0, ob0), (1, ob1)):
                            last = e.matmul(mk.pool[0:65, ob * 512: ob * 512 + 512],
                                            vh[:, kt * 130 + hh * 65: kt * 130 + hh * 65 + 65],
                                            pslice, start=(kt == 0), stop=(kt == NKT - 1))
                    return last
                mk.emit("T", pv, reads=[("pT", ppar), ("vh",), ("pb", ob0), ("pb", ob1)],
                        writes=[("pb", ob0), ("pb", ob1)])

            for ktp in range(0, NKT + 2, 2):
                if ktp < NKT:
                    emit_scexp(ktp)
                if ktp >= 2:
                    emit_pv(ktp - 2)
            ob0 = 4 + qpar
            ob1 = 6 + qpar
            mk.emit("V", (lambda ob0=ob0: lambda e: e.reciprocal(
                denr[0:1, 0:512], mk.pool[64:65, ob0 * 512: ob0 * 512 + 512]))(),
                reads=[("pb", ob0)], writes=[("denr",)])
            sb0 = 0 if qpar == 0 else 2

            def dbc(e, sb0=sb0):
                return e.matmul(mk.pool[0:64, sb0 * 512: sb0 * 512 + 512],
                                consts["ones1f"][0:1, 0:64],
                                denr[0:1, 0:512], start=True, stop=True)
            mk.emit("T", dbc, reads=[("denr",), ("c", "ones1f"), ("pb", sb0), ("pb", sb0 + 1)],
                    writes=[("pb", sb0)])
            s0 = qpar * 512
            mk.emit("V", (lambda sb0=sb0, s0=s0: lambda e: e.tensor_copy(
                bsb[0:64, s0:s0 + 512], mk.pool[0:64, sb0 * 512: sb0 * 512 + 512]))(),
                reads=[("pb", sb0)], writes=[("bsb", qpar)])
            mk.emit("V", (lambda h=h, qc=qc, ob0=ob0, s0=s0: lambda e: e.tensor_mul(
                fsb[:, h * TOK + qc * 512: h * TOK + qc * 512 + 512][0:64, :],
                mk.pool[0:64, ob0 * 512: ob0 * 512 + 512],
                bsb[0:64, s0:s0 + 512]))(),
                reads=[("bsb", qpar), ("pb", ob0)], writes=[("f", h, qc)])
            mk.emit("V", (lambda h=h, qc=qc, ob1=ob1, s0=s0: lambda e: e.tensor_mul(
                fsb[:, h * TOK + qc * 512: h * TOK + qc * 512 + 512][64:128, :],
                mk.pool[0:64, ob1 * 512: ob1 * 512 + 512],
                bsb[0:64, s0:s0 + 512]))(),
                reads=[("bsb", qpar), ("pb", ob1), ("f", h, qc)], writes=[("f", h, qc)])


def emit_phase_A2(mk, consts, io):
    xT_in, qT_in, kTf_in, vf_in, x1_out = io["xT"], io["qT"], io["kTf"], io["vf"], io["x1_out"]
    res = mk.asbuf("res", [128, NJ * TOK], F32)
    fsb = mk.asbuf("fsb", [128, NJ * TOK], BF16)
    rk = lambda j, c: ("res", j, c)
    for j in range(NJ):
        mk.emit("Yq", (lambda j=j: lambda e: e.dma_start(
            out=res[:, j * TOK:(j + 1) * TOK], in_=xT_in[j * 128:(j + 1) * 128, :]))(),
            writes=[rk(j, c) for c in range(NC4)])

    emit_attention_l0(mk, consts, qT_in, kTf_in, vf_in, fsb)

    ln_tmp = alloc_ln_tmp(mk, with_xb=False)
    fkey = lambda j, c: ("f", j, c)
    emit_ln_fm(mk, consts, ln_tmp, "ln2", fsb, fkey, dst_fn=None, res=res, res_key=rk, src_bf16=True)
    for j in range(NJ):
        mk.emit("Yq", (lambda j=j: lambda e: e.dma_start(
            out=x1_out[j * 128:(j + 1) * 128, :], in_=res[:, j * TOK:(j + 1) * TOK]))(),
            reads=[rk(j, c) for c in range(NC4)], writes=[("x1st", j)])
    mk.wait_only("Y", [("x1st", j) for j in range(NJ)])


def build_A2():
    mk = MK()
    io = {
        "xT": mk.dram("xT", [D, TOK], F32, kind="ExternalInput").ap(),
        "qT": mk.dram("qT", [D, TOK], BF16, kind="ExternalInput").ap(),
        "kTf": mk.dram("kTf", [D, S], BF16, kind="ExternalInput").ap(),
        "vf": mk.dram("vf", [S, D], BF16, kind="ExternalInput").ap(),
        "x1_out": mk.dram("x1T", [D, TOK], F32, kind="ExternalOutput").ap(),
    }
    mk.psum_pool()
    consts = emit_consts(mk)
    mk.init_arena()
    emit_phase_A2(mk, consts, io)
    return mk




def emit_ln_fm_stream(mk, consts, tag, src_dram, xln, bufs):
    for j in range(NJ):
        par = j % 2
        mk.emit("Yq", (lambda j=j: lambda e: e.dma_start(
            out=bufs["resb"], in_=src_dram[j * 128:(j + 1) * 128, :]))(),
            reads=[(tag, "resb")], writes=[(tag, "resbw")])
        mk.emit("S", (lambda par=par: lambda e: e.activation(
            bufs["xb"][:, par * TOK:(par + 1) * TOK], bufs["resb"], AF.Copy))(),
            reads=[(tag, "resbw")], writes=[(tag, "xb", par)])
        mk.emit("S", (lambda par=par: lambda e: e.activation(
            bufs["xq"][:, par * TOK:(par + 1) * TOK], bufs["resb"], AF.Square))(),
            reads=[(tag, "resbw")], writes=[(tag, "xq", par), (tag, "resb")])

        def mmstats(e, j=j, par=par):
            last = None
            for c in range(NC4):
                last = e.matmul(mk.pool[0:1, c * 512:c * 512 + 512], consts["onesb"][:],
                                bufs["xb"][:, par * TOK + c * 512: par * TOK + c * 512 + 512],
                                start=(j == 0), stop=(j == NJ - 1))
                last = e.matmul(mk.pool[0:1, 2048 + c * 512:2048 + c * 512 + 512], consts["onesb"][:],
                                bufs["xq"][:, par * TOK + c * 512: par * TOK + c * 512 + 512],
                                start=(j == 0), stop=(j == NJ - 1))
            return last
        mk.emit("T", mmstats, reads=[(tag, "xb", par), (tag, "xq", par), ("c", "onesb")],
                writes=PB)
    tmp = bufs
    prow = mk.pool[0:1, 0:TOK]
    mk.emit("V", lambda e: e.tensor_scalar_mul(tmp["brow"], mk.pool[0:1, 0:TOK], 1.0 / D),
            reads=[("pb", i) for i in range(4)], writes=[(tag, "brow")])
    mk.emit("V", lambda e: e.tensor_scalar_mul(tmp["arow"], mk.pool[0:1, 2048:2048 + TOK], 1.0 / D),
            reads=[("pb", i) for i in range(4, 8)], writes=[(tag, "arow")])
    mk.emit("V", lambda e: e.tensor_mul(prow, tmp["brow"], tmp["brow"]),
            reads=[(tag, "brow")] + [("pb", i) for i in range(4)],
            writes=[("pb", 0), ("pb", 1), ("pb", 2), ("pb", 3)])
    mk.emit("V", lambda e: e.tensor_sub(tmp["arow"], tmp["arow"], prow),
            reads=[(tag, "arow")] + [("pb", i) for i in range(4)], writes=[(tag, "arow")])
    mk.emit("S", lambda e: e.activation(tmp["arow"], tmp["arow"], AF.Ln, bias=consts["epsrow"][0:1, 0:1]),
            reads=[(tag, "arow"), ("c", "eps")], writes=[(tag, "arow")])
    mk.emit("S", lambda e: e.activation(tmp["arow"], tmp["arow"], AF.Exp, scale=-0.5),
            reads=[(tag, "arow")], writes=[(tag, "arow")])
    mk.emit("V", lambda e: e.scalar_tensor_tensor(tmp["brow"], tmp["brow"], -1.0,
                                                  tmp["arow"], ALU.mult, ALU.mult),
            reads=[(tag, "brow"), (tag, "arow")], writes=[(tag, "brow")])
    mk.barrier()
    for c in range(NC4):
        def bcast(e, c=c):
            e.matmul(mk.pb(0), consts["ones1f"][:], tmp["arow"][0:1, c * 512:c * 512 + 512],
                     start=True, stop=True)
            return e.matmul(mk.pb(1), consts["ones1f"][:], tmp["brow"][0:1, c * 512:c * 512 + 512],
                            start=True, stop=True)
        mk.emit("T", bcast, reads=[(tag, "arow"), (tag, "brow"), ("c", "ones1f")],
                writes=[("pb", 0), ("pb", 1)])
        mk.emit("V", (lambda c=c: lambda e: e.tensor_copy(
            bufs["absf"][:, c * 512:c * 512 + 512], mk.pb(0)))(),
            reads=[("pb", 0)], writes=[(tag, "absf", c)])
        mk.emit("V", (lambda c=c: lambda e: e.tensor_copy(
            bufs["absf"][:, TOK + c * 512:TOK + c * 512 + 512], mk.pb(1)))(),
            reads=[("pb", 1)], writes=[(tag, "absf", c)])
    for j in range(NJ):
        mk.emit("Yq", (lambda j=j: lambda e: e.dma_start(
            out=bufs["resb"], in_=src_dram[j * 128:(j + 1) * 128, :]))(),
            reads=[(tag, "resb")], writes=[(tag, "resbw")])
        for c in range(NC4):
            ch = "V" if c % 2 == 0 else "G"
            tk = (tag, "t1a") if ch == "V" else (tag, "t1b")
            tb = bufs["t1a"] if ch == "V" else bufs["t1b"]
            mk.emit(ch, (lambda c=c, tb=tb: lambda e: e.tensor_mul(
                tb, bufs["resb"][:, c * 512:c * 512 + 512],
                bufs["absf"][:, c * 512:c * 512 + 512]))(),
                reads=[(tag, "resbw"), (tag, "absf", c)], writes=[tk])
            mk.emit(ch, (lambda j=j, c=c, tb=tb: lambda e: e.tensor_add(
                xln[:, j * TOK + c * 512: j * TOK + c * 512 + 512], tb,
                bufs["absf"][:, TOK + c * 512:TOK + c * 512 + 512]))(),
                reads=[tk, (tag, "absf", c)], writes=[(tag, "xln", j, c)])
    return



NE = 8
NHC = 16


def emit_phase_B(mk, consts, eyef, io, debug_stop=None):
    xT_in, x2_out = io["xT"], io["x2_out"]
    gwT_in, gb_in, w1_in, b1_in, w2_in, b2_in = (io["gwT"], io["gb"], io["w1"],
                                                 io["b1r"], io["w2"], io["b2"])
    x2_out2 = io.get("x2_out2")
    gwT = mk.asbuf("gwT_s", [128, NJ * NE], BF16)
    mk.emit("Gq", lambda e: e.dma_start(out=gwT[:, :].rearrange("p (j e) -> p j e", j=NJ),
                                        in_=gwT_in.rearrange("(j p) e -> p j e", p=128)),
            writes=[("c", "gwT")])
    gbs = mk.asbuf("gb_s", [NE, 1], F32)
    mk.emit("Yq", lambda e: e.dma_start(out=gbs[:], in_=gb_in[:, :]), writes=[("c", "gb")])
    b1s = mk.asbuf("b1_s", [128, NE * NHC], F32)
    mk.emit("Yq", lambda e: e.dma_start(out=b1s[:, :], in_=b1_in[:, :]),
            writes=[("c", "b1")])
    b2s = mk.asbuf("b2_s", [1, NE * D], BF16)
    mk.emit("Gq", lambda e: e.dma_start(out=b2s[:], in_=b2_in.rearrange("e o -> (e o)")[None, :]),
            writes=[("c", "b2")])
    zer = mk.asbuf("zer_s", [128, 512], BF16)
    mk.emit("G", lambda e: e.memset(zer[:], 0.0), writes=[("c", "zer")])

    xln = mk.asbuf("xln", [128, NJ * TOK], BF16)
    acc = mk.asbuf("acc", [128, NTT * 1024], BF16)
    gtm = mk.asbuf("gtm", [128, NTT * NE], F32)
    tmp2 = mk.asbuf("tmp2", [128, 2 * 512], BF16)
    ARN = mk.asbuf("arn", [128, 40960], BF16)

    lnb = {
        "xb": ARN[:, 16384:20480],
        "xq": ARN[:, 20480:24576],
        "absf": ARN[:, 16384:24576].bitcast(F32),
        "arow": ARN[:, 24576:28672].bitcast(F32)[0:1, 0:TOK],
        "brow": ARN[:, 28672:32768].bitcast(F32)[0:1, 0:TOK],
        "t1a": ARN[:, 32768:34816].bitcast(F32)[:, 0:512],
        "t1b": ARN[:, 32768:34816].bitcast(F32)[:, 512:1024],
        "resb": ARN[:, 34816:38912].bitcast(F32),
    }
    w1s = ARN[:, 0:16384]
    w2s = ARN[:, 16384:32768]
    htc = ARN[:, 32768:40960]

    def load_w1(e_, first=False):
        for hc in range(NHC):
            mk.emit("Gq", (lambda e_=e_, hc=hc: lambda e: e.dma_start(
                out=w1s[:, hc * 1024:(hc + 1) * 1024].rearrange("p (j f) -> p j f", f=128),
                in_=w1_in[e_, :, hc * 128:(hc + 1) * 128].rearrange("(j p) f -> p j f", p=128)))(),
                reads=[("w1s", hc)], writes=[("w1s", hc)])

    def load_w2(e_):
        for hc in range(NHC):
            mk.emit("Gq", (lambda e_=e_, hc=hc: lambda e: e.dma_start(
                out=w2s[:, hc * 1024:(hc + 1) * 1024],
                in_=w2_in[e_, hc * 128:(hc + 1) * 128, :]))(),
                reads=[("w2s", hc)], writes=[("w2s", hc)])

    load_w1(0, first=True)
    emit_ln_fm_stream(mk, consts, "lnb", xT_in, xln, lnb)
    xk = lambda j, c: ("lnb", "xln", j, c)
    if debug_stop == "ln":
        return

    logT = ARN[:, 16384:20480].bitcast(F32)[0:8, 0:TOK]
    expT = ARN[:, 20480:24576].bitcast(F32)[0:8, 0:TOK]
    prbT = ARN[:, 24576:28672].bitcast(F32)[0:8, 0:TOK]
    denB = ARN[:, 28672:32768].bitcast(F32)[0:1, 0:TOK]
    ptm = ARN[:, 32768:33280].bitcast(F32)[:, 0:128]
    sc1 = ARN[:, 33280:33792].bitcast(F32)[:, 0:128]
    sc2 = ARN[:, 33792:34304].bitcast(F32)[:, 0:128]
    scm = ARN[:, 34304:34816].bitcast(F32)[:, 0:32]

    def gmm(e, c=None):
        last = None
        for c_ in range(NC4):
            for j in range(NJ):
                last = e.matmul(mk.pool[0:8, 2048 + c_ * 512:2048 + c_ * 512 + 512],
                                gwT[:, j * NE:(j + 1) * NE],
                                xln[:, j * TOK + c_ * 512: j * TOK + c_ * 512 + 512],
                                start=(j == 0), stop=(j == NJ - 1))
        return last
    mk.emit("T", gmm, reads=[("c", "gwT")] + [xk(j, c) for j in range(NJ) for c in range(NC4)],
            writes=[("pb", 4), ("pb", 5), ("pb", 6), ("pb", 7)])
    mk.emit("V", lambda e: e.tensor_scalar(logT, mk.pool[0:8, 2048:2048 + TOK],
                                           gbs[:, 0:1], None, ALU.add),
            reads=[("pb", i) for i in range(4, 8)] + [("c", "gb")], writes=[("g", "log")])
    mk.emit("S", lambda e: e.activation(expT, logT, AF.Exp),
            reads=[("g", "log")], writes=[("g", "exp")])

    def gden(e):
        last = None
        for c_ in range(NC4):
            last = e.matmul(mk.pool[0:1, c_ * 512:c_ * 512 + 512], consts["ones8f"][:, :],
                            expT[:, c_ * 512:c_ * 512 + 512], start=True, stop=True)
        return last
    mk.emit("T", gden, reads=[("g", "exp"), ("c", "ones8f")], writes=[("pb", 0), ("pb", 1)])
    mk.emit("V", lambda e: e.reciprocal(denB, mk.pool[0:1, 0:TOK]),
            reads=[("pb", 0), ("pb", 1)], writes=[("g", "den")])

    def gbc(e):
        last = None
        for c_ in range(NC4):
            last = e.matmul(mk.pool[0:8, c_ * 512:c_ * 512 + 512], consts["ones1f"][0:1, 0:8],
                            denB[0:1, c_ * 512:c_ * 512 + 512], start=True, stop=True)
        return last
    mk.emit("T", gbc, reads=[("g", "den"), ("c", "ones1f")], writes=[("pb", 0), ("pb", 1)])
    mk.emit("V", lambda e: e.tensor_mul(prbT, expT, mk.pool[0:8, 0:TOK]),
            reads=[("pb", 0), ("pb", 1), ("g", "exp")], writes=[("g", "prb")])

    def gtr(e):
        last = None
        for tt in range(NTT):
            last = e.transpose(mk.pool[:, 2048 + tt * 8: 2048 + tt * 8 + 8],
                               prbT[:, tt * 128:(tt + 1) * 128], eyef[0:8, 0:8])
        return last
    mk.emit("T", gtr, reads=[("g", "prb"), ("c", "eyef")], writes=[("pb", 4)])
    mk.emit("V", lambda e: e.tensor_copy(ptm[:, :], mk.pool[:, 2048:2048 + 128]),
            reads=[("pb", 4)], writes=[("g", "ptm")])
    p3 = ptm.rearrange("p (t e) -> p t e", e=NE)
    s13 = sc1.rearrange("p (t e) -> p t e", e=NE)
    s23 = sc2.rearrange("p (t e) -> p t e", e=NE)
    m3 = scm.rearrange("p (t o) -> p t o", o=2)
    mk.emit("V", lambda e: e.reduce_max(m3[:, :, 0:1], p3, axis=AX),
            reads=[("g", "ptm")], writes=[("g", "m1")])
    mk.emit("V", lambda e: e.tensor_tensor(s13, p3, m3[:, :, 0:1].broadcast_to([128, NTT, NE]), ALU.is_ge),
            reads=[("g", "ptm"), ("g", "m1")], writes=[("g", "is1")])
    mk.emit("V", lambda e: e.tensor_mul(s23, p3, s13),
            reads=[("g", "ptm"), ("g", "is1")], writes=[("g", "t")])
    mk.emit("V", lambda e: e.tensor_sub(s23, p3, s23),
            reads=[("g", "ptm"), ("g", "t")], writes=[("g", "pm")])
    mk.emit("V", lambda e: e.reduce_max(m3[:, :, 1:2], s23, axis=AX),
            reads=[("g", "pm")], writes=[("g", "m2")])
    mk.emit("V", lambda e: e.tensor_tensor(s23, s23, m3[:, :, 1:2].broadcast_to([128, NTT, NE]), ALU.is_ge),
            reads=[("g", "pm"), ("g", "m2")], writes=[("g", "is2")])
    mk.emit("V", lambda e: e.tensor_add(s13, s13, s23),
            reads=[("g", "is1"), ("g", "is2")], writes=[("g", "sel")])
    mk.emit("V", lambda e: e.tensor_mul(gtm[:, :].rearrange("p (t e) -> p t e", e=NE), p3, s13),
            reads=[("g", "sel"), ("g", "ptm")], writes=[("c", "gtm")])

    if debug_stop == "gate":
        return
    mk.emit("G", lambda e: e.memset(acc[:, :], 0.0),
            writes=[("acc", tt, oc) for tt in range(NTT) for oc in range(2)])
    mk.barrier()
    load_w2(0)
    for e_ in range(NE):
        for tc in range(NC4):
            for hc in range(NHC):
                pbk = hc % 2

                def p1mm(e, e_=e_, tc=tc, hc=hc, pbk=pbk):
                    last = None
                    for j in range(NJ):
                        last = e.matmul(mk.pb(pbk),
                                        w1s[:, hc * 1024 + j * 128: hc * 1024 + j * 128 + 128],
                                        xln[:, j * TOK + tc * 512: j * TOK + tc * 512 + 512],
                                        start=(j == 0), stop=(j == NJ - 1))
                    return last
                mk.emit("T", p1mm, reads=[("w1s", hc)] + [xk(j, tc) for j in range(NJ)],
                        writes=[("pb", pbk)])
                if hc % 2 == 0:
                    mk.emit("S", (lambda e_=e_, hc=hc, pbk=pbk: lambda e: e.activation(
                        htc[:, hc * 512:(hc + 1) * 512], mk.pb(pbk), AF.Relu,
                        bias=b1s[:, e_ * NHC + hc: e_ * NHC + hc + 1]))(),
                        reads=[("pb", pbk), ("c", "b1")], writes=[("h", hc)])
                else:
                    mk.emit("V", (lambda e_=e_, hc=hc, pbk=pbk: lambda e: e.scalar_tensor_tensor(
                        htc[:, hc * 512:(hc + 1) * 512], mk.pb(pbk),
                        b1s[:, e_ * NHC + hc: e_ * NHC + hc + 1], zer[:, :],
                        ALU.add, ALU.max))(),
                        reads=[("pb", pbk), ("c", "b1"), ("c", "zer")], writes=[("h", hc)])
            for ttl in range(4):
                tt = tc * 4 + ttl
                for oc in range(2):
                    pbk = 2 + (ttl * 2 + oc) % 2

                    def p2mm(e, e_=e_, ttl=ttl, oc=oc, pbk=pbk):
                        last = None
                        for hc in range(NHC):
                            e.matmul(mk.pb(pbk),
                                     htc[:, hc * 512 + ttl * 128: hc * 512 + ttl * 128 + 128],
                                     w2s[:, hc * 1024 + oc * 512: hc * 1024 + oc * 512 + 512],
                                     start=(hc == 0), stop=False)
                        last = e.matmul(mk.pb(pbk), consts["ones1b"][0:1, :],
                                        b2s[0:1, e_ * D + oc * 512: e_ * D + oc * 512 + 512],
                                        start=False, stop=True)
                        return last
                    mk.emit("T", p2mm,
                            reads=[("h", hc) for hc in range(NHC)] + [("w2s", hc) for hc in range(NHC)] +
                                  [("c", "b2"), ("c", "ones1b")],
                            writes=[("pb", pbk)])
                    tpar = (ttl * 2 + oc) % 2
                    mk.emit("V", (lambda e_=e_, tt=tt, pbk=pbk, tpar=tpar: lambda e: e.tensor_scalar(
                        tmp2[:, tpar * 512:(tpar + 1) * 512], mk.pb(pbk),
                        gtm[:, tt * NE + e_: tt * NE + e_ + 1], None, ALU.mult))(),
                        reads=[("pb", pbk), ("c", "gtm")], writes=[("tmp2", tpar)])
                    mk.emit("G", (lambda tt=tt, oc=oc, tpar=tpar: lambda e: e.tensor_add(
                        acc[:, tt * 1024 + oc * 512: tt * 1024 + oc * 512 + 512],
                        acc[:, tt * 1024 + oc * 512: tt * 1024 + oc * 512 + 512],
                        tmp2[:, tpar * 512:(tpar + 1) * 512]))(),
                        reads=[("tmp2", tpar), ("acc", tt, oc)], writes=[("acc", tt, oc)])
        if e_ + 1 < NE:
            load_w1(e_ + 1)
            load_w2(e_ + 1)
    if debug_stop == "experts":
        return
    mk.barrier()
    av = {
        "sqt": ARN[:, 32768:34816][:, 0:1024],
        "lnf": ARN[:, 34816:38912].bitcast(F32)[:, 0:1024],
        "svec": ARN[:, 38912:39040].bitcast(F32)[:, 0:48],
        "lnfT": ARN[:, 0:16384],
        "resb": ARN[:, 16384:20480].bitcast(F32),
        "outb": ARN[:, 20480:24576].bitcast(F32),
    }
    emit_postln_tm(mk, consts, eyef, acc,
                   lambda tt: [("acc", tt, 0), ("acc", tt, 1)],
                   av, xT_in, x2_out, out_dram2=x2_out2, tag="plB")


def emit_postln_tm(mk, consts, eyef, src, skey, av, xT_in, out_dram, tag="pl", out_dram2=None):
    sqt, lnf, svec, lnfT, resb, outb = av["sqt"], av["lnf"], av["svec"], av["lnfT"], av["resb"], av["outb"]
    mvec = svec[:, 0:16]
    vvec = svec[:, 16:32]
    rvec = svec[:, 32:48]
    for tt in range(NTT):
        at = src[:, tt * 1024:(tt + 1) * 1024]
        mk.emit("V", (lambda at=at, tt=tt: lambda e: e.reduce_sum(
            mvec[:, tt:tt + 1], at, axis=AX))(),
            reads=skey(tt), writes=[(tag, "m", tt)])
        mk.emit("S", (lambda at=at: lambda e: e.activation(sqt[:, 0:1024], at, AF.Square))(),
            reads=skey(tt), writes=[(tag, "sq")])
        mk.emit("V", (lambda tt=tt: lambda e: e.reduce_sum(
            vvec[:, tt:tt + 1], sqt[:, 0:1024], axis=AX))(),
            reads=[(tag, "sq")], writes=[(tag, "v", tt)])
    mk.emit("V", lambda e: e.tensor_scalar_mul(mvec[:, :], mvec[:, :], 1.0 / D),
            reads=[(tag, "m", tt) for tt in range(NTT)], writes=[(tag, "mv")])
    mk.emit("V", lambda e: e.tensor_scalar_mul(vvec[:, :], vvec[:, :], 1.0 / D),
            reads=[(tag, "v", tt) for tt in range(NTT)], writes=[(tag, "vv")])
    mk.emit("V", lambda e: e.scalar_tensor_tensor(rvec[:, :], mvec[:, :], -1.0, mvec[:, :],
                                                  ALU.mult, ALU.mult),
            reads=[(tag, "mv")], writes=[(tag, "rv")])
    mk.emit("V", lambda e: e.tensor_add(vvec[:, :], vvec[:, :], rvec[:, :]),
            reads=[(tag, "vv"), (tag, "rv")], writes=[(tag, "vv")])
    mk.emit("S", lambda e: e.activation(vvec[:, :], vvec[:, :], AF.Ln, bias=consts["epscol"][:, 0:1]),
            reads=[(tag, "vv"), ("c", "epsc")], writes=[(tag, "vv")])
    mk.emit("S", lambda e: e.activation(rvec[:, :], vvec[:, :], AF.Exp, scale=-0.5),
            reads=[(tag, "vv")], writes=[(tag, "rv")])
    for tt in range(NTT):
        at = src[:, tt * 1024:(tt + 1) * 1024]
        mk.emit("V", (lambda at=at, tt=tt: lambda e: e.tensor_scalar(
            lnf[:, 0:1024], at, mvec[:, tt:tt + 1], rvec[:, tt:tt + 1],
            ALU.subtract, ALU.mult))(),
            reads=skey(tt) + [(tag, "mv"), (tag, "rv")],
            writes=[(tag, "lnf")])

        def ttr(e, tt=tt):
            last = None
            for j in range(NJ):
                last = e.transpose(mk.pool[:, 2048 + j * 128: 2048 + j * 128 + 128],
                                   lnf[:, j * 128:(j + 1) * 128], eyef[:])
            return last
        mk.emit("T", ttr, reads=[(tag, "lnf"), ("c", "eyef")],
                writes=[("pb", 4), ("pb", 5), ("pb", 6), ("pb", 7)])

        def tev(e, tt=tt):
            last = None
            for j in range(NJ):
                last = e.copy(lnfT[:, j * TOK + tt * 128: j * TOK + tt * 128 + 128],
                              mk.pool[:, 2048 + j * 128: 2048 + j * 128 + 128])
            return last
        mk.emit("S", tev, reads=[("pb", 4), ("pb", 5), ("pb", 6), ("pb", 7), (tag, "lnf")],
                writes=[(tag, "lnfT", tt)])
    for j in range(NJ):
        mk.emit("Yq", (lambda j=j: lambda e: e.dma_start(
            out=resb[:, :], in_=xT_in[j * 128:(j + 1) * 128, :]))(),
            reads=[(tag, "resb")], writes=[(tag, "resbw")])
        mk.emit("V", (lambda j=j: lambda e: e.tensor_add(
            outb[:, :], resb[:, :], lnfT[:, j * TOK:(j + 1) * TOK]))(),
            reads=[(tag, "resbw"), (tag, "outb")] + [(tag, "lnfT", tt) for tt in range(NTT)],
            writes=[(tag, "outbw"), (tag, "resb")])
        mk.emit("Yq", (lambda j=j: lambda e: e.dma_start(
            out=out_dram[j * 128:(j + 1) * 128, :], in_=outb[:, :]))(),
            reads=[(tag, "outbw")], writes=[(tag, "st", j)])
        if out_dram2 is not None:
            mk.emit("Gq", (lambda j=j: lambda e: e.dma_start(
                out=out_dram2[j * 128:(j + 1) * 128, :], in_=outb[:, :]))(),
                reads=[(tag, "outbw")], writes=[(tag, "st2", j)])
        mk.regions[(tag, "outb")] = [mk.regions[(tag, "st", j)][0], []]
        if out_dram2 is not None:
            mk.regions[(tag, "outb")][1].append(mk.regions[(tag, "st2", j)][0])
    mk.wait_only("Y", [(tag, "st", j) for j in range(NJ)] +
                 ([(tag, "st2", j) for j in range(NJ)] if out_dram2 is not None else []))


def build_B_full():
    mk = MK()
    io = {
        "xT": mk.dram("xT", [D, TOK], F32, kind="ExternalInput").ap(),
        "gwT": mk.dram("gwT", [D, NE], F32, kind="ExternalInput").ap(),
        "gb": mk.dram("gb", [NE, 1], F32, kind="ExternalInput").ap(),
        "w1": mk.dram("w1", [NE, D, 2048], F32, kind="ExternalInput").ap(),
        "b1r": mk.dram("b1r", [128, NE * NHC], F32, kind="ExternalInput").ap(),
        "w2": mk.dram("w2", [NE, 2048, D], F32, kind="ExternalInput").ap(),
        "b2": mk.dram("b2", [NE, D], F32, kind="ExternalInput").ap(),
        "x2_out": mk.dram("x2T", [D, TOK], F32, kind="ExternalOutput").ap(),
    }
    eyef_in = mk.dram("eyef", [128, 128], F32, kind="ExternalInput").ap()
    mk.psum_pool()
    consts = emit_consts(mk)
    eyef = mk.sbuf("eyef_s", [128, 128], F32)
    mk.emit("Yq", lambda e: e.dma_start(out=eyef[:], in_=eyef_in[:, :]), writes=[("c", "eyef")])
    mk.init_arena()
    emit_phase_B(mk, consts, eyef, io)
    return mk



def emit_phase_C1(mk, consts, eyef, io):
    xT_in, wq_in, cos_in, sin_in, prot_in = (io["xT"], io["wqkv"], io["cos2"],
                                             io["sin2"], io["prot"])
    qT_out, kT_out, v_out = io["qT"], io["kT"], io["v"]
    cos2 = mk.asbuf("cos2_s", [128, TOK], F32)
    sin2 = mk.asbuf("sin2_s", [128, TOK], F32)
    protb = mk.asbuf("prot_s", [128, 128], BF16)
    mk.emit("Yq", lambda e: e.dma_start(out=cos2[:], in_=cos_in[:, :]), writes=[("c", "cos")])
    mk.emit("Yq", lambda e: e.dma_start(out=sin2[:], in_=sin_in[:, :]), writes=[("c", "sin")])
    mk.emit("Gq", lambda e: e.dma_start(out=protb[:], in_=prot_in[:, :]), writes=[("c", "prot")])
    wq = mk.asbuf("wq_s", [128, 24 * NJ * 128], BF16)
    for ot in range(24):
        mk.emit("Gq", (lambda ot=ot: lambda e: e.dma_start(
            out=wq[:, ot * 1024:(ot + 1) * 1024].rearrange("p (j f) -> p j f", f=128),
            in_=wq_in[:, ot * 128:(ot + 1) * 128].rearrange("(j p) f -> p j f", p=128)))(),
            writes=[("wq", ot)])

    xln = mk.asbuf("xln_c1", [128, NJ * TOK], BF16)
    qbuf = mk.asbuf("qbuf", [128, 2 * TOK], BF16)
    kbuf = mk.asbuf("kbuf", [128, 2 * TOK], BF16)
    vbuf = mk.asbuf("vbuf", [128, 2 * TOK], BF16)
    braw = mk.asbuf("braw", [128, 2 * 512], BF16)
    t1f = mk.asbuf("t1f", [128, 2 * 512], F32)
    t2f = mk.asbuf("t2f", [128, 2 * 512], F32)
    vf32 = mk.asbuf("vf32", [128, 2 * 512], F32)
    ARN = mk.asbuf("arn_c1", [128, 24576], BF16)
    lnb = {
        "xb": ARN[:, 0:4096],
        "xq": ARN[:, 4096:8192],
        "absf": ARN[:, 0:8192].bitcast(F32),
        "arow": ARN[:, 8192:12288].bitcast(F32)[0:1, 0:TOK],
        "brow": ARN[:, 12288:16384].bitcast(F32)[0:1, 0:TOK],
        "t1a": ARN[:, 16384:18432].bitcast(F32)[:, 0:512],
        "t1b": ARN[:, 16384:18432].bitcast(F32)[:, 512:1024],
        "resb": ARN[:, 18432:22528].bitcast(F32),
    }
    emit_ln_fm_stream(mk, consts, "lnc", xT_in, xln, lnb)
    xk = lambda j, c: ("lnc", "xln", j, c)

    for ot in range(24):
        kind = 0 if ot < 8 else (1 if ot < 16 else 2)
        jd = ot % 8
        for c in range(NC4):
            cpar = c % 2

            def pmm(e, ot=ot, c=c, cpar=cpar):
                last = None
                for j in range(NJ):
                    last = e.matmul(mk.pb(cpar),
                                    wq[:, ot * 1024 + j * 128: ot * 1024 + j * 128 + 128],
                                    xln[:, j * TOK + c * 512: j * TOK + c * 512 + 512],
                                    start=(j == 0), stop=(j == NJ - 1))
                return last
            mk.emit("T", pmm, reads=[("wq", ot)] + [xk(j, c) for j in range(NJ)],
                    writes=[("pb", cpar)])
            if kind < 2:
                dst = qbuf if kind == 0 else kbuf
                mk.emit("S", (lambda cpar=cpar: lambda e: e.copy(
                    braw[:, cpar * 512:(cpar + 1) * 512], mk.pb(cpar)))(),
                    reads=[("pb", cpar)], writes=[("braw", cpar)])
                mk.emit("T", (lambda cpar=cpar: lambda e: e.matmul(
                    mk.pb(2 + cpar), protb[:],
                    braw[:, cpar * 512:(cpar + 1) * 512], start=True, stop=True))(),
                    reads=[("braw", cpar), ("c", "prot")], writes=[("pb", 2 + cpar)])
                mk.emit("V", (lambda c=c, cpar=cpar: lambda e: e.tensor_mul(
                    t1f[:, cpar * 512:(cpar + 1) * 512],
                    braw[:, cpar * 512:(cpar + 1) * 512],
                    cos2[:, c * 512:(c + 1) * 512]))(),
                    reads=[("braw", cpar), ("c", "cos")], writes=[("t1", cpar)])
                mk.emit("V", (lambda c=c, cpar=cpar: lambda e: e.tensor_mul(
                    t2f[:, cpar * 512:(cpar + 1) * 512], mk.pb(2 + cpar),
                    sin2[:, c * 512:(c + 1) * 512]))(),
                    reads=[("pb", 2 + cpar), ("c", "sin")], writes=[("t2", cpar)])
                jp = jd % 2
                mk.emit("V", (lambda dst=dst, jp=jp, c=c, cpar=cpar: lambda e: e.tensor_add(
                    dst[:, jp * TOK + c * 512: jp * TOK + c * 512 + 512],
                    t1f[:, cpar * 512:(cpar + 1) * 512],
                    t2f[:, cpar * 512:(cpar + 1) * 512]))(),
                    reads=[("t1", cpar), ("t2", cpar)], writes=[("qk", kind, jp, c)])
            else:
                mk.emit("V", (lambda cpar=cpar: lambda e: e.tensor_copy(
                    vf32[:, cpar * 512:(cpar + 1) * 512], mk.pb(cpar)))(),
                    reads=[("pb", cpar)], writes=[("vf32", cpar)])

                def vtr(e, cpar=cpar):
                    last = None
                    for i in range(4):
                        last = e.transpose(
                            mk.pool[:, 2048 + cpar * 1024 + i * 128: 2048 + cpar * 1024 + i * 128 + 128],
                            vf32[:, cpar * 512 + i * 128: cpar * 512 + i * 128 + 128], eyef[:])
                    return last
                mk.emit("T", vtr, reads=[("vf32", cpar), ("c", "eyef")],
                        writes=[("pb", 4 + 2 * cpar), ("pb", 5 + 2 * cpar)])

                jp = jd % 2

                def vev(e, jp=jp, c=c, cpar=cpar):
                    last = None
                    for i in range(4):
                        tt = c * 4 + i
                        last = e.copy(vbuf[:, jp * TOK + tt * 128: jp * TOK + tt * 128 + 128],
                                      mk.pool[:, 2048 + cpar * 1024 + i * 128: 2048 + cpar * 1024 + i * 128 + 128])
                    return last
                mk.emit("S", vev, reads=[("pb", 4 + 2 * cpar), ("pb", 5 + 2 * cpar)],
                        writes=[("qk", 2, jp, c)])
        jp = jd % 2
        if kind == 0:
            mk.emit("Yq", (lambda jd=jd, jp=jp: lambda e: e.dma_start(
                out=qT_out[jd * 128:(jd + 1) * 128, :], in_=qbuf[:, jp * TOK:(jp + 1) * TOK]))(),
                reads=[("qk", 0, jp, c) for c in range(NC4)], writes=[("qk", 0, jp, "st")])
        elif kind == 1:
            mk.emit("Yq", (lambda jd=jd, jp=jp: lambda e: e.dma_start(
                out=kT_out[jd * 128:(jd + 1) * 128, :], in_=kbuf[:, jp * TOK:(jp + 1) * TOK]))(),
                reads=[("qk", 1, jp, c) for c in range(NC4)], writes=[("qk", 1, jp, "st")])
        else:
            mk.emit("Yq", (lambda jd=jd, jp=jp: lambda e: e.dma_start(
                out=v_out.rearrange("(tt p) d -> p tt d", p=128)[:, :, jd * 128:(jd + 1) * 128],
                in_=vbuf[:, jp * TOK:(jp + 1) * TOK].rearrange("p (tt f) -> p tt f", f=128)))(),
                reads=[("qk", 2, jp, c) for c in range(NC4)], writes=[("qk", 2, jp, "st")])
    mk.wait_only("Y", [("qk", k, jp, "st") for k in range(3) for jp in range(2)])


def build_C1():
    mk = MK()
    io = {
        "xT": mk.dram("xT", [D, TOK], F32, kind="ExternalInput").ap(),
        "wqkv": mk.dram("wqkv", [D, 3 * D], F32, kind="ExternalInput").ap(),
        "cos2": mk.dram("cos2", [128, TOK], F32, kind="ExternalInput").ap(),
        "sin2": mk.dram("sin2", [128, TOK], F32, kind="ExternalInput").ap(),
        "prot": mk.dram("prot", [128, 128], F32, kind="ExternalInput").ap(),
        "qT": mk.dram("qT", [D, TOK], BF16, kind="ExternalOutput").ap(),
        "kT": mk.dram("kT", [D, TOK], BF16, kind="ExternalOutput").ap(),
        "v": mk.dram("v", [TOK, D], BF16, kind="ExternalOutput").ap(),
    }
    eyef_in = mk.dram("eyef", [128, 128], F32, kind="ExternalInput").ap()
    mk.psum_pool()
    consts = emit_consts(mk)
    eyef = mk.sbuf("eyef_s", [128, 128], F32)
    mk.emit("Yq", lambda e: e.dma_start(out=eyef[:], in_=eyef_in[:, :]), writes=[("c", "eyef")])
    mk.init_arena()
    emit_phase_C1(mk, consts, eyef, io)
    return mk



def emit_phase_C2(mk, consts, eyef, io):
    xT_in, qT_in, kTf_in, vf_in, wo_in = (io["xT"], io["qT"], io["kTf"], io["vf"], io["wout"])
    x3_out = io["x3_out"]
    x3_out2 = io.get("x3_out2")
    wo = mk.asbuf("wo_s", [128, NJ * 1024], BF16)
    for j in range(NJ):
        mk.emit("Gq", (lambda j=j: lambda e: e.dma_start(
            out=wo[:, j * 1024:(j + 1) * 1024], in_=wo_in[j * 128:(j + 1) * 128, :]))(),
            writes=[("wo", j)])

    fattn = mk.asbuf("fattn", [128, NJ * TOK], BF16)
    ftm = mk.asbuf("ftm_c2", [128, NTT * 1024], BF16)
    pT = mk.asbuf("pT_c2", [128, 3 * 1024], BF16)
    denr = mk.asbuf("denr_c2", [1, 1024], F32)
    bsb = mk.asbuf("bsb_c2", [128, 4 * 512], F32)
    ARN = mk.asbuf("arn_c2", [128, 36864], BF16)
    kh = ARN[:, 0:8192]
    vh = ARN[:, 8192:16512]
    qh = ARN[:, 16512:18560]
    mk.emit("G", lambda e: e.memset(vh[:, :], 1.0), writes=[("vhones",)])

    SC = 1.0 / np.sqrt(64.0)
    for hp in range(NJ):
        mk.emit("Yq", (lambda hp=hp: lambda e: e.dma_start(
            out=kh[:, :], in_=kTf_in[hp * 128:(hp + 1) * 128, :]))(),
            reads=[("kh",)], writes=[("kh",)])
        for hh in range(2):
            mk.emit("Yq", (lambda hp=hp, hh=hh: lambda e: e.dma_start(
                out=vh[:, :].rearrange("p (kt g) -> p kt g", g=130)[:, :, hh * 65:hh * 65 + 64],
                in_=vf_in.rearrange("(kt p) d -> p kt d", p=128)[:, :, hp * 128 + hh * 64: hp * 128 + hh * 64 + 64]))(),
                reads=[("vh",), ("vhones",)], writes=[("vh",)])
        mk.emit("Yq", (lambda hp=hp: lambda e: e.dma_start(
            out=qh[:, :], in_=qT_in[hp * 128:(hp + 1) * 128, :]))(),
            reads=[("qh",)], writes=[("qh",)])
        for qc in range(NC4):
            qpar = qc % 2
            obank = 6 + qpar
            def emit_scexp(kt, qc=qc):
                spar = kt % 2
                sb0 = spar * 2
                ppar = kt % 3

                def scmm(e, qc=qc, kt=kt, sb0=sb0):
                    last = None
                    for hh in range(2):
                        last = e.matmul(mk.pb(sb0 + hh),
                                        kh[hh * 64:hh * 64 + 64, kt * 128: kt * 128 + 128],
                                        qh[hh * 64:hh * 64 + 64, qc * 512: qc * 512 + 512],
                                        start=True, stop=True, tile_position=(hh * 64, 0))
                    return last
                mk.emit("T", scmm, reads=[("kh",), ("qh",), ("pb", sb0), ("pb", sb0 + 1)],
                        writes=[("pb", sb0), ("pb", sb0 + 1)])

                def expop(e, sb0=sb0, ppar=ppar):
                    return e.activation(pT[:, ppar * 1024:(ppar + 1) * 1024],
                                        mk.pool[:, sb0 * 512:sb0 * 512 + 1024],
                                        AF.Exp, scale=SC)
                mk.emit("S", expop, reads=[("pb", sb0), ("pb", sb0 + 1)], writes=[("pT", ppar)])

            def emit_pv(kt, qpar=qpar):
                ppar = kt % 3
                ob0 = 4 + qpar
                ob1 = 6 + qpar

                def pv(e, kt=kt, ppar=ppar, ob0=ob0, ob1=ob1):
                    last = None
                    for hh, ob in ((0, ob0), (1, ob1)):
                        pslice = pT[:, ppar * 1024 + hh * 512: ppar * 1024 + hh * 512 + 512]
                        last = e.matmul(mk.pool[0:65, ob * 512: ob * 512 + 512],
                                        vh[:, kt * 130 + hh * 65: kt * 130 + hh * 65 + 65],
                                        pslice, start=(kt == 0), stop=(kt == NKT - 1))
                    return last
                mk.emit("T", pv, reads=[("pT", ppar), ("vh",), ("pb", ob0), ("pb", ob1)],
                        writes=[("pb", ob0), ("pb", ob1)])

            for kt in range(NKT + 1):
                if kt < NKT:
                    emit_scexp(kt)
                if kt >= 1:
                    emit_pv(kt - 1)
            ob0 = 4 + qpar
            ob1 = 6 + qpar
            mk.emit("V", (lambda ob0=ob0: lambda e: e.reciprocal(
                denr[0:1, 0:512], mk.pool[64:65, ob0 * 512: ob0 * 512 + 512]))(),
                reads=[("pb", ob0)], writes=[("denr", 0)])
            mk.emit("V", (lambda ob1=ob1: lambda e: e.reciprocal(
                denr[0:1, 512:1024], mk.pool[64:65, ob1 * 512: ob1 * 512 + 512]))(),
                reads=[("pb", ob1)], writes=[("denr", 1)])
            sb0 = 0 if qpar == 0 else 2

            def dbc(e, sb0=sb0):
                e.matmul(mk.pool[0:64, sb0 * 512: sb0 * 512 + 512], consts["ones1f"][0:1, 0:64],
                         denr[0:1, 0:512], start=True, stop=True)
                return e.matmul(mk.pool[64:128, sb0 * 512: sb0 * 512 + 512], consts["ones1f"][0:1, 0:64],
                                denr[0:1, 512:1024], start=True, stop=True)
            mk.emit("T", dbc, reads=[("denr", 0), ("denr", 1), ("c", "ones1f"),
                                     ("pb", sb0), ("pb", sb0 + 1)],
                    writes=[("pb", sb0)])
            s0 = (qpar * 2) * 512
            s1 = (qpar * 2 + 1) * 512
            mk.emit("V", (lambda sb0=sb0, s0=s0: lambda e: e.tensor_copy(
                bsb[0:64, s0:s0 + 512], mk.pool[0:64, sb0 * 512: sb0 * 512 + 512]))(),
                reads=[("pb", sb0)], writes=[("bsb", qpar, 0)])
            mk.emit("V", (lambda sb0=sb0, s1=s1: lambda e: e.tensor_copy(
                bsb[0:64, s1:s1 + 512], mk.pool[64:128, sb0 * 512: sb0 * 512 + 512]))(),
                reads=[("pb", sb0)], writes=[("bsb", qpar, 1)])
            mk.emit("V", (lambda hp=hp, qc=qc, ob0=ob0, s0=s0: lambda e: e.tensor_mul(
                fattn[:, hp * TOK + qc * 512: hp * TOK + qc * 512 + 512][0:64, :],
                mk.pool[0:64, ob0 * 512: ob0 * 512 + 512],
                bsb[0:64, s0:s0 + 512]))(),
                reads=[("bsb", qpar, 0), ("pb", ob0)], writes=[("fa", hp, qc)])
            mk.emit("V", (lambda hp=hp, qc=qc, ob1=ob1, s1=s1: lambda e: e.tensor_mul(
                fattn[:, hp * TOK + qc * 512: hp * TOK + qc * 512 + 512][64:128, :],
                mk.pool[0:64, ob1 * 512: ob1 * 512 + 512],
                bsb[0:64, s1:s1 + 512]))(),
                reads=[("bsb", qpar, 1), ("pb", ob1), ("fa", hp, qc)], writes=[("fa", hp, qc)])

    for tt in range(NTT):
        for oc in range(2):
            pbk = (tt * 2 + oc) % 2

            def womm(e, tt=tt, oc=oc, pbk=pbk):
                last = None
                for j in range(NJ):
                    last = e.matmul(mk.pb(pbk),
                                    fattn[:, j * TOK + tt * 128: j * TOK + tt * 128 + 128],
                                    wo[:, j * 1024 + oc * 512: j * 1024 + oc * 512 + 512],
                                    start=(j == 0), stop=(j == NJ - 1))
                return last
            mk.emit("T", womm,
                    reads=[("fa", j, tt // 4) for j in range(NJ)] + [("wo", j) for j in range(NJ)] +
                          [("pb", pbk)],
                    writes=[("pb", pbk)])
            mk.emit("S", (lambda tt=tt, oc=oc, pbk=pbk: lambda e: e.copy(
                ftm[:, tt * 1024 + oc * 512: tt * 1024 + oc * 512 + 512], mk.pb(pbk)))(),
                reads=[("pb", pbk)], writes=[("ftm", tt, oc)])

    mk.barrier()
    av = {
        "sqt": ARN[:, 18560:20608][:, 0:1024],
        "lnf": ARN[:, 20608:24704].bitcast(F32)[:, 0:1024],
        "svec": ARN[:, 24704:24832].bitcast(F32)[:, 0:48],
        "lnfT": ARN[:, 0:16384],
        "resb": ARN[:, 24832:28928].bitcast(F32)[:, 0:TOK],
        "outb": ARN[:, 28928:33024].bitcast(F32)[:, 0:TOK],
    }
    emit_postln_tm(mk, consts, eyef, ftm,
                   lambda tt: [("ftm", tt, 0), ("ftm", tt, 1)],
                   av, xT_in, x3_out, tag="plC", out_dram2=x3_out2)


def build_C2():
    mk = MK()
    io = {
        "xT": mk.dram("xT", [D, TOK], F32, kind="ExternalInput").ap(),
        "qT": mk.dram("qT", [D, TOK], BF16, kind="ExternalInput").ap(),
        "kTf": mk.dram("kTf", [D, S], BF16, kind="ExternalInput").ap(),
        "vf": mk.dram("vf", [S, D], BF16, kind="ExternalInput").ap(),
        "wout": mk.dram("wout", [D, D], F32, kind="ExternalInput").ap(),
        "x3_out": mk.dram("x3T", [D, TOK], F32, kind="ExternalOutput").ap(),
    }
    eyef_in = mk.dram("eyef", [128, 128], F32, kind="ExternalInput").ap()
    mk.psum_pool()
    consts = emit_consts(mk)
    eyef = mk.sbuf("eyef_s", [128, 128], F32)
    mk.emit("Yq", lambda e: e.dma_start(out=eyef[:], in_=eyef_in[:, :]), writes=[("c", "eyef")])
    mk.init_arena()
    emit_phase_C2(mk, consts, eyef, io)
    return mk




NLP = 8


def emit_phase_D1(mk, consts, eyef, io):
    xT_in, nwT_in, nb_in, w1_in, b1_in, w2_in, b2_in, selm_in = (
        io["xT"], io["nwT3"], io["nb3"], io["w1f"], io["b1f"], io["w2f"], io["b2f"], io["selm"])
    qT_out, kT_out, v_out = io["qT"], io["kT"], io["v"]
    selm = mk.asbuf("selm_s", [16, 16 * 64], BF16)
    mk.emit("Gq", lambda e: e.dma_start(out=selm[:], in_=selm_in[:, :]), writes=[("c", "selm")])
    nwT = mk.asbuf("nwT_s", [128, 3 * NJ * 15], BF16)
    mk.emit("Gq", lambda e: e.dma_start(out=nwT[:, :].rearrange("p (x j r) -> p x j r", x=3, j=NJ),
                                        in_=nwT_in.rearrange("x (j p) r -> p x j r", p=128)),
            writes=[("c", "nwT")])
    nbs = mk.asbuf("nb_s", [15, 3], F32)
    mk.emit("Yq", lambda e: e.dma_start(out=nbs[:, :], in_=nb_in[:, :]),
            writes=[("c", "nb")])
    b1s = mk.asbuf("b1d_s", [128, 3 * NLP], F32)
    mk.emit("Yq", lambda e: e.dma_start(out=b1s[:, :].rearrange("p (x l) -> p x l", x=3),
                                        in_=b1_in.rearrange("x p l -> p x l")),
            writes=[("c", "b1")])
    b2s = mk.asbuf("b2d_s", [16, 3 * D], BF16)
    mk.emit("Gq", lambda e: e.dma_start(out=b2s[:, :].rearrange("p (x o) -> p x o", x=3),
                                        in_=b2_in.rearrange("x p o -> p x o")),
            writes=[("c", "b2")])

    xln = mk.asbuf("xln_d1", [128, NJ * TOK], BF16)
    mH = mk.asbuf("mH", [128, NLP * TOK], BF16)
    selT = mk.asbuf("selT", [16, TOK], BF16)
    sTM = mk.asbuf("sTM", [128, NTT * 16], F32)
    sel3f = mk.asbuf("sel3f", [128, NTT * 16], F32)
    tmpT = mk.asbuf("tmpT", [128, NTT * 16], F32)
    hp_ = mk.asbuf("hp_", [128, 2 * 512], BF16)
    qkbuf = mk.asbuf("qkbuf", [128, 2 * TOK], BF16)
    vtm = mk.asbuf("vtm_d1", [128, NTT * 1024], BF16)
    w1w = mk.asbuf("w1w", [128, NLP * 1024], BF16)
    w2w = mk.asbuf("w2w", [128, NLP * 1024], BF16)
    ARN = mk.asbuf("arn_d1", [128, 22528], BF16)
    lnb = {
        "xb": ARN[:, 0:4096],
        "xq": ARN[:, 4096:8192],
        "absf": ARN[:, 0:8192].bitcast(F32),
        "arow": ARN[:, 8192:12288].bitcast(F32)[0:1, 0:TOK],
        "brow": ARN[:, 12288:16384].bitcast(F32)[0:1, 0:TOK],
        "t1a": ARN[:, 16384:18432].bitcast(F32)[:, 0:512],
        "t1b": ARN[:, 16384:18432].bitcast(F32)[:, 512:1024],
        "resb": ARN[:, 18432:22528].bitcast(F32),
    }

    def load_wf(f):
        for lp in range(NLP):
            mk.emit("Gq", (lambda f=f, lp=lp: lambda e: e.dma_start(
                out=w1w[:, lp * 1024:(lp + 1) * 1024].rearrange("p (j g) -> p j g", g=128),
                in_=w1_in[f, lp].rearrange("(j p) g -> p j g", p=128)))(),
                reads=[("w1w", lp)], writes=[("w1w", lp)])
            mk.emit("Gq", (lambda f=f, lp=lp: lambda e: e.dma_start(
                out=w2w[:, lp * 1024:(lp + 1) * 1024], in_=w2_in[f, lp]))(),
                reads=[("w2w", lp)], writes=[("w2w", lp)])

    load_wf(0)
    emit_ln_fm_stream(mk, consts, "lnd", xT_in, xln, lnb)
    xk = lambda j, c: ("lnd", "xln", j, c)
    mk.barrier()
    sT = ARN[:, 0:4096].bitcast(F32)[0:15, 0:TOK]

    for f in range(3):
        def nsmm(e, f=f):
            last = None
            for c in range(NC4):
                for j in range(NJ):
                    last = e.matmul(mk.pool[0:15, 2048 + c * 512:2048 + c * 512 + 512],
                                    nwT[:, (f * NJ + j) * 15:(f * NJ + j) * 15 + 15],
                                    xln[:, j * TOK + c * 512: j * TOK + c * 512 + 512],
                                    start=(j == 0), stop=(j == NJ - 1))
            return last
        mk.emit("T", nsmm, reads=[("c", "nwT")] + [xk(j, c) for j in range(NJ) for c in range(NC4)],
                writes=[("pb", 4), ("pb", 5), ("pb", 6), ("pb", 7)])
        mk.emit("V", (lambda f=f: lambda e: e.tensor_scalar(
            sT[:, :], mk.pool[0:15, 2048:2048 + TOK], nbs[0:15, f:f + 1], None, ALU.add))(),
            reads=[("pb", i) for i in range(4, 8)] + [("c", "nb")], writes=[("sT",)])

        def str_(e):
            last = None
            for tt in range(NTT):
                last = e.transpose(mk.pool[:, tt * 16: tt * 16 + 15],
                                   sT[:, tt * 128:(tt + 1) * 128], eyef[0:15, 0:15])
            return last
        mk.emit("T", str_, reads=[("sT",), ("c", "eyef")],
                writes=[("pb", 0), ("pb", 1), ("pb", 2), ("pb", 3)])
        mk.emit("V", lambda e: e.tensor_copy(sTM[:, :].rearrange("p (t g) -> p t g", g=16)[:, :, 0:15],
                                             mk.pool[:, 0:NTT * 16].rearrange("p (t g) -> p t g", g=16)[:, :, 0:15]),
                reads=[("pb", 0), ("pb", 1), ("pb", 2), ("pb", 3)], writes=[("sTM",)])
        mk.emit("V", lambda e: e.tensor_single_scalar(sTM[:, :], sTM[:, :], 0.0, ALU.is_ge),
                reads=[("sTM",)], writes=[("sg",)])
        s3 = sTM[:, :].rearrange("p (t g) -> p t g", g=16)
        sel3 = sel3f[:, :].rearrange("p (t g) -> p t g", g=16)
        tp3 = tmpT[:, :].rearrange("p (t g) -> p t g", g=16)

        def bc(ap):
            return ap.broadcast_to([128, NTT, ap.shape[2]]) if False else ap

        mk.emit("V", lambda e: e.tensor_scalar(sel3[:, :, 0:8],
                                               s3[:, :, 0:1].broadcast_to([128, NTT, 8]),
                                               -1.0, 1.0, ALU.mult, ALU.add),
                reads=[("sg",)], writes=[("sel",)])
        mk.emit("V", lambda e: e.tensor_copy(sel3[:, :, 8:16],
                                             s3[:, :, 0:1].broadcast_to([128, NTT, 8])),
                reads=[("sg",), ("sel",)], writes=[("sel",)])
        for lev, base, wid in ((1, 1, 4), (2, 3, 2), (3, 7, 1)):
            ncomb = 2 ** lev
            for comb in range(ncomb):
                node = base + comb
                for b_ in range(2):
                    lo = (comb * 2 + b_) * wid
                    if b_ == 1:
                        mk.emit("V", (lambda node=node, lo=lo, wid=wid: lambda e: e.tensor_copy(
                            tp3[:, :, lo:lo + wid],
                            s3[:, :, node:node + 1].broadcast_to([128, NTT, wid])))(),
                            reads=[("sg",), ("tmpT",)], writes=[("tmpT",)])
                    else:
                        mk.emit("V", (lambda node=node, lo=lo, wid=wid: lambda e: e.tensor_scalar(
                            tp3[:, :, lo:lo + wid],
                            s3[:, :, node:node + 1].broadcast_to([128, NTT, wid]),
                            -1.0, 1.0, ALU.mult, ALU.add))(),
                            reads=[("sg",), ("tmpT",)], writes=[("tmpT",)])
            mk.emit("V", lambda e: e.tensor_mul(sel3f[:, :], sel3f[:, :], tmpT[:, :]),
                    reads=[("sel",), ("tmpT",)], writes=[("sel",)])

        def seltr(e):
            last = None
            for tt in range(NTT):
                last = e.transpose(mk.pool[0:16, tt * 128: tt * 128 + 128],
                                   sel3f[:, tt * 16:(tt + 1) * 16], eyef[:])
            return last
        mk.emit("T", seltr, reads=[("sel",), ("c", "eyef")],
                writes=[("pb", 0), ("pb", 1), ("pb", 2), ("pb", 3)])
        mk.emit("S", lambda e: e.copy(selT[:, :], mk.pool[0:16, 0:TOK]),
                reads=[("pb", 0), ("pb", 1), ("pb", 2), ("pb", 3)], writes=[("selT",)])

        for lp in range(NLP):
            for c in range(NC4):
                cb = c % 2

                def h1mm(e, lp=lp, c=c, cb=cb):
                    last = None
                    for j in range(NJ):
                        last = e.matmul(mk.pb(cb),
                                        w1w[:, lp * 1024 + j * 128: lp * 1024 + j * 128 + 128],
                                        xln[:, j * TOK + c * 512: j * TOK + c * 512 + 512],
                                        start=(j == 0), stop=(j == NJ - 1))
                    return last
                mk.emit("T", h1mm, reads=[("w1w", lp)] + [xk(j, c) for j in range(NJ)],
                        writes=[("pb", cb)])
                mk.emit("S", (lambda f=f, lp=lp, cb=cb: lambda e: e.activation(
                    hp_[:, cb * 512:(cb + 1) * 512], mk.pb(cb), AF.Relu,
                    bias=b1s[:, f * NLP + lp: f * NLP + lp + 1]))(),
                    reads=[("pb", cb), ("c", "b1")], writes=[("hp", cb)])

                def mmm(e, lp=lp, c=c, cb=cb):
                    e.matmul(mk.pool[0:64, (2 + cb) * 512:(2 + cb) * 512 + 512],
                             selm[:, (2 * lp) * 64:(2 * lp) * 64 + 64],
                             selT[:, c * 512:(c + 1) * 512], start=True, stop=True)
                    return e.matmul(mk.pool[64:128, (2 + cb) * 512:(2 + cb) * 512 + 512],
                                    selm[:, (2 * lp + 1) * 64:(2 * lp + 1) * 64 + 64],
                                    selT[:, c * 512:(c + 1) * 512], start=True, stop=True)
                mk.emit("T", mmm, reads=[("selT",), ("c", "selm"), ("pb", 2 + cb)],
                        writes=[("pb", 2 + cb)])
                mk.emit("V", (lambda lp=lp, c=c, cb=cb: lambda e: e.tensor_mul(
                    mH[:, lp * TOK + c * 512: lp * TOK + c * 512 + 512],
                    hp_[:, cb * 512:(cb + 1) * 512], mk.pb(2 + cb)))(),
                    reads=[("hp", cb), ("pb", 2 + cb)], writes=[("mH", lp, c)])

        if f < 2:
            dst_out = qT_out if f == 0 else kT_out
            for ot in range(NJ):
                op = ot % 2
                for c in range(NC4):
                    cb = 4 + (c % 2)

                    def w2mm(e, f=f, ot=ot, c=c, cb=cb):
                        for lp in range(NLP):
                            e.matmul(mk.pb(cb),
                                     w2w[:, lp * 1024 + ot * 128: lp * 1024 + ot * 128 + 128],
                                     mH[:, lp * TOK + c * 512: lp * TOK + c * 512 + 512],
                                     start=(lp == 0), stop=False)
                        return e.matmul(mk.pb(cb), b2s[0:16, f * D + ot * 128: f * D + ot * 128 + 128],
                                        selT[:, c * 512:(c + 1) * 512], start=False, stop=True)
                    mk.emit("T", w2mm,
                            reads=[("mH", lp, c) for lp in range(NLP)] +
                                  [("w2w", lp) for lp in range(NLP)] +
                                  [("selT",), ("c", "b2"), ("pb", cb)],
                            writes=[("pb", cb)])
                    mk.emit("S", (lambda ot=ot, op=op, c=c, cb=cb: lambda e: e.copy(
                        qkbuf[:, op * TOK + c * 512: op * TOK + c * 512 + 512], mk.pb(cb)))(),
                        reads=[("pb", cb)], writes=[("qk", op, c)])
                mk.emit("Yq", (lambda dst_out=dst_out, ot=ot, op=op: lambda e: e.dma_start(
                    out=dst_out[ot * 128:(ot + 1) * 128, :],
                    in_=qkbuf[:, op * TOK:(op + 1) * TOK]))(),
                    reads=[("qk", op, c) for c in range(NC4)], writes=[("qkst", op)])
        else:
            for c in range(NC4):
                for ttl in range(4):
                    tt = c * 4 + ttl
                    for oc in range(2):
                        cb = 4 + ((ttl * 2 + oc) % 2)

                        def vmm(e, c=c, ttl=ttl, oc=oc, cb=cb):
                            for lp in range(NLP):
                                e.matmul(mk.pb(cb),
                                         mH[:, lp * TOK + c * 512 + ttl * 128: lp * TOK + c * 512 + ttl * 128 + 128],
                                         w2w[:, lp * 1024 + oc * 512: lp * 1024 + oc * 512 + 512],
                                         start=(lp == 0), stop=False)
                            return e.matmul(mk.pb(cb),
                                            selT[:, c * 512 + ttl * 128: c * 512 + ttl * 128 + 128],
                                            b2s[0:16, 2 * D + oc * 512: 2 * D + oc * 512 + 512],
                                            start=False, stop=True)
                        mk.emit("T", vmm,
                                reads=[("mH", lp, c) for lp in range(NLP)] +
                                      [("w2w", lp) for lp in range(NLP)] +
                                      [("selT",), ("c", "b2"), ("pb", cb)],
                                writes=[("pb", cb)])
                        mk.emit("S", (lambda tt=tt, oc=oc, cb=cb: lambda e: e.copy(
                            vtm[:, tt * 1024 + oc * 512: tt * 1024 + oc * 512 + 512], mk.pb(cb)))(),
                            reads=[("pb", cb)], writes=[("vt", tt, oc)])
        if f < 2:
            load_wf(f + 1)
    mk.emit("Yq", lambda e: e.dma_start(
        out=v_out.rearrange("(tt p) d -> p tt d", p=128),
        in_=vtm[:, :].rearrange("p (tt d) -> p tt d", d=1024)),
        reads=[("vt", tt, oc) for tt in range(NTT) for oc in range(2)], writes=[("vst",)])
    mk.wait_only("Y", [("vst",), ("qkst", 0), ("qkst", 1)])


def build_D1():
    mk = MK()
    io = {
        "xT": mk.dram("xT", [D, TOK], F32, kind="ExternalInput").ap(),
        "nwT3": mk.dram("nwT3", [3, D, 15], F32, kind="ExternalInput").ap(),
        "nb3": mk.dram("nb3", [15, 3], F32, kind="ExternalInput").ap(),
        "w1f": mk.dram("w1f", [3, NLP, D, 128], F32, kind="ExternalInput").ap(),
        "b1f": mk.dram("b1f", [3, 128, NLP], F32, kind="ExternalInput").ap(),
        "w2f": mk.dram("w2f", [3, NLP, 128, D], F32, kind="ExternalInput").ap(),
        "b2f": mk.dram("b2f", [3, 16, D], F32, kind="ExternalInput").ap(),
        "selm": mk.dram("selm", [16, 16 * 64], F32, kind="ExternalInput").ap(),
        "qT": mk.dram("qT", [D, TOK], BF16, kind="ExternalOutput").ap(),
        "kT": mk.dram("kT", [D, TOK], BF16, kind="ExternalOutput").ap(),
        "v": mk.dram("v", [TOK, D], BF16, kind="ExternalOutput").ap(),
    }
    eyef_in = mk.dram("eyef", [128, 128], F32, kind="ExternalInput").ap()
    mk.psum_pool()
    consts = emit_consts(mk)
    eyef = mk.sbuf("eyef_s", [128, 128], F32)
    mk.emit("Yq", lambda e: e.dma_start(out=eyef[:], in_=eyef_in[:, :]), writes=[("c", "eyef")])
    mk.init_arena()
    emit_phase_D1(mk, consts, eyef, io)
    return mk



def build_D2():
    mk = MK()
    xT_in = mk.dram("xT", [D, TOK], F32, kind="ExternalInput").ap()
    qT_in = mk.dram("qT", [D, TOK], BF16, kind="ExternalInput").ap()
    kTf_in = mk.dram("kTf", [D, S], BF16, kind="ExternalInput").ap()
    vf_in = mk.dram("vf", [S, D], BF16, kind="ExternalInput").ap()
    eyef_in = mk.dram("eyef", [128, 128], F32, kind="ExternalInput").ap()
    out_tm = mk.dram("xout", [TOK, D], F32, kind="ExternalOutput").ap()

    mk.psum_pool()
    consts = emit_consts(mk)
    eyef = mk.sbuf("eyef_s", [128, 128], F32)
    mk.emit("Yq", lambda e: e.dma_start(out=eyef[:], in_=eyef_in[:, :]), writes=[("c", "eyef")])
    qT = mk.sbuf("qT_s", [128, NJ * TOK], BF16)
    for j in range(NJ):
        mk.emit("Yq", (lambda j=j: lambda e: e.dma_start(
            out=qT[:, j * TOK:(j + 1) * TOK], in_=qT_in[j * 128:(j + 1) * 128, :]))(),
            writes=[("qT", j)])
    ftm = mk.sbuf("ftm", [128, NTT * 1024], BF16)
    denr = mk.sbuf("denr", [1, 2 * 256], F32)
    rcols = mk.sbuf("rcols", [128, 2 * 2], F32)
    onesc = mk.sbuf("onesc", [1, 1], F32)
    mk.emit("G", lambda e: e.memset(onesc[:], 1.0), writes=[("c", "o1")])
    ARN = mk.sbuf("arn", [128, 40960], BF16)
    ppan = ARN[:, 0:32768]
    kbb = ARN[:, 32768:36864]
    vbb = ARN[:, 36864:40960]

    SC = 1.0 / np.sqrt(1024.0)
    QCN = TOK // 256

    def p1_load(qc, ktp):
        par = (ktp // 2) % 2
        mk.emit("Yq", (lambda ktp=ktp, par=par: lambda e: e.dma_start(
            out=kbb[:, par * 2048:(par + 1) * 2048].rearrange("p (j i f) -> p j i f", j=NJ, i=2),
            in_=kTf_in[:, ktp * 128:(ktp + 2) * 128].rearrange("(j p) (i f) -> p j i f", p=128, i=2)))(),
            reads=[("kbb", par)], writes=[("kbb", par)])

    def p1_sc(qc, kt):
        pp = qc % 2
        par = (kt // 2) % 2
        i = kt % 2
        sb = 4 + (kt % 2)

        def scmm(e, qc=qc, par=par, i=i, sb=sb):
            last = None
            for j in range(NJ):
                last = e.matmul(mk.pool[:, sb * 512: sb * 512 + 256],
                                kbb[:, par * 2048 + j * 256 + i * 128: par * 2048 + j * 256 + i * 128 + 128],
                                qT[:, j * TOK + (qc // 2) * 512 + (qc % 2) * 256:
                                   j * TOK + (qc // 2) * 512 + (qc % 2) * 256 + 256],
                                start=(j == 0), stop=(j == NJ - 1))
            return last
        mk.emit("T", scmm, reads=[("kbb", par), ("pb2", sb)] + [("qT", j) for j in range(NJ)],
                writes=[("pb2", sb)])
        mk.emit("S", (lambda qc=qc, kt=kt, sb=sb, pp=pp: lambda e: e.activation(
            ppan[:, pp * 16384 + kt * 256: pp * 16384 + kt * 256 + 256],
            mk.pool[:, sb * 512: sb * 512 + 256], AF.Exp, scale=SC))(),
            reads=[("pb2", sb)], writes=[("pp", pp, kt)])
        mk.emit("T", (lambda kt=kt, pp=pp, qc=qc: lambda e: e.matmul(
            mk.pool[0:1, 6 * 512 + (qc % 2) * 256: 6 * 512 + (qc % 2) * 256 + 256],
            consts["onesb"][:], ppan[:, pp * 16384 + kt * 256: pp * 16384 + kt * 256 + 256],
            start=(kt == 0), stop=(kt == NKT - 1)))(),
            reads=[("pp", pp, kt), ("c", "onesb"), ("pbden", qc % 2)], writes=[("pbden", qc % 2)])

    def p1_fin(qc):
        qp = qc % 2
        mk.emit("V", (lambda qp=qp: lambda e: e.reciprocal(
            denr[0:1, qp * 256:(qp + 1) * 256],
            mk.pool[0:1, 6 * 512 + qp * 256: 6 * 512 + qp * 256 + 256]))(),
            reads=[("pbden", qp)], writes=[("denr", qp)])

        def rtr(e, qp=qp):
            last = None
            for qt in range(2):
                last = e.matmul(mk.pool[:, 7 * 512 + qp * 2 + qt: 7 * 512 + qp * 2 + qt + 1],
                                denr[0:1, qp * 256 + qt * 128: qp * 256 + (qt + 1) * 128],
                                onesc[0:1, 0:1], start=True, stop=True)
            return last
        mk.emit("T", rtr, reads=[("denr", qp), ("c", "o1"), ("pbr", qp)], writes=[("pbr", qp)])
        mk.emit("V", (lambda qp=qp: lambda e: e.tensor_copy(
            rcols[:, qp * 2:(qp + 1) * 2], mk.pool[:, 7 * 512 + qp * 2: 7 * 512 + qp * 2 + 2]))(),
            reads=[("pbr", qp)], writes=[("rcols", qp)])

    def p2_load(qc, ktp):
        par = (ktp // 2) % 2
        mk.emit("Yq", (lambda ktp=ktp, par=par: lambda e: e.dma_start(
            out=vbb[:, par * 2048:(par + 1) * 2048].rearrange("p (i f) -> p i f", i=2),
            in_=vf_in.rearrange("(kt p) d -> p kt d", p=128)[:, ktp:ktp + 2, :]))(),
            reads=[("vbb", par)], writes=[("vbb", par)])

    def p2_pv(qc, kt):
        pp = qc % 2
        par = (kt // 2) % 2
        i = kt % 2

        def pvmm(e, qc=qc, kt=kt, pp=pp, par=par, i=i):
            last = None
            for qt in range(2):
                for oc in range(2):
                    ob = qt * 2 + oc
                    last = e.matmul(
                        mk.pool[:, ob * 512: ob * 512 + 512],
                        ppan[:, pp * 16384 + kt * 256 + qt * 128: pp * 16384 + kt * 256 + qt * 128 + 128],
                        vbb[:, par * 2048 + i * 1024 + oc * 512: par * 2048 + i * 1024 + oc * 512 + 512],
                        start=(kt == 0), stop=(kt == NKT - 1))
            return last
        mk.emit("T", pvmm,
                reads=[("vbb", par), ("pp", pp, kt)] + [("pbo", b) for b in range(4)],
                writes=[("pbo", b) for b in range(4)])

    def p2_fin(qc):
        pp = qc % 2
        for qt in range(2):
            tt = qc * 2 + qt
            mk.emit("S", (lambda tt=tt, qt=qt, pp=pp: lambda e: e.activation(
                ftm[:, tt * 1024:(tt + 1) * 1024],
                mk.pool[:, qt * 1024: qt * 1024 + 1024],
                AF.Copy, scale=rcols[:, pp * 2 + qt: pp * 2 + qt + 1]))(),
                reads=[("pbo", qt * 2), ("pbo", qt * 2 + 1), ("rcols", pp)],
                writes=[("ftm", tt)])

    for qc in range(QCN + 1):
        for kt in range(NKT):
            if qc < QCN:
                if kt % 2 == 0:
                    p1_load(qc, kt)
                p1_sc(qc, kt)
            if qc >= 1:
                if kt % 2 == 0:
                    p2_load(qc - 1, kt)
                p2_pv(qc - 1, kt)
        if qc < QCN:
            p1_fin(qc)
        if qc >= 1:
            p2_fin(qc - 1)

    mk.barrier()
    sqt = ARN[:, 0:1024]
    lnf = ARN[:, 1024:3072].bitcast(F32)[:, 0:1024]
    svec = ARN[:, 3072:3200].bitcast(F32)[:, 0:48]
    rjb = ARN[:, 3200:5248].bitcast(F32)[:, 0:1024]
    outb = ARN[:, 5248:9344].bitcast(F32)[:, 0:2048]
    mvec = svec[:, 0:16]
    vvec = svec[:, 16:32]
    rvec = svec[:, 32:48]
    tag = "po"
    for tt in range(NTT):
        at = ftm[:, tt * 1024:(tt + 1) * 1024]
        mk.emit("V", (lambda at=at, tt=tt: lambda e: e.reduce_sum(
            mvec[:, tt:tt + 1], at, axis=AX))(),
            reads=[("ftm", tt)], writes=[(tag, "m", tt)])
        mk.emit("S", (lambda at=at: lambda e: e.activation(sqt[:, 0:1024], at, AF.Square))(),
            reads=[("ftm", tt)], writes=[(tag, "sq")])
        mk.emit("V", (lambda tt=tt: lambda e: e.reduce_sum(
            vvec[:, tt:tt + 1], sqt[:, 0:1024], axis=AX))(),
            reads=[(tag, "sq")], writes=[(tag, "v", tt)])
    mk.emit("V", lambda e: e.tensor_scalar_mul(mvec[:, :], mvec[:, :], 1.0 / D),
            reads=[(tag, "m", tt) for tt in range(NTT)], writes=[(tag, "mv")])
    mk.emit("V", lambda e: e.tensor_scalar_mul(vvec[:, :], vvec[:, :], 1.0 / D),
            reads=[(tag, "v", tt) for tt in range(NTT)], writes=[(tag, "vv")])
    mk.emit("V", lambda e: e.scalar_tensor_tensor(rvec[:, :], mvec[:, :], -1.0, mvec[:, :],
                                                  ALU.mult, ALU.mult),
            reads=[(tag, "mv")], writes=[(tag, "rv")])
    mk.emit("V", lambda e: e.tensor_add(vvec[:, :], vvec[:, :], rvec[:, :]),
            reads=[(tag, "vv"), (tag, "rv")], writes=[(tag, "vv")])
    mk.emit("S", lambda e: e.activation(vvec[:, :], vvec[:, :], AF.Ln, bias=consts["epscol"][:, 0:1]),
            reads=[(tag, "vv"), ("c", "epsc")], writes=[(tag, "vv")])
    mk.emit("S", lambda e: e.activation(rvec[:, :], vvec[:, :], AF.Exp, scale=-0.5),
            reads=[(tag, "vv")], writes=[(tag, "rv")])
    for tt in range(NTT):
        mk.emit("Yq", (lambda tt=tt: lambda e: e.dma_start(
            out=rjb[:, 0:1024].rearrange("p (j f) -> p j f", f=128),
            in_=xT_in[:, tt * 128:(tt + 1) * 128].rearrange("(j p) f -> p j f", p=128)))(),
            reads=[(tag, "rjb")], writes=[(tag, "rjbw")])

        def rtt(e, tt=tt):
            last = None
            for j in range(NJ):
                last = e.transpose(mk.pool[:, 2048 + j * 128: 2048 + j * 128 + 128],
                                   rjb[:, j * 128:(j + 1) * 128], eyef[:])
            return last
        mk.emit("T", rtt, reads=[(tag, "rjbw"), ("c", "eyef")],
                writes=[("pb", 4), ("pb", 5), ("pb", 6), ("pb", 7)])
        at = ftm[:, tt * 1024:(tt + 1) * 1024]
        mk.emit("V", (lambda at=at, tt=tt: lambda e: e.tensor_scalar(
            lnf[:, 0:1024], at, mvec[:, tt:tt + 1], rvec[:, tt:tt + 1],
            ALU.subtract, ALU.mult))(),
            reads=[("ftm", tt), (tag, "mv"), (tag, "rv")],
            writes=[(tag, "lnf")])
        mk.emit("V", (lambda tt=tt: lambda e: e.tensor_add(
            outb[:, 0:1024], lnf[:, 0:1024], mk.pool[:, 2048:3072]))(),
            reads=[(tag, "lnf"), ("pb", 4), ("pb", 5), ("pb", 6), ("pb", 7), (tag, "outb")],
            writes=[(tag, "outbw"), (tag, "rjb")])
        mk.emit("Yq", (lambda tt=tt: lambda e: e.dma_start(
            out=out_tm[tt * 128:(tt + 1) * 128, :], in_=outb[:, 0:1024]))(),
            reads=[(tag, "outbw")], writes=[(tag, "outb"), (tag, "st", tt)])
    mk.wait_only("Y", [(tag, "st", tt) for tt in range(NTT)])
    return mk



def build_M1():
    mk = MK()
    io_a = {
        "xT": mk.dram("xT", [D, TOK], F32, kind="ExternalInput").ap(),
        "qT": mk.dram("qT", [D, TOK], BF16, kind="ExternalInput").ap(),
        "kTf": mk.dram("kTf", [D, S], BF16, kind="ExternalInput").ap(),
        "vf": mk.dram("vf", [S, D], BF16, kind="ExternalInput").ap(),
        "x1_out": mk.dram("x1s", [D, TOK], F32).ap(),
    }
    x2s = mk.dram("x2s", [D, TOK], F32).ap()
    io_b = {
        "xT": io_a["x1_out"],
        "gwT": mk.dram("gwT", [D, NE], F32, kind="ExternalInput").ap(),
        "gb": mk.dram("gb", [NE, 1], F32, kind="ExternalInput").ap(),
        "w1": mk.dram("w1", [NE, D, 2048], F32, kind="ExternalInput").ap(),
        "b1r": mk.dram("b1r", [128, NE * NHC], F32, kind="ExternalInput").ap(),
        "w2": mk.dram("w2", [NE, 2048, D], F32, kind="ExternalInput").ap(),
        "b2": mk.dram("b2", [NE, D], F32, kind="ExternalInput").ap(),
        "x2_out": x2s,
        "x2_out2": mk.dram("x2T", [D, TOK], F32, kind="ExternalOutput").ap(),
    }
    io_c = {
        "xT": x2s,
        "wqkv": mk.dram("wqkv", [D, 3 * D], F32, kind="ExternalInput").ap(),
        "cos2": mk.dram("cos2", [128, TOK], F32, kind="ExternalInput").ap(),
        "sin2": mk.dram("sin2", [128, TOK], F32, kind="ExternalInput").ap(),
        "prot": mk.dram("prot", [128, 128], F32, kind="ExternalInput").ap(),
        "qT": mk.dram("qT1", [D, TOK], BF16, kind="ExternalOutput").ap(),
        "kT": mk.dram("kT1", [D, TOK], BF16, kind="ExternalOutput").ap(),
        "v": mk.dram("v1", [TOK, D], BF16, kind="ExternalOutput").ap(),
    }
    eyef_in = mk.dram("eyef", [128, 128], F32, kind="ExternalInput").ap()
    mk.psum_pool()
    consts = emit_consts(mk)
    eyef = mk.sbuf("eyef_s", [128, 128], F32)
    mk.emit("Yq", lambda e: e.dma_start(out=eyef[:], in_=eyef_in[:, :]), writes=[("c", "eyef")])
    mk.init_arena()
    emit_phase_A2(mk, consts, io_a)
    mk.arena_reset()
    emit_phase_B(mk, consts, eyef, io_b)
    mk.arena_reset()
    emit_phase_C1(mk, consts, eyef, io_c)
    return mk




def build_M2():
    mk = MK()
    x3s = mk.dram("x3s", [D, TOK], F32).ap()
    io_c = {
        "xT": mk.dram("xT", [D, TOK], F32, kind="ExternalInput").ap(),
        "qT": mk.dram("qT", [D, TOK], BF16, kind="ExternalInput").ap(),
        "kTf": mk.dram("kTf", [D, S], BF16, kind="ExternalInput").ap(),
        "vf": mk.dram("vf", [S, D], BF16, kind="ExternalInput").ap(),
        "wout": mk.dram("wout", [D, D], F32, kind="ExternalInput").ap(),
        "x3_out": x3s,
        "x3_out2": mk.dram("x3T", [D, TOK], F32, kind="ExternalOutput").ap(),
    }
    io_d = {
        "xT": x3s,
        "nwT3": mk.dram("nwT3", [3, D, 15], F32, kind="ExternalInput").ap(),
        "nb3": mk.dram("nb3", [15, 3], F32, kind="ExternalInput").ap(),
        "w1f": mk.dram("w1f", [3, NLP, D, 128], F32, kind="ExternalInput").ap(),
        "b1f": mk.dram("b1f", [3, 128, NLP], F32, kind="ExternalInput").ap(),
        "w2f": mk.dram("w2f", [3, NLP, 128, D], F32, kind="ExternalInput").ap(),
        "b2f": mk.dram("b2f", [3, 16, D], F32, kind="ExternalInput").ap(),
        "selm": mk.dram("selm", [16, 16 * 64], F32, kind="ExternalInput").ap(),
        "qT": mk.dram("qT1", [D, TOK], BF16, kind="ExternalOutput").ap(),
        "kT": mk.dram("kT1", [D, TOK], BF16, kind="ExternalOutput").ap(),
        "v": mk.dram("v1", [TOK, D], BF16, kind="ExternalOutput").ap(),
    }
    eyef_in = mk.dram("eyef", [128, 128], F32, kind="ExternalInput").ap()
    mk.psum_pool()
    consts = emit_consts(mk)
    eyef = mk.sbuf("eyef_s", [128, 128], F32)
    mk.emit("Yq", lambda e: e.dma_start(out=eyef[:], in_=eyef_in[:, :]), writes=[("c", "eyef")])
    mk.init_arena()
    emit_phase_C2(mk, consts, eyef, io_c)
    mk.arena_reset()
    emit_phase_D1(mk, consts, eyef, io_d)
    return mk



_NC_CACHE = {}


def _get_nc(name, builder):
    if name not in _NC_CACHE:
        _NC_CACHE[name] = builder().finalize()
    return _NC_CACHE[name]


def phaseA(inp):
    x = np.asarray(inp["L0_attn_x"], np.float32)
    eye = np.eye(128, dtype=np.float32)
    vT3 = np.stack([np.asarray(inp["L0_attn_V" + p], np.float32).transpose(0, 2, 1) for p in "qkv"])
    uTb3 = np.stack([
        np.concatenate([np.asarray(inp["L0_attn_U" + p], np.float32).transpose(0, 2, 1),
                        np.asarray(inp["L0_attn_b" + p], np.float32)[:, None, :]], axis=1)
        for p in "qkv"])
    nc1 = _get_nc("A1", build_A1)
    in_maps = []
    for c in range(NCORES):
        b, q0 = c // 4, (c % 4) * TOK
        in_maps.append({"x": np.ascontiguousarray(x[b, q0:q0 + TOK]),
                        "vT3": vT3, "uTb3": uTb3, "eyef": eye})
    r1 = run_bass_kernel_spmd(nc1, in_maps, core_ids=list(range(NCORES))).results
    kTf = [np.ascontiguousarray(np.concatenate([r1[4 * b + i]["kT"] for i in range(4)], axis=1)) for b in range(B)]
    vf = [np.ascontiguousarray(np.concatenate([r1[4 * b + i]["v"] for i in range(4)], axis=0)) for b in range(B)]
    nc2 = _get_nc("A2", build_A2)
    in_maps2 = []
    for c in range(NCORES):
        b = c // 4
        in_maps2.append({"xT": r1[c]["xT"], "qT": r1[c]["qT"], "kTf": kTf[b], "vf": vf[b]})
    r2 = run_bass_kernel_spmd(nc2, in_maps2, core_ids=list(range(NCORES))).results
    return [r2[c]["x1T"] for c in range(NCORES)]


def phaseB(inp, x1):
    eye = np.eye(128, dtype=np.float32)
    gwT = np.ascontiguousarray(np.asarray(inp["L0_ffn_gate_w"], np.float32).T)
    gb = np.asarray(inp["L0_ffn_gate_b"], np.float32).reshape(NE, 1)
    w1 = np.ascontiguousarray(np.asarray(inp["L0_ffn_w1"], np.float32))
    b1r = np.ascontiguousarray(np.asarray(inp["L0_ffn_b1"], np.float32)
                               .reshape(NE, NHC, 128).transpose(2, 0, 1).reshape(128, NE * NHC))
    w2 = np.ascontiguousarray(np.asarray(inp["L0_ffn_w2"], np.float32))
    b2 = np.ascontiguousarray(np.asarray(inp["L0_ffn_b2"], np.float32))
    ncb = _get_nc("B", build_B_full)
    in_maps = [{"xT": x1[c], "gwT": gwT, "gb": gb, "w1": w1, "b1r": b1r,
                "w2": w2, "b2": b2, "eyef": eye} for c in range(NCORES)]
    r = run_bass_kernel_spmd(ncb, in_maps, core_ids=list(range(NCORES))).results
    return [r[c]["x2T"] for c in range(NCORES)]


def _rope_tables():
    inv = 1.0 / (10000.0 ** (np.arange(0, 64, 2, dtype=np.float32) / 64.0))
    t = np.arange(S, dtype=np.float32)
    fr = np.outer(t, inv)
    emb = np.concatenate([fr, fr], axis=-1)
    return np.cos(emb).T.astype(np.float32), np.sin(emb).T.astype(np.float32)


def _prot():
    P = np.zeros((128, 128), np.float32)
    for base in (0, 64):
        for i in range(32):
            P[base + 32 + i, base + i] = -1.0
            P[base + i, base + 32 + i] = 1.0
    return P


def phaseC(inp, x2):
    eye = np.eye(128, dtype=np.float32)
    wqkv = np.ascontiguousarray(np.asarray(inp["L1_attn_Wqkv"], np.float32))
    wout = np.ascontiguousarray(np.asarray(inp["L1_attn_Wout"], np.float32))
    cosT, sinT = _rope_tables()
    prot = _prot()
    nc1 = _get_nc("C1", build_C1)
    in_maps = []
    for c in range(NCORES):
        q0 = (c % 4) * TOK
        cos2 = np.ascontiguousarray(np.concatenate([cosT[:, q0:q0 + TOK]] * 2, axis=0))
        sin2 = np.ascontiguousarray(np.concatenate([sinT[:, q0:q0 + TOK]] * 2, axis=0))
        in_maps.append({"xT": x2[c], "wqkv": wqkv, "cos2": cos2, "sin2": sin2,
                        "prot": prot, "eyef": eye})
    r1 = run_bass_kernel_spmd(nc1, in_maps, core_ids=list(range(NCORES))).results
    kTf = [np.ascontiguousarray(np.concatenate([r1[4 * b + i]["kT"] for i in range(4)], axis=1)) for b in range(B)]
    vf = [np.ascontiguousarray(np.concatenate([r1[4 * b + i]["v"] for i in range(4)], axis=0)) for b in range(B)]
    nc2 = _get_nc("C2", build_C2)
    in_maps2 = []
    for c in range(NCORES):
        b = c // 4
        in_maps2.append({"xT": x2[c], "qT": r1[c]["qT"], "kTf": kTf[b], "vf": vf[b],
                         "wout": wout, "eyef": eye})
    r2 = run_bass_kernel_spmd(nc2, in_maps2, core_ids=list(range(NCORES))).results
    return [r2[c]["x3T"] for c in range(NCORES)]


def phaseD(inp, x3):
    eye = np.eye(128, dtype=np.float32)
    f32 = np.float32
    nwT3 = np.stack([np.ascontiguousarray(np.asarray(inp["L1_ffn_%s_node_w" % p], f32).T) for p in "qkv"])
    nb3 = np.stack([np.asarray(inp["L1_ffn_%s_node_b" % p], f32) for p in "qkv"], axis=1)
    w1f = np.stack([np.asarray(inp["L1_ffn_%s_w1" % p], f32).reshape(NLP, 2, D, 64)
                    .transpose(0, 2, 1, 3).reshape(NLP, D, 128) for p in "qkv"])
    b1f = np.stack([np.asarray(inp["L1_ffn_%s_b1" % p], f32).reshape(NLP, 128).T for p in "qkv"])
    w2f = np.stack([np.asarray(inp["L1_ffn_%s_w2" % p], f32).reshape(NLP, 128, D) for p in "qkv"])
    b2f = np.stack([np.asarray(inp["L1_ffn_%s_b2" % p], f32) for p in "qkv"])
    selm = np.zeros((16, 16 * 64), f32)
    for l in range(16):
        selm[l, l * 64:(l + 1) * 64] = 1.0
    nc1 = _get_nc("D1", build_D1)
    in_maps = [{"xT": x3[c], "nwT3": nwT3, "nb3": np.ascontiguousarray(nb3),
                "w1f": np.ascontiguousarray(w1f), "b1f": np.ascontiguousarray(b1f),
                "w2f": np.ascontiguousarray(w2f), "b2f": np.ascontiguousarray(b2f),
                "selm": selm, "eyef": eye} for c in range(NCORES)]
    r1 = run_bass_kernel_spmd(nc1, in_maps, core_ids=list(range(NCORES))).results
    kTf = [np.ascontiguousarray(np.concatenate([r1[4 * b + i]["kT"] for i in range(4)], axis=1)) for b in range(B)]
    vf = [np.ascontiguousarray(np.concatenate([r1[4 * b + i]["v"] for i in range(4)], axis=0)) for b in range(B)]
    nc2 = _get_nc("D2", build_D2)
    in_maps2 = []
    for c in range(NCORES):
        b = c // 4
        in_maps2.append({"xT": x3[c], "qT": r1[c]["qT"], "kTf": kTf[b], "vf": vf[b], "eyef": eye})
    r2 = run_bass_kernel_spmd(nc2, in_maps2, core_ids=list(range(NCORES))).results
    return [r2[c]["xout"] for c in range(NCORES)]


_ALL_INPUT_NAMES = (
    "L0_attn_x", "L0_attn_Uq", "L0_attn_Vq", "L0_attn_bq", "L0_attn_Uk", "L0_attn_Vk",
    "L0_attn_bk", "L0_attn_Uv", "L0_attn_Vv", "L0_attn_bv", "L0_ffn_gate_w",
    "L0_ffn_gate_b", "L0_ffn_w1", "L0_ffn_b1", "L0_ffn_w2", "L0_ffn_b2",
    "L1_attn_Wqkv", "L1_attn_Wout",
    "L1_ffn_q_node_w", "L1_ffn_q_node_b", "L1_ffn_q_w1", "L1_ffn_q_b1", "L1_ffn_q_w2",
    "L1_ffn_q_b2", "L1_ffn_k_node_w", "L1_ffn_k_node_b", "L1_ffn_k_w1", "L1_ffn_k_b1",
    "L1_ffn_k_w2", "L1_ffn_k_b2", "L1_ffn_v_node_w", "L1_ffn_v_node_b", "L1_ffn_v_w1",
    "L1_ffn_v_b1", "L1_ffn_v_w2", "L1_ffn_v_b2",
)


def run_M1(inp, r1):
    kTf = [np.ascontiguousarray(np.concatenate([r1[4 * b + i]["kT"] for i in range(4)], axis=1)) for b in range(B)]
    vf = [np.ascontiguousarray(np.concatenate([r1[4 * b + i]["v"] for i in range(4)], axis=0)) for b in range(B)]
    eye = np.eye(128, dtype=np.float32)
    gwT = np.ascontiguousarray(np.asarray(inp["L0_ffn_gate_w"], np.float32).T)
    gb = np.asarray(inp["L0_ffn_gate_b"], np.float32).reshape(NE, 1)
    w1 = np.ascontiguousarray(np.asarray(inp["L0_ffn_w1"], np.float32))
    b1r = np.ascontiguousarray(np.asarray(inp["L0_ffn_b1"], np.float32)
                               .reshape(NE, NHC, 128).transpose(2, 0, 1).reshape(128, NE * NHC))
    w2 = np.ascontiguousarray(np.asarray(inp["L0_ffn_w2"], np.float32))
    b2 = np.ascontiguousarray(np.asarray(inp["L0_ffn_b2"], np.float32))
    wqkv = np.ascontiguousarray(np.asarray(inp["L1_attn_Wqkv"], np.float32))
    cosT, sinT = _rope_tables()
    prot = _prot()
    ncm = _get_nc("M1", build_M1)
    in_maps = []
    for c in range(NCORES):
        b, q0 = c // 4, (c % 4) * TOK
        cos2 = np.ascontiguousarray(np.concatenate([cosT[:, q0:q0 + TOK]] * 2, axis=0))
        sin2 = np.ascontiguousarray(np.concatenate([sinT[:, q0:q0 + TOK]] * 2, axis=0))
        in_maps.append({"xT": r1[c]["xT"], "qT": r1[c]["qT"], "kTf": kTf[b], "vf": vf[b],
                        "gwT": gwT, "gb": gb, "w1": w1, "b1r": b1r, "w2": w2, "b2": b2,
                        "wqkv": wqkv, "cos2": cos2, "sin2": sin2, "prot": prot, "eyef": eye})
    return run_bass_kernel_spmd(ncm, in_maps, core_ids=list(range(NCORES))).results


def kernel_merged(inputs):
    inp = inputs
    x = np.asarray(inp["L0_attn_x"], np.float32)
    eye = np.eye(128, dtype=np.float32)
    vT3 = np.stack([np.asarray(inp["L0_attn_V" + p], np.float32).transpose(0, 2, 1) for p in "qkv"])
    uTb3 = np.stack([
        np.concatenate([np.asarray(inp["L0_attn_U" + p], np.float32).transpose(0, 2, 1),
                        np.asarray(inp["L0_attn_b" + p], np.float32)[:, None, :]], axis=1)
        for p in "qkv"])
    nc1 = _get_nc("A1", build_A1)
    in_maps = []
    for c in range(NCORES):
        b, q0 = c // 4, (c % 4) * TOK
        in_maps.append({"x": np.ascontiguousarray(x[b, q0:q0 + TOK]),
                        "vT3": vT3, "uTb3": uTb3, "eyef": eye})
    r1 = run_bass_kernel_spmd(nc1, in_maps, core_ids=list(range(NCORES))).results
    rm = run_M1(inp, r1)
    x2 = [rm[c]["x2T"] for c in range(NCORES)]
    r1b = [{"qT": rm[c]["qT1"], "kT": rm[c]["kT1"], "v": rm[c]["v1"]} for c in range(NCORES)]
    rm2 = run_M2(inp, x2, r1b)
    x3 = [rm2[c]["x3T"] for c in range(NCORES)]
    r2b = [{"qT": rm2[c]["qT1"], "kT": rm2[c]["kT1"], "v": rm2[c]["v1"]} for c in range(NCORES)]
    x4 = run_D2(inp, x3, r2b)
    out = np.zeros((B, S, D), np.float32)
    for c in range(NCORES):
        b, q0 = c // 4, (c % 4) * TOK
        out[b, q0:q0 + TOK] = x4[c]
    return out


def run_M2(inp, x2, r1):
    eye = np.eye(128, dtype=np.float32)
    f32 = np.float32
    wout = np.ascontiguousarray(np.asarray(inp["L1_attn_Wout"], f32))
    kTf = [np.ascontiguousarray(np.concatenate([r1[4 * b + i]["kT"] for i in range(4)], axis=1)) for b in range(B)]
    vf = [np.ascontiguousarray(np.concatenate([r1[4 * b + i]["v"] for i in range(4)], axis=0)) for b in range(B)]
    nwT3 = np.stack([np.ascontiguousarray(np.asarray(inp["L1_ffn_%s_node_w" % p], f32).T) for p in "qkv"])
    nb3 = np.ascontiguousarray(np.stack([np.asarray(inp["L1_ffn_%s_node_b" % p], f32) for p in "qkv"], axis=1))
    w1f = np.ascontiguousarray(np.stack([np.asarray(inp["L1_ffn_%s_w1" % p], f32).reshape(NLP, 2, D, 64)
                                         .transpose(0, 2, 1, 3).reshape(NLP, D, 128) for p in "qkv"]))
    b1f = np.ascontiguousarray(np.stack([np.asarray(inp["L1_ffn_%s_b1" % p], f32).reshape(NLP, 128).T for p in "qkv"]))
    w2f = np.ascontiguousarray(np.stack([np.asarray(inp["L1_ffn_%s_w2" % p], f32).reshape(NLP, 128, D) for p in "qkv"]))
    b2f = np.ascontiguousarray(np.stack([np.asarray(inp["L1_ffn_%s_b2" % p], f32) for p in "qkv"]))
    selm = np.zeros((16, 16 * 64), f32)
    for l in range(16):
        selm[l, l * 64:(l + 1) * 64] = 1.0
    ncm = _get_nc("M2", build_M2)
    in_maps = []
    for c in range(NCORES):
        b = c // 4
        in_maps.append({"xT": x2[c], "qT": r1[c]["qT"], "kTf": kTf[b], "vf": vf[b], "wout": wout,
                        "nwT3": nwT3, "nb3": nb3, "w1f": w1f, "b1f": b1f, "w2f": w2f, "b2f": b2f,
                        "selm": selm, "eyef": eye})
    return run_bass_kernel_spmd(ncm, in_maps, core_ids=list(range(NCORES))).results


def run_D2(inp, x3, r1):
    eye = np.eye(128, dtype=np.float32)
    kTf = [np.ascontiguousarray(np.concatenate([r1[4 * b + i]["kT"] for i in range(4)], axis=1)) for b in range(B)]
    vf = [np.ascontiguousarray(np.concatenate([r1[4 * b + i]["v"] for i in range(4)], axis=0)) for b in range(B)]
    nc2 = _get_nc("D2", build_D2)
    in_maps2 = []
    for c in range(NCORES):
        b = c // 4
        in_maps2.append({"xT": x3[c], "qT": r1[c]["qT"], "kTf": kTf[b], "vf": vf[b], "eyef": eye})
    r2 = run_bass_kernel_spmd(nc2, in_maps2, core_ids=list(range(NCORES))).results
    return [r2[c]["xout"] for c in range(NCORES)]


def phaseC2_only(inp, x2, r1):
    eye = np.eye(128, dtype=np.float32)
    wout = np.ascontiguousarray(np.asarray(inp["L1_attn_Wout"], np.float32))
    kTf = [np.ascontiguousarray(np.concatenate([r1[4 * b + i]["kT"] for i in range(4)], axis=1)) for b in range(B)]
    vf = [np.ascontiguousarray(np.concatenate([r1[4 * b + i]["v"] for i in range(4)], axis=0)) for b in range(B)]
    nc2 = _get_nc("C2", build_C2)
    in_maps2 = []
    for c in range(NCORES):
        b = c // 4
        in_maps2.append({"xT": x2[c], "qT": r1[c]["qT"], "kTf": kTf[b], "vf": vf[b],
                         "wout": wout, "eyef": eye})
    r2 = run_bass_kernel_spmd(nc2, in_maps2, core_ids=list(range(NCORES))).results
    return [r2[c]["x3T"] for c in range(NCORES)]


def kernel(**inputs):
    assert all(n in inputs for n in _ALL_INPUT_NAMES)
    return kernel_merged(inputs)


def kernel_old(**inputs):
    assert all(n in inputs for n in _ALL_INPUT_NAMES)
    x1 = phaseA(inputs)
    x2 = phaseB(inputs, x1)
    x3 = phaseC(inputs, x2)
    x4 = phaseD(inputs, x3)
    out = np.zeros((B, S, D), np.float32)
    for c in range(NCORES):
        b, q0 = c // 4, (c % 4) * TOK
        out[b, q0:q0 + TOK] = x4[c]
    return out


# version 15
# speedup vs baseline: 1.0902x; 1.0902x over previous
import numpy as np
import ml_dtypes
from contextlib import ExitStack
import concourse.bass as bass
import concourse.mybir as mybir
from concourse.bass_utils import run_bass_kernel_spmd

F32 = mybir.dt.float32
BF16 = mybir.dt.bfloat16
AF = mybir.ActivationFunctionType
ALU = mybir.AluOpType
AX = mybir.AxisListType.X
BF = ml_dtypes.bfloat16

NCORES = 8
B = 2
S = 8192
D = 1024
TOK = 2048
NJ = 8
NTT = TOK // 128
NC4 = TOK // 512
NKT = S // 128
LN_EPS = 1e-5

CH2ENG = {"T": "tensor", "V": "vector", "S": "scalar", "G": "gpsimd",
          "Y": "sync", "Gq": "gpsimd", "Yq": "sync", "Sq": "scalar"}
DMACH = ("Gq", "Yq", "Sq")


class MK:

    def __init__(self):
        self.nc = bass.Bass("TRN2", target_bir_lowering=False)
        self.es = ExitStack()
        self.ops = {e: [] for e in ("tensor", "vector", "scalar", "gpsimd", "sync")}
        self.cnt = {c: 0 for c in CH2ENG}
        self.sems = {}
        self.regions = {}
        self.marks = {}

    def dram(self, name, shape, dtype, kind="Internal"):
        return self.nc.dram_tensor(name, shape, dtype, kind=kind)

    def sbuf(self, name, shape, dtype):
        return self.es.enter_context(self.nc.sbuf_tensor(name, shape, dtype))

    def psum_pool(self):
        self.pool = self.es.enter_context(self.nc.psum_tensor("pool", [128, 4096], F32))
        return self.pool

    def init_arena(self, cols_bf16=104448):
        self.arena = self.sbuf("ARENA", [128, cols_bf16], BF16)
        self.acols = cols_bf16
        self.aoff = 0

    def asbuf(self, name, shape, dtype):
        cols = shape[1] * (2 if dtype == F32 else 1)
        cols = (cols + 31) & ~31
        assert self.aoff + cols <= self.acols, (name, self.aoff, cols)
        ap = self.arena[:, self.aoff:self.aoff + cols]
        self.aoff += cols
        if dtype == F32:
            ap = ap.bitcast(F32)
        return ap[0:shape[0], 0:shape[1]]

    def arena_reset(self):
        self.barrier()
        self.aoff = 0

    def pb(self, bank, width=512, rows=128, off=0):
        return self.pool[0:rows, bank * 512 + off: bank * 512 + off + width]

    def _sem(self, ch):
        if ch not in self.sems:
            self.sems[ch] = self.es.enter_context(self.nc.semaphore("sem_" + ch))
        return self.sems[ch]

    def emit(self, ch, fn, reads=(), writes=(), ndma=1):
        eng = CH2ENG[ch]
        is_dma = ch in DMACH
        waits = {}

        def dep(d):
            if d is None:
                return
            dch, dval = d
            if dch == "T" and ch == "T":
                return
            if self.marks.get((eng, dch), 0) >= dval:
                return
            waits[dch] = max(waits.get(dch, 0), dval)

        for k in reads:
            r = self.regions.get(k)
            if r is not None:
                dep(r[0])
        for k in writes:
            r = self.regions.get(k)
            if r is not None:
                dep(r[0])
                for rd in r[1]:
                    dep(rd)
        for dch, dval in waits.items():
            self.marks[(eng, dch)] = dval
        inc_total = (16 * ndma) if is_dma else (0 if fn is None else 1)
        self.cnt[ch] += inc_total
        val = self.cnt[ch]
        for k in writes:
            self.regions[k] = [(ch, val), []]
        for k in reads:
            self.regions.setdefault(k, [None, []])[1].append((ch, val))
        self._sem(ch)
        self.ops[eng].append((ch, fn, sorted(waits.items()), is_dma))
        return (ch, val)

    def wait_only(self, ch, reads):
        self.emit(ch, None, reads=reads, ndma=0)

    def barrier(self):
        snap = [(ch, v) for ch, v in self.cnt.items() if v > 0]
        for eng in self.ops:
            if not self.ops[eng]:
                continue
            chs = [(ch, v) for (ch, v) in snap
                   if CH2ENG[ch] != eng and self.marks.get((eng, ch), 0) < v]
            for ch, v in chs:
                self.marks[(eng, ch)] = v
                self._sem(ch)
            mych = [c for c, e in CH2ENG.items() if e == eng and c not in DMACH]
            mych = mych[0] if mych else "Y"
            self.ops[eng].append((mych, None, sorted(chs), False))

    def finalize(self):
        nc = self.nc
        with nc.Block() as blk:
            for eng in self.ops:
                if not self.ops[eng]:
                    continue

                def mk_body(eng):
                    oplist = self.ops[eng]

                    def body(e):
                        for (ch, fn, waits, is_dma) in oplist:
                            for dch, dval in waits:
                                e.wait_ge(self.sems[dch], dval)
                            if fn is None:
                                continue
                            r = fn(e)
                            if is_dma:
                                insts = r if isinstance(r, (list, tuple)) else [r]
                                for i_ in insts:
                                    i_.then_inc(self.sems[ch], 16)
                            else:
                                r.then_inc(self.sems[ch], 1)
                    return body
                getattr(blk, eng)(mk_body(eng))
        return self.nc


PB = [("pb", i) for i in range(8)]


def emit_consts(mk):
    c = {}
    c["onesb"] = mk.sbuf("c_onesb", [128, 1], BF16)
    c["ones1f"] = mk.sbuf("c_ones1f", [1, 128], F32)
    c["epsrow"] = mk.sbuf("c_epsrow", [1, 1], F32)
    c["epscol"] = mk.sbuf("c_epscol", [128, 1], F32)
    c["ones1b"] = mk.sbuf("c_ones1b", [1, 128], BF16)
    c["ones8f"] = mk.sbuf("c_ones8f", [8, 1], F32)
    mk.emit("G", lambda e: e.memset(c["ones1b"][:], 1.0), writes=[("c", "ones1b")])
    mk.emit("G", lambda e: e.memset(c["ones8f"][:], 1.0), writes=[("c", "ones8f")])
    mk.emit("G", lambda e: e.memset(c["onesb"][:], 1.0), writes=[("c", "onesb")])
    mk.emit("G", lambda e: e.memset(c["ones1f"][:], 1.0), writes=[("c", "ones1f")])
    mk.emit("G", lambda e: e.memset(c["epsrow"][:], LN_EPS), writes=[("c", "eps")])
    mk.emit("G", lambda e: e.memset(c["epscol"][:], LN_EPS), writes=[("c", "epsc")])
    return c


def alloc_ln_tmp(mk, t_tok=TOK, with_xb=True):
    return {
        **({"xb": mk.asbuf("ln_xb", [128, 2 * t_tok], BF16)} if with_xb else {}),
        "xq": mk.asbuf("ln_xq", [128, 2 * t_tok], BF16),
        "arow": mk.asbuf("ln_arow", [1, t_tok], F32),
        "brow": mk.asbuf("ln_brow", [1, t_tok], F32),
        "trow": mk.asbuf("ln_trow", [1, t_tok], F32),
        "mrow": mk.asbuf("ln_mrow", [1, t_tok], F32),
        "absb": mk.asbuf("ln_absb", [128, 1024], F32),
        "t1a": mk.asbuf("ln_t1a", [128, 512], F32),
        "t1b": mk.asbuf("ln_t1b", [128, 512], F32),
    }


def emit_ln_fm(mk, consts, tmp, tag, src, src_key, dst_fn, res=None, res_key=None, src_bf16=False):
    for j in range(NJ):
        par = j % 2
        if not src_bf16:
            mk.emit("S", (lambda j=j, par=par: lambda e: e.activation(
                tmp["xb"][:, par * TOK:(par + 1) * TOK],
                src[:, j * TOK:(j + 1) * TOK], AF.Copy))(),
                reads=[src_key(j, c) for c in range(NC4)], writes=[(tag, "xb", par)])
        mk.emit("S", (lambda j=j, par=par: lambda e: e.activation(
            tmp["xq"][:, par * TOK:(par + 1) * TOK],
            src[:, j * TOK:(j + 1) * TOK], AF.Square))(),
            reads=[src_key(j, c) for c in range(NC4)], writes=[(tag, "xq", par)])

        def mmstats(e, j=j, par=par):
            last = None
            for c in range(NC4):
                xbap = (src[:, j * TOK + c * 512: j * TOK + c * 512 + 512] if src_bf16
                        else tmp["xb"][:, par * TOK + c * 512: par * TOK + c * 512 + 512])
                last = e.matmul(mk.pool[0:1, c * 512:c * 512 + 512], consts["onesb"][:],
                                xbap, start=(j == 0), stop=(j == NJ - 1))
                last = e.matmul(mk.pool[0:1, 2048 + c * 512:2048 + c * 512 + 512], consts["onesb"][:],
                                tmp["xq"][:, par * TOK + c * 512: par * TOK + c * 512 + 512],
                                start=(j == 0), stop=(j == NJ - 1))
            return last
        rd = [(tag, "xq", par), ("c", "onesb")]
        rd += [src_key(j, c) for c in range(NC4)] if src_bf16 else [(tag, "xb", par)]
        mk.emit("T", mmstats, reads=rd, writes=PB)
    mk.emit("V", lambda e: e.tensor_scalar_mul(tmp["mrow"][0:1, :], mk.pool[0:1, 0:TOK], 1.0 / D),
            reads=[("pb", i) for i in range(4)], writes=[(tag, "mrow")])
    mk.emit("V", lambda e: e.tensor_scalar_mul(tmp["trow"][0:1, :], mk.pool[0:1, 2048:2048 + TOK], 1.0 / D),
            reads=[("pb", i) for i in range(4, 8)], writes=[(tag, "trow")])
    mk.emit("V", lambda e: e.scalar_tensor_tensor(tmp["arow"][0:1, :], tmp["mrow"][0:1, :], -1.0,
                                                  tmp["mrow"][0:1, :], ALU.mult, ALU.mult),
            reads=[(tag, "mrow")], writes=[(tag, "arow")])
    mk.emit("V", lambda e: e.tensor_add(tmp["trow"][0:1, :], tmp["trow"][0:1, :], tmp["arow"][0:1, :]),
            reads=[(tag, "trow"), (tag, "arow")], writes=[(tag, "trow")])
    mk.emit("S", lambda e: e.activation(tmp["trow"][0:1, :], tmp["trow"][0:1, :], AF.Ln, bias=consts["epsrow"][0:1, 0:1]),
            reads=[(tag, "trow"), ("c", "eps")], writes=[(tag, "trow")])
    mk.emit("S", lambda e: e.activation(tmp["arow"][0:1, :], tmp["trow"][0:1, :], AF.Exp, scale=-0.5),
            reads=[(tag, "trow")], writes=[(tag, "arow")])
    mk.emit("V", lambda e: e.scalar_tensor_tensor(tmp["brow"][0:1, :], tmp["mrow"][0:1, :], -1.0,
                                                  tmp["arow"][0:1, :], ALU.mult, ALU.mult),
            reads=[(tag, "mrow"), (tag, "arow")], writes=[(tag, "brow")])
    for c in range(NC4):
        def bcast(e, c=c):
            e.matmul(mk.pb(0), consts["ones1f"][:], tmp["arow"][0:1, c * 512:c * 512 + 512],
                     start=True, stop=True)
            return e.matmul(mk.pb(1), consts["ones1f"][:], tmp["brow"][0:1, c * 512:c * 512 + 512],
                            start=True, stop=True)
        mk.emit("T", bcast, reads=[(tag, "arow"), (tag, "brow"), ("c", "ones1f")],
                writes=[("pb", 0), ("pb", 1)])
        mk.emit("V", lambda e: e.tensor_copy(tmp["absb"][:, 0:1024], mk.pool[:, 0:1024]),
                reads=[("pb", 0), ("pb", 1)], writes=[(tag, "absb")])
        for j in range(NJ):
            ch = "V" if j % 2 == 0 else "G"
            tk = (tag, "t1a") if ch == "V" else (tag, "t1b")
            tb = tmp["t1a"] if ch == "V" else tmp["t1b"]
            off = j * TOK + c * 512
            mk.emit(ch, (lambda off=off, tb=tb: lambda e: e.tensor_mul(
                tb[:, 0:512], src[:, off:off + 512], tmp["absb"][:, 0:512]))(),
                reads=[src_key(j, c), (tag, "absb")], writes=[tk])
            if res is None:
                dap, dkey = dst_fn(j, c)
                mk.emit(ch, (lambda dap=dap, tb=tb: lambda e: e.tensor_add(
                    dap, tb[:, 0:512], tmp["absb"][:, 512:1024]))(),
                    reads=[tk, (tag, "absb")], writes=[dkey])
            else:
                mk.emit(ch, (lambda tb=tb: lambda e: e.tensor_add(
                    tb[:, 0:512], tb[:, 0:512], tmp["absb"][:, 512:1024]))(),
                    reads=[tk, (tag, "absb")], writes=[tk])
                mk.emit(ch, (lambda off=off, tb=tb: lambda e: e.tensor_add(
                    res[:, off:off + 512], res[:, off:off + 512], tb[:, 0:512]))(),
                    reads=[tk, res_key(j, c)], writes=[res_key(j, c)])


def emit_loadx_fm(mk, x_dram, res, res_key, eyef, tag, xtm=None):
    if xtm is None:
        xtm = mk.sbuf(tag + "_xtm", [128, 2 * 1024], F32)
    for tt in range(NTT):
        par = tt % 2
        mk.emit("Yq", (lambda tt=tt, par=par: lambda e: e.dma_start(
            out=xtm[:, par * 1024:(par + 1) * 1024],
            in_=x_dram[tt * 128:(tt + 1) * 128, :]))(),
            reads=[(tag, "xtm", par)], writes=[(tag, "xtmw", par)])

        def trans(e, par=par):
            last = None
            for j in range(NJ):
                last = e.transpose(mk.pool[:, 2048 + par * 1024 + j * 128: 2048 + par * 1024 + j * 128 + 128],
                                   xtm[:, par * 1024 + j * 128: par * 1024 + j * 128 + 128], eyef[:])
            return last
        mk.emit("T", trans, reads=[(tag, "xtmw", par), ("c", "eyef")],
                writes=[("pb", 4 + 2 * par), ("pb", 5 + 2 * par)])

        def evac(e, tt=tt, par=par):
            last = None
            for j in range(NJ):
                srcap = mk.pool[:, 2048 + par * 1024 + j * 128: 2048 + par * 1024 + j * 128 + 128]
                dstap = res[:, j * TOK + tt * 128: j * TOK + tt * 128 + 128]
                last = e.tensor_copy(dstap, srcap) if tt % 2 == 0 else e.copy(dstap, srcap)
            return last
        ch = "V" if tt % 2 == 0 else "S"
        mk.emit(ch, evac, reads=[("pb", 4 + 2 * par), ("pb", 5 + 2 * par)],
                writes=[res_key(j, tt // 4) for j in range(NJ)] + [(tag, "xtm", par)])



def build_A1():
    mk = MK()
    x_in = mk.dram("x", [TOK, D], F32, kind="ExternalInput").ap()
    vT_in = mk.dram("vT3", [3, NJ, 128, 32], F32, kind="ExternalInput").ap()
    uTb_in = mk.dram("uTb3", [3, NJ, 33, 128], F32, kind="ExternalInput").ap()
    eyef_in = mk.dram("eyef", [128, 128], F32, kind="ExternalInput").ap()
    qT_out = mk.dram("qT", [D, TOK], BF16, kind="ExternalOutput").ap()
    kT_out = mk.dram("kT", [D, TOK], BF16, kind="ExternalOutput").ap()
    v_out = mk.dram("v", [TOK, D], BF16, kind="ExternalOutput").ap()
    xT_out = mk.dram("xT", [D, TOK], F32, kind="ExternalOutput").ap()

    mk.psum_pool()
    consts = emit_consts(mk)
    mk.init_arena(32768)
    eyef = mk.sbuf("eyef_s", [128, 128], F32)
    mk.emit("Yq", lambda e: e.dma_start(out=eyef[:], in_=eyef_in[:, :]), writes=[("c", "eyef")])
    vTs = mk.sbuf("vTs", [128, 3 * NJ * 32], BF16)
    uTbs = mk.sbuf("uTbs", [33, 3 * NJ * 128], BF16)
    mk.emit("Gq", lambda e: e.dma_start(out=vTs[:, :].rearrange("p (x j r) -> p x j r", x=3, j=NJ),
                                        in_=vT_in.rearrange("x j p r -> p x j r")),
            writes=[("c", "vTs")])
    mk.emit("Gq", lambda e: e.dma_start(out=uTbs[:, :].rearrange("p (x j f) -> p x j f", x=3, j=NJ),
                                        in_=uTb_in.rearrange("x j p f -> p x j f")),
            writes=[("c", "uTbs")])

    res = mk.sbuf("res", [128, NJ * TOK], F32)
    xln = mk.sbuf("xln", [128, NJ * TOK], BF16)
    rk = lambda j, c: ("res", j, c)
    emit_loadx_fm(mk, x_in, res, rk, eyef, "lx")

    ln_tmp = alloc_ln_tmp(mk)
    emit_ln_fm(mk, consts, ln_tmp, "ln1", res, rk,
               dst_fn=lambda j, c: (xln[:, j * TOK + c * 512: j * TOK + c * 512 + 512], ("xln", j, c)))

    for j in range(NJ):
        mk.emit("Yq", (lambda j=j: lambda e: e.dma_start(
            out=xT_out[j * 128:(j + 1) * 128, :], in_=res[:, j * TOK:(j + 1) * TOK]))(),
            reads=[rk(j, c) for c in range(NC4)], writes=[("resst", j)])

    taug = mk.sbuf("taug", [33, 2 * 512], BF16)
    mk.emit("G", lambda e: e.memset(taug[:], 1.0), writes=[("taug", 0), ("taug", 1)])
    qbuf = mk.sbuf("qbuf", [128, 2 * TOK], BF16)
    kbuf = mk.sbuf("kbuf", [128, 2 * TOK], BF16)
    vbuf = mk.sbuf("vbuf", [128, 2 * TOK], BF16)
    bufs = {0: qbuf, 1: kbuf, 2: vbuf}
    outs = {0: qT_out, 1: kT_out}
    for h in range(NJ):
        hpar = h % 2
        for p in range(3):
            for c in range(NC4):
                cpar = c % 2
                xkeys = [("xln", h, c)]
                mk.emit("T", (lambda h=h, p=p, c=c, cpar=cpar: lambda e: e.matmul(
                    mk.pool[0:32, cpar * 512:cpar * 512 + 512],
                    vTs[:, (p * NJ + h) * 32:(p * NJ + h) * 32 + 32],
                    xln[:, h * TOK + c * 512: h * TOK + c * 512 + 512],
                    start=True, stop=True))(),
                    reads=xkeys + [("c", "vTs")], writes=[("pb", cpar)])
                mk.emit("V", (lambda cpar=cpar: lambda e: e.tensor_copy(
                    taug[0:32, cpar * 512:cpar * 512 + 512],
                    mk.pool[0:32, cpar * 512:cpar * 512 + 512]))(),
                    reads=[("pb", cpar)], writes=[("taug", cpar)])
                if p < 2:
                    mk.emit("T", (lambda h=h, p=p, cpar=cpar: lambda e: e.matmul(
                        mk.pb(2 + cpar),
                        uTbs[0:33, (p * NJ + h) * 128:(p * NJ + h) * 128 + 128],
                        taug[0:33, cpar * 512:cpar * 512 + 512],
                        start=True, stop=True))(),
                        reads=[("taug", cpar), ("c", "uTbs")], writes=[("pb", 2 + cpar)])
                    mk.emit("S", (lambda p=p, c=c, cpar=cpar, hpar=hpar: lambda e: e.copy(
                        bufs[p][:, hpar * TOK + c * 512: hpar * TOK + c * 512 + 512],
                        mk.pb(2 + cpar)))(),
                        reads=[("pb", 2 + cpar)], writes=[("ob", p, hpar, c)])
                else:
                    def vmm(e, h=h, cpar=cpar):
                        last = None
                        for i in range(4):
                            last = e.matmul(
                                mk.pool[:, 2048 + cpar * 1024 + i * 128: 2048 + cpar * 1024 + i * 128 + 128],
                                taug[0:33, cpar * 512 + i * 128: cpar * 512 + i * 128 + 128],
                                uTbs[0:33, (2 * NJ + h) * 128:(2 * NJ + h) * 128 + 128],
                                start=True, stop=True)
                        return last
                    mk.emit("T", vmm, reads=[("taug", cpar), ("c", "uTbs")],
                            writes=[("pb", 4 + 2 * cpar), ("pb", 5 + 2 * cpar)])

                    def vevac(e, c=c, cpar=cpar, hpar=hpar):
                        last = None
                        for i in range(4):
                            tt = c * 4 + i
                            last = e.copy(
                                bufs[2][:, hpar * TOK + tt * 128: hpar * TOK + tt * 128 + 128],
                                mk.pool[:, 2048 + cpar * 1024 + i * 128: 2048 + cpar * 1024 + i * 128 + 128])
                        return last
                    mk.emit("S", vevac, reads=[("pb", 4 + 2 * cpar), ("pb", 5 + 2 * cpar)],
                            writes=[("ob", 2, hpar, c)])
        for p in range(3):
            if p < 2:
                mk.emit("Yq", (lambda p=p, h=h, hpar=hpar: lambda e: e.dma_start(
                    out=outs[p][h * 128:(h + 1) * 128, :],
                    in_=bufs[p][:, hpar * TOK:(hpar + 1) * TOK]))(),
                    reads=[("ob", p, hpar, c) for c in range(NC4)], writes=[("obst", p, h)])
            else:
                mk.emit("Yq", (lambda h=h, hpar=hpar: lambda e: e.dma_start(
                    out=v_out.rearrange("(tt p) d -> p tt d", p=128)[:, :, h * 128:(h + 1) * 128],
                    in_=bufs[2][:, hpar * TOK:(hpar + 1) * TOK].rearrange("p (tt f) -> p tt f", f=128)))(),
                    reads=[("ob", 2, hpar, c) for c in range(NC4)], writes=[("obst", 2, h)])
    mk.wait_only("Y", [("obst", p, h) for p in range(3) for h in range(NJ)] +
                 [("resst", j) for j in range(NJ)])
    return mk



def emit_attention_l0(mk, consts, qT_src, kTf_in, vf_in, fsb, nheads=NJ, scale=1.0 / np.sqrt(128.0)):
    kh = mk.asbuf("kh", [128, S], BF16)
    vh = mk.asbuf("vh", [128, NKT * 130], BF16)
    qh = mk.asbuf("qh", [128, TOK], BF16)
    pT = mk.asbuf("pT", [128, 3 * 1024], BF16)
    denr = mk.asbuf("denr", [1, 512], F32)
    bsb = mk.asbuf("bsb", [128, 4 * 512], F32)
    mk.emit("G", lambda e: e.memset(vh[:, :], 1.0), writes=[("vhones",)])
    for h in range(nheads):
        mk.emit("Yq", (lambda h=h: lambda e: e.dma_start(
            out=kh[:], in_=kTf_in[h * 128:(h + 1) * 128, :]))(),
            reads=[("kh",)], writes=[("kh",)])
        for hh in range(2):
            mk.emit("Yq", (lambda h=h, hh=hh: lambda e: e.dma_start(
                out=vh[:, :].rearrange("p (kt g) -> p kt g", g=130)[:, :, hh * 65:hh * 65 + 64],
                in_=vf_in.rearrange("(kt p) d -> p kt d", p=128)[:, :, h * 128 + hh * 64: h * 128 + hh * 64 + 64]))(),
                reads=[("vh",), ("vhones",)], writes=[("vh",)])
        mk.emit("Yq", (lambda h=h: lambda e: e.dma_start(
            out=qh[:], in_=qT_src[h * 128:(h + 1) * 128, :]))(),
            reads=[("qh",)], writes=[("qh",)])
        for qc in range(NC4):
            qpar = qc % 2
            dbank = 4 + qpar
            obank = 6 + qpar
            def emit_scexp(ktp, qc=qc):
                spar = (ktp // 2) % 2
                sb0 = spar * 2
                ppar = (ktp // 2) % 3

                def scmm(e, qc=qc, ktp=ktp, sb0=sb0):
                    last = None
                    for i in range(2):
                        kt = ktp + i
                        last = e.matmul(mk.pb(sb0 + i),
                                        kh[:, kt * 128: kt * 128 + 128],
                                        qh[:, qc * 512: qc * 512 + 512],
                                        start=True, stop=True)
                    return last
                mk.emit("T", scmm, reads=[("kh",), ("qh",), ("pb", sb0), ("pb", sb0 + 1)],
                        writes=[("pb", sb0), ("pb", sb0 + 1)])

                def expop(e, sb0=sb0, ppar=ppar):
                    return e.activation(pT[:, ppar * 1024:(ppar + 1) * 1024],
                                        mk.pool[:, sb0 * 512:sb0 * 512 + 1024],
                                        AF.Exp, scale=scale)
                mk.emit("S", expop, reads=[("pb", sb0), ("pb", sb0 + 1)], writes=[("pT", ppar)])

            def emit_pv(ktp, qpar=qpar):
                ppar = (ktp // 2) % 3
                ob0 = 4 + qpar
                ob1 = 6 + qpar

                def pv(e, ktp=ktp, ppar=ppar, ob0=ob0, ob1=ob1):
                    last = None
                    for i in range(2):
                        kt = ktp + i
                        pslice = pT[:, ppar * 1024 + i * 512: ppar * 1024 + i * 512 + 512]
                        for hh, ob in ((0, ob0), (1, ob1)):
                            last = e.matmul(mk.pool[0:65, ob * 512: ob * 512 + 512],
                                            vh[:, kt * 130 + hh * 65: kt * 130 + hh * 65 + 65],
                                            pslice, start=(kt == 0), stop=(kt == NKT - 1))
                    return last
                mk.emit("T", pv, reads=[("pT", ppar), ("vh",), ("pb", ob0), ("pb", ob1)],
                        writes=[("pb", ob0), ("pb", ob1)])

            for ktp in range(0, NKT + 2, 2):
                if ktp < NKT:
                    emit_scexp(ktp)
                if ktp >= 2:
                    emit_pv(ktp - 2)
            ob0 = 4 + qpar
            ob1 = 6 + qpar
            mk.emit("V", (lambda ob0=ob0: lambda e: e.reciprocal(
                denr[0:1, 0:512], mk.pool[64:65, ob0 * 512: ob0 * 512 + 512]))(),
                reads=[("pb", ob0)], writes=[("denr",)])
            sb0 = 0 if qpar == 0 else 2

            def dbc(e, sb0=sb0):
                return e.matmul(mk.pool[0:64, sb0 * 512: sb0 * 512 + 512],
                                consts["ones1f"][0:1, 0:64],
                                denr[0:1, 0:512], start=True, stop=True)
            mk.emit("T", dbc, reads=[("denr",), ("c", "ones1f"), ("pb", sb0), ("pb", sb0 + 1)],
                    writes=[("pb", sb0)])
            s0 = qpar * 512
            mk.emit("V", (lambda sb0=sb0, s0=s0: lambda e: e.tensor_copy(
                bsb[0:64, s0:s0 + 512], mk.pool[0:64, sb0 * 512: sb0 * 512 + 512]))(),
                reads=[("pb", sb0)], writes=[("bsb", qpar)])
            mk.emit("V", (lambda h=h, qc=qc, ob0=ob0, s0=s0: lambda e: e.tensor_mul(
                fsb[:, h * TOK + qc * 512: h * TOK + qc * 512 + 512][0:64, :],
                mk.pool[0:64, ob0 * 512: ob0 * 512 + 512],
                bsb[0:64, s0:s0 + 512]))(),
                reads=[("bsb", qpar), ("pb", ob0)], writes=[("f", h, qc)])
            mk.emit("V", (lambda h=h, qc=qc, ob1=ob1, s0=s0: lambda e: e.tensor_mul(
                fsb[:, h * TOK + qc * 512: h * TOK + qc * 512 + 512][64:128, :],
                mk.pool[0:64, ob1 * 512: ob1 * 512 + 512],
                bsb[0:64, s0:s0 + 512]))(),
                reads=[("bsb", qpar), ("pb", ob1), ("f", h, qc)], writes=[("f", h, qc)])


def emit_phase_A2(mk, consts, io):
    xT_in, qT_in, kTf_in, vf_in, x1_out = io["xT"], io["qT"], io["kTf"], io["vf"], io["x1_out"]
    res = mk.asbuf("res", [128, NJ * TOK], F32)
    fsb = mk.asbuf("fsb", [128, NJ * TOK], BF16)
    rk = lambda j, c: ("res", j, c)
    for j in range(NJ):
        mk.emit("Yq", (lambda j=j: lambda e: e.dma_start(
            out=res[:, j * TOK:(j + 1) * TOK], in_=xT_in[j * 128:(j + 1) * 128, :]))(),
            writes=[rk(j, c) for c in range(NC4)])

    emit_attention_l0(mk, consts, qT_in, kTf_in, vf_in, fsb)

    ln_tmp = alloc_ln_tmp(mk, with_xb=False)
    fkey = lambda j, c: ("f", j, c)
    emit_ln_fm(mk, consts, ln_tmp, "ln2", fsb, fkey, dst_fn=None, res=res, res_key=rk, src_bf16=True)
    for j in range(NJ):
        mk.emit("Yq", (lambda j=j: lambda e: e.dma_start(
            out=x1_out[j * 128:(j + 1) * 128, :], in_=res[:, j * TOK:(j + 1) * TOK]))(),
            reads=[rk(j, c) for c in range(NC4)], writes=[("x1st", j)])
    mk.wait_only("Y", [("x1st", j) for j in range(NJ)])


def build_A2():
    mk = MK()
    io = {
        "xT": mk.dram("xT", [D, TOK], F32, kind="ExternalInput").ap(),
        "qT": mk.dram("qT", [D, TOK], BF16, kind="ExternalInput").ap(),
        "kTf": mk.dram("kTf", [D, S], BF16, kind="ExternalInput").ap(),
        "vf": mk.dram("vf", [S, D], BF16, kind="ExternalInput").ap(),
        "x1_out": mk.dram("x1T", [D, TOK], F32, kind="ExternalOutput").ap(),
    }
    mk.psum_pool()
    consts = emit_consts(mk)
    mk.init_arena()
    emit_phase_A2(mk, consts, io)
    return mk




def emit_ln_fm_stream(mk, consts, tag, src_dram, xln, bufs):
    for j in range(NJ):
        par = j % 2
        mk.emit("Yq", (lambda j=j: lambda e: e.dma_start(
            out=bufs["resb"], in_=src_dram[j * 128:(j + 1) * 128, :]))(),
            reads=[(tag, "resb")], writes=[(tag, "resbw")])
        mk.emit("S", (lambda par=par: lambda e: e.activation(
            bufs["xb"][:, par * TOK:(par + 1) * TOK], bufs["resb"], AF.Copy))(),
            reads=[(tag, "resbw")], writes=[(tag, "xb", par)])
        mk.emit("S", (lambda par=par: lambda e: e.activation(
            bufs["xq"][:, par * TOK:(par + 1) * TOK], bufs["resb"], AF.Square))(),
            reads=[(tag, "resbw")], writes=[(tag, "xq", par), (tag, "resb")])

        def mmstats(e, j=j, par=par):
            last = None
            for c in range(NC4):
                last = e.matmul(mk.pool[0:1, c * 512:c * 512 + 512], consts["onesb"][:],
                                bufs["xb"][:, par * TOK + c * 512: par * TOK + c * 512 + 512],
                                start=(j == 0), stop=(j == NJ - 1))
                last = e.matmul(mk.pool[0:1, 2048 + c * 512:2048 + c * 512 + 512], consts["onesb"][:],
                                bufs["xq"][:, par * TOK + c * 512: par * TOK + c * 512 + 512],
                                start=(j == 0), stop=(j == NJ - 1))
            return last
        mk.emit("T", mmstats, reads=[(tag, "xb", par), (tag, "xq", par), ("c", "onesb")],
                writes=PB)
    tmp = bufs
    prow = mk.pool[0:1, 0:TOK]
    mk.emit("V", lambda e: e.tensor_scalar_mul(tmp["brow"], mk.pool[0:1, 0:TOK], 1.0 / D),
            reads=[("pb", i) for i in range(4)], writes=[(tag, "brow")])
    mk.emit("V", lambda e: e.tensor_scalar_mul(tmp["arow"], mk.pool[0:1, 2048:2048 + TOK], 1.0 / D),
            reads=[("pb", i) for i in range(4, 8)], writes=[(tag, "arow")])
    mk.emit("V", lambda e: e.tensor_mul(prow, tmp["brow"], tmp["brow"]),
            reads=[(tag, "brow")] + [("pb", i) for i in range(4)],
            writes=[("pb", 0), ("pb", 1), ("pb", 2), ("pb", 3)])
    mk.emit("V", lambda e: e.tensor_sub(tmp["arow"], tmp["arow"], prow),
            reads=[(tag, "arow")] + [("pb", i) for i in range(4)], writes=[(tag, "arow")])
    mk.emit("S", lambda e: e.activation(tmp["arow"], tmp["arow"], AF.Ln, bias=consts["epsrow"][0:1, 0:1]),
            reads=[(tag, "arow"), ("c", "eps")], writes=[(tag, "arow")])
    mk.emit("S", lambda e: e.activation(tmp["arow"], tmp["arow"], AF.Exp, scale=-0.5),
            reads=[(tag, "arow")], writes=[(tag, "arow")])
    mk.emit("V", lambda e: e.scalar_tensor_tensor(tmp["brow"], tmp["brow"], -1.0,
                                                  tmp["arow"], ALU.mult, ALU.mult),
            reads=[(tag, "brow"), (tag, "arow")], writes=[(tag, "brow")])
    mk.barrier()
    for c in range(NC4):
        def bcast(e, c=c):
            e.matmul(mk.pb(0), consts["ones1f"][:], tmp["arow"][0:1, c * 512:c * 512 + 512],
                     start=True, stop=True)
            return e.matmul(mk.pb(1), consts["ones1f"][:], tmp["brow"][0:1, c * 512:c * 512 + 512],
                            start=True, stop=True)
        mk.emit("T", bcast, reads=[(tag, "arow"), (tag, "brow"), ("c", "ones1f")],
                writes=[("pb", 0), ("pb", 1)])
        mk.emit("V", (lambda c=c: lambda e: e.tensor_copy(
            bufs["absf"][:, c * 512:c * 512 + 512], mk.pb(0)))(),
            reads=[("pb", 0)], writes=[(tag, "absf", c)])
        mk.emit("V", (lambda c=c: lambda e: e.tensor_copy(
            bufs["absf"][:, TOK + c * 512:TOK + c * 512 + 512], mk.pb(1)))(),
            reads=[("pb", 1)], writes=[(tag, "absf", c)])
    for j in range(NJ):
        mk.emit("Yq", (lambda j=j: lambda e: e.dma_start(
            out=bufs["resb"], in_=src_dram[j * 128:(j + 1) * 128, :]))(),
            reads=[(tag, "resb")], writes=[(tag, "resbw")])
        for c in range(NC4):
            ch = "V" if c % 2 == 0 else "G"
            tk = (tag, "t1a") if ch == "V" else (tag, "t1b")
            tb = bufs["t1a"] if ch == "V" else bufs["t1b"]
            mk.emit(ch, (lambda c=c, tb=tb: lambda e: e.tensor_mul(
                tb, bufs["resb"][:, c * 512:c * 512 + 512],
                bufs["absf"][:, c * 512:c * 512 + 512]))(),
                reads=[(tag, "resbw"), (tag, "absf", c)], writes=[tk])
            mk.emit(ch, (lambda j=j, c=c, tb=tb: lambda e: e.tensor_add(
                xln[:, j * TOK + c * 512: j * TOK + c * 512 + 512], tb,
                bufs["absf"][:, TOK + c * 512:TOK + c * 512 + 512]))(),
                reads=[tk, (tag, "absf", c)], writes=[(tag, "xln", j, c)])
    return



NE = 8
NHC = 16


def emit_phase_B(mk, consts, eyef, io, debug_stop=None):
    xT_in, x2_out = io["xT"], io["x2_out"]
    gwT_in, gb_in, w1_in, b1_in, w2_in, b2_in = (io["gwT"], io["gb"], io["w1"],
                                                 io["b1r"], io["w2"], io["b2"])
    x2_out2 = io.get("x2_out2")
    gwT = mk.asbuf("gwT_s", [128, NJ * NE], BF16)
    mk.emit("Gq", lambda e: e.dma_start(out=gwT[:, :].rearrange("p (j e) -> p j e", j=NJ),
                                        in_=gwT_in.rearrange("(j p) e -> p j e", p=128)),
            writes=[("c", "gwT")])
    gbs = mk.asbuf("gb_s", [NE, 1], F32)
    mk.emit("Yq", lambda e: e.dma_start(out=gbs[:], in_=gb_in[:, :]), writes=[("c", "gb")])
    b1s = mk.asbuf("b1_s", [128, NE * NHC], F32)
    mk.emit("Yq", lambda e: e.dma_start(out=b1s[:, :], in_=b1_in[:, :]),
            writes=[("c", "b1")])
    b2s = mk.asbuf("b2_s", [1, NE * D], BF16)
    mk.emit("Gq", lambda e: e.dma_start(out=b2s[:], in_=b2_in.rearrange("e o -> (e o)")[None, :]),
            writes=[("c", "b2")])
    zer = mk.asbuf("zer_s", [128, 512], BF16)
    mk.emit("G", lambda e: e.memset(zer[:], 0.0), writes=[("c", "zer")])

    xln = mk.asbuf("xln", [128, NJ * TOK], BF16)
    acc = mk.asbuf("acc", [128, NTT * 1024], BF16)
    gtm = mk.asbuf("gtm", [128, NTT * NE], F32)
    tmp2 = mk.asbuf("tmp2", [128, 2 * 512], BF16)
    ARN = mk.asbuf("arn", [128, 40960], BF16)

    lnb = {
        "xb": ARN[:, 16384:20480],
        "xq": ARN[:, 20480:24576],
        "absf": ARN[:, 16384:24576].bitcast(F32),
        "arow": ARN[:, 24576:28672].bitcast(F32)[0:1, 0:TOK],
        "brow": ARN[:, 28672:32768].bitcast(F32)[0:1, 0:TOK],
        "t1a": ARN[:, 32768:34816].bitcast(F32)[:, 0:512],
        "t1b": ARN[:, 32768:34816].bitcast(F32)[:, 512:1024],
        "resb": ARN[:, 34816:38912].bitcast(F32),
    }
    w1s = ARN[:, 0:16384]
    w2s = ARN[:, 16384:32768]
    htc = ARN[:, 32768:40960]

    def load_w1(e_, first=False):
        for hc in range(NHC):
            mk.emit("Gq", (lambda e_=e_, hc=hc: lambda e: e.dma_start(
                out=w1s[:, hc * 1024:(hc + 1) * 1024].rearrange("p (j f) -> p j f", f=128),
                in_=w1_in[e_, :, hc * 128:(hc + 1) * 128].rearrange("(j p) f -> p j f", p=128)))(),
                reads=[("w1s", hc)], writes=[("w1s", hc)])

    def load_w2(e_):
        for hc in range(NHC):
            mk.emit("Gq", (lambda e_=e_, hc=hc: lambda e: e.dma_start(
                out=w2s[:, hc * 1024:(hc + 1) * 1024],
                in_=w2_in[e_, hc * 128:(hc + 1) * 128, :]))(),
                reads=[("w2s", hc)], writes=[("w2s", hc)])

    load_w1(0, first=True)
    emit_ln_fm_stream(mk, consts, "lnb", xT_in, xln, lnb)
    xk = lambda j, c: ("lnb", "xln", j, c)
    if debug_stop == "ln":
        return

    logT = ARN[:, 16384:20480].bitcast(F32)[0:8, 0:TOK]
    expT = ARN[:, 20480:24576].bitcast(F32)[0:8, 0:TOK]
    prbT = ARN[:, 24576:28672].bitcast(F32)[0:8, 0:TOK]
    denB = ARN[:, 28672:32768].bitcast(F32)[0:1, 0:TOK]
    ptm = ARN[:, 32768:33280].bitcast(F32)[:, 0:128]
    sc1 = ARN[:, 33280:33792].bitcast(F32)[:, 0:128]
    sc2 = ARN[:, 33792:34304].bitcast(F32)[:, 0:128]
    scm = ARN[:, 34304:34816].bitcast(F32)[:, 0:32]

    def gmm(e, c=None):
        last = None
        for c_ in range(NC4):
            for j in range(NJ):
                last = e.matmul(mk.pool[0:8, 2048 + c_ * 512:2048 + c_ * 512 + 512],
                                gwT[:, j * NE:(j + 1) * NE],
                                xln[:, j * TOK + c_ * 512: j * TOK + c_ * 512 + 512],
                                start=(j == 0), stop=(j == NJ - 1))
        return last
    mk.emit("T", gmm, reads=[("c", "gwT")] + [xk(j, c) for j in range(NJ) for c in range(NC4)],
            writes=[("pb", 4), ("pb", 5), ("pb", 6), ("pb", 7)])
    mk.emit("V", lambda e: e.tensor_scalar(logT, mk.pool[0:8, 2048:2048 + TOK],
                                           gbs[:, 0:1], None, ALU.add),
            reads=[("pb", i) for i in range(4, 8)] + [("c", "gb")], writes=[("g", "log")])
    mk.emit("S", lambda e: e.activation(expT, logT, AF.Exp),
            reads=[("g", "log")], writes=[("g", "exp")])

    def gden(e):
        last = None
        for c_ in range(NC4):
            last = e.matmul(mk.pool[0:1, c_ * 512:c_ * 512 + 512], consts["ones8f"][:, :],
                            expT[:, c_ * 512:c_ * 512 + 512], start=True, stop=True)
        return last
    mk.emit("T", gden, reads=[("g", "exp"), ("c", "ones8f")], writes=[("pb", 0), ("pb", 1)])
    mk.emit("V", lambda e: e.reciprocal(denB, mk.pool[0:1, 0:TOK]),
            reads=[("pb", 0), ("pb", 1)], writes=[("g", "den")])

    def gbc(e):
        last = None
        for c_ in range(NC4):
            last = e.matmul(mk.pool[0:8, c_ * 512:c_ * 512 + 512], consts["ones1f"][0:1, 0:8],
                            denB[0:1, c_ * 512:c_ * 512 + 512], start=True, stop=True)
        return last
    mk.emit("T", gbc, reads=[("g", "den"), ("c", "ones1f")], writes=[("pb", 0), ("pb", 1)])
    mk.emit("V", lambda e: e.tensor_mul(prbT, expT, mk.pool[0:8, 0:TOK]),
            reads=[("pb", 0), ("pb", 1), ("g", "exp")], writes=[("g", "prb")])

    def gtr(e):
        last = None
        for tt in range(NTT):
            last = e.transpose(mk.pool[:, 2048 + tt * 8: 2048 + tt * 8 + 8],
                               prbT[:, tt * 128:(tt + 1) * 128], eyef[0:8, 0:8])
        return last
    mk.emit("T", gtr, reads=[("g", "prb"), ("c", "eyef")], writes=[("pb", 4)])
    mk.emit("V", lambda e: e.tensor_copy(ptm[:, :], mk.pool[:, 2048:2048 + 128]),
            reads=[("pb", 4)], writes=[("g", "ptm")])
    p3 = ptm.rearrange("p (t e) -> p t e", e=NE)
    s13 = sc1.rearrange("p (t e) -> p t e", e=NE)
    s23 = sc2.rearrange("p (t e) -> p t e", e=NE)
    m3 = scm.rearrange("p (t o) -> p t o", o=2)
    mk.emit("V", lambda e: e.reduce_max(m3[:, :, 0:1], p3, axis=AX),
            reads=[("g", "ptm")], writes=[("g", "m1")])
    mk.emit("V", lambda e: e.tensor_tensor(s13, p3, m3[:, :, 0:1].broadcast_to([128, NTT, NE]), ALU.is_ge),
            reads=[("g", "ptm"), ("g", "m1")], writes=[("g", "is1")])
    mk.emit("V", lambda e: e.tensor_mul(s23, p3, s13),
            reads=[("g", "ptm"), ("g", "is1")], writes=[("g", "t")])
    mk.emit("V", lambda e: e.tensor_sub(s23, p3, s23),
            reads=[("g", "ptm"), ("g", "t")], writes=[("g", "pm")])
    mk.emit("V", lambda e: e.reduce_max(m3[:, :, 1:2], s23, axis=AX),
            reads=[("g", "pm")], writes=[("g", "m2")])
    mk.emit("V", lambda e: e.tensor_tensor(s23, s23, m3[:, :, 1:2].broadcast_to([128, NTT, NE]), ALU.is_ge),
            reads=[("g", "pm"), ("g", "m2")], writes=[("g", "is2")])
    mk.emit("V", lambda e: e.tensor_add(s13, s13, s23),
            reads=[("g", "is1"), ("g", "is2")], writes=[("g", "sel")])
    mk.emit("V", lambda e: e.tensor_mul(gtm[:, :].rearrange("p (t e) -> p t e", e=NE), p3, s13),
            reads=[("g", "sel"), ("g", "ptm")], writes=[("c", "gtm")])

    if debug_stop == "gate":
        return
    mk.emit("G", lambda e: e.memset(acc[:, :], 0.0),
            writes=[("acc", tt, oc) for tt in range(NTT) for oc in range(2)])
    mk.barrier()
    load_w2(0)
    for e_ in range(NE):
        for tc in range(NC4):
            for hc in range(NHC):
                pbk = hc % 2

                def p1mm(e, e_=e_, tc=tc, hc=hc, pbk=pbk):
                    last = None
                    for j in range(NJ):
                        last = e.matmul(mk.pb(pbk),
                                        w1s[:, hc * 1024 + j * 128: hc * 1024 + j * 128 + 128],
                                        xln[:, j * TOK + tc * 512: j * TOK + tc * 512 + 512],
                                        start=(j == 0), stop=(j == NJ - 1))
                    return last
                mk.emit("T", p1mm, reads=[("w1s", hc)] + [xk(j, tc) for j in range(NJ)],
                        writes=[("pb", pbk)])
                if hc % 2 == 0:
                    mk.emit("S", (lambda e_=e_, hc=hc, pbk=pbk: lambda e: e.activation(
                        htc[:, hc * 512:(hc + 1) * 512], mk.pb(pbk), AF.Relu,
                        bias=b1s[:, e_ * NHC + hc: e_ * NHC + hc + 1]))(),
                        reads=[("pb", pbk), ("c", "b1")], writes=[("h", hc)])
                else:
                    mk.emit("V", (lambda e_=e_, hc=hc, pbk=pbk: lambda e: e.scalar_tensor_tensor(
                        htc[:, hc * 512:(hc + 1) * 512], mk.pb(pbk),
                        b1s[:, e_ * NHC + hc: e_ * NHC + hc + 1], zer[:, :],
                        ALU.add, ALU.max))(),
                        reads=[("pb", pbk), ("c", "b1"), ("c", "zer")], writes=[("h", hc)])
            for ttl in range(4):
                tt = tc * 4 + ttl
                for oc in range(2):
                    pbk = 2 + (ttl * 2 + oc) % 2

                    def p2mm(e, e_=e_, ttl=ttl, oc=oc, pbk=pbk):
                        last = None
                        for hc in range(NHC):
                            e.matmul(mk.pb(pbk),
                                     htc[:, hc * 512 + ttl * 128: hc * 512 + ttl * 128 + 128],
                                     w2s[:, hc * 1024 + oc * 512: hc * 1024 + oc * 512 + 512],
                                     start=(hc == 0), stop=False)
                        last = e.matmul(mk.pb(pbk), consts["ones1b"][0:1, :],
                                        b2s[0:1, e_ * D + oc * 512: e_ * D + oc * 512 + 512],
                                        start=False, stop=True)
                        return last
                    mk.emit("T", p2mm,
                            reads=[("h", hc) for hc in range(NHC)] + [("w2s", hc) for hc in range(NHC)] +
                                  [("c", "b2"), ("c", "ones1b")],
                            writes=[("pb", pbk)])
                    tpar = (ttl * 2 + oc) % 2
                    mk.emit("V", (lambda e_=e_, tt=tt, pbk=pbk, tpar=tpar: lambda e: e.tensor_scalar(
                        tmp2[:, tpar * 512:(tpar + 1) * 512], mk.pb(pbk),
                        gtm[:, tt * NE + e_: tt * NE + e_ + 1], None, ALU.mult))(),
                        reads=[("pb", pbk), ("c", "gtm")], writes=[("tmp2", tpar)])
                    mk.emit("G", (lambda tt=tt, oc=oc, tpar=tpar: lambda e: e.tensor_add(
                        acc[:, tt * 1024 + oc * 512: tt * 1024 + oc * 512 + 512],
                        acc[:, tt * 1024 + oc * 512: tt * 1024 + oc * 512 + 512],
                        tmp2[:, tpar * 512:(tpar + 1) * 512]))(),
                        reads=[("tmp2", tpar), ("acc", tt, oc)], writes=[("acc", tt, oc)])
        if e_ + 1 < NE:
            load_w1(e_ + 1)
            load_w2(e_ + 1)
    if debug_stop == "experts":
        return
    mk.barrier()
    av = {
        "sqt": ARN[:, 32768:34816][:, 0:1024],
        "lnf": ARN[:, 34816:38912].bitcast(F32)[:, 0:1024],
        "svec": ARN[:, 38912:39040].bitcast(F32)[:, 0:48],
        "lnfT": ARN[:, 0:16384],
        "resb": ARN[:, 16384:20480].bitcast(F32),
        "outb": ARN[:, 20480:24576].bitcast(F32),
    }
    emit_postln_tm(mk, consts, eyef, acc,
                   lambda tt: [("acc", tt, 0), ("acc", tt, 1)],
                   av, xT_in, x2_out, out_dram2=x2_out2, tag="plB")


def emit_postln_tm(mk, consts, eyef, src, skey, av, xT_in, out_dram, tag="pl", out_dram2=None):
    sqt, lnf, svec, lnfT, resb, outb = av["sqt"], av["lnf"], av["svec"], av["lnfT"], av["resb"], av["outb"]
    mvec = svec[:, 0:16]
    vvec = svec[:, 16:32]
    rvec = svec[:, 32:48]
    for tt in range(NTT):
        at = src[:, tt * 1024:(tt + 1) * 1024]
        mk.emit("V", (lambda at=at, tt=tt: lambda e: e.reduce_sum(
            mvec[:, tt:tt + 1], at, axis=AX))(),
            reads=skey(tt), writes=[(tag, "m", tt)])
        mk.emit("S", (lambda at=at: lambda e: e.activation(sqt[:, 0:1024], at, AF.Square))(),
            reads=skey(tt), writes=[(tag, "sq")])
        mk.emit("V", (lambda tt=tt: lambda e: e.reduce_sum(
            vvec[:, tt:tt + 1], sqt[:, 0:1024], axis=AX))(),
            reads=[(tag, "sq")], writes=[(tag, "v", tt)])
    mk.emit("V", lambda e: e.tensor_scalar_mul(mvec[:, :], mvec[:, :], 1.0 / D),
            reads=[(tag, "m", tt) for tt in range(NTT)], writes=[(tag, "mv")])
    mk.emit("V", lambda e: e.tensor_scalar_mul(vvec[:, :], vvec[:, :], 1.0 / D),
            reads=[(tag, "v", tt) for tt in range(NTT)], writes=[(tag, "vv")])
    mk.emit("V", lambda e: e.scalar_tensor_tensor(rvec[:, :], mvec[:, :], -1.0, mvec[:, :],
                                                  ALU.mult, ALU.mult),
            reads=[(tag, "mv")], writes=[(tag, "rv")])
    mk.emit("V", lambda e: e.tensor_add(vvec[:, :], vvec[:, :], rvec[:, :]),
            reads=[(tag, "vv"), (tag, "rv")], writes=[(tag, "vv")])
    mk.emit("S", lambda e: e.activation(vvec[:, :], vvec[:, :], AF.Ln, bias=consts["epscol"][:, 0:1]),
            reads=[(tag, "vv"), ("c", "epsc")], writes=[(tag, "vv")])
    mk.emit("S", lambda e: e.activation(rvec[:, :], vvec[:, :], AF.Exp, scale=-0.5),
            reads=[(tag, "vv")], writes=[(tag, "rv")])
    for tt in range(NTT):
        at = src[:, tt * 1024:(tt + 1) * 1024]
        mk.emit("V", (lambda at=at, tt=tt: lambda e: e.tensor_scalar(
            lnf[:, 0:1024], at, mvec[:, tt:tt + 1], rvec[:, tt:tt + 1],
            ALU.subtract, ALU.mult))(),
            reads=skey(tt) + [(tag, "mv"), (tag, "rv")],
            writes=[(tag, "lnf")])

        def ttr(e, tt=tt):
            last = None
            for j in range(NJ):
                last = e.transpose(mk.pool[:, 2048 + j * 128: 2048 + j * 128 + 128],
                                   lnf[:, j * 128:(j + 1) * 128], eyef[:])
            return last
        mk.emit("T", ttr, reads=[(tag, "lnf"), ("c", "eyef")],
                writes=[("pb", 4), ("pb", 5), ("pb", 6), ("pb", 7)])

        def tev(e, tt=tt):
            last = None
            for j in range(NJ):
                last = e.copy(lnfT[:, j * TOK + tt * 128: j * TOK + tt * 128 + 128],
                              mk.pool[:, 2048 + j * 128: 2048 + j * 128 + 128])
            return last
        mk.emit("S", tev, reads=[("pb", 4), ("pb", 5), ("pb", 6), ("pb", 7), (tag, "lnf")],
                writes=[(tag, "lnfT", tt)])
    for j in range(NJ):
        mk.emit("Yq", (lambda j=j: lambda e: e.dma_start(
            out=resb[:, :], in_=xT_in[j * 128:(j + 1) * 128, :]))(),
            reads=[(tag, "resb")], writes=[(tag, "resbw")])
        mk.emit("V", (lambda j=j: lambda e: e.tensor_add(
            outb[:, :], resb[:, :], lnfT[:, j * TOK:(j + 1) * TOK]))(),
            reads=[(tag, "resbw"), (tag, "outb")] + [(tag, "lnfT", tt) for tt in range(NTT)],
            writes=[(tag, "outbw"), (tag, "resb")])
        mk.emit("Yq", (lambda j=j: lambda e: e.dma_start(
            out=out_dram[j * 128:(j + 1) * 128, :], in_=outb[:, :]))(),
            reads=[(tag, "outbw")], writes=[(tag, "st", j)])
        if out_dram2 is not None:
            mk.emit("Gq", (lambda j=j: lambda e: e.dma_start(
                out=out_dram2[j * 128:(j + 1) * 128, :], in_=outb[:, :]))(),
                reads=[(tag, "outbw")], writes=[(tag, "st2", j)])
        mk.regions[(tag, "outb")] = [mk.regions[(tag, "st", j)][0], []]
        if out_dram2 is not None:
            mk.regions[(tag, "outb")][1].append(mk.regions[(tag, "st2", j)][0])
    mk.wait_only("Y", [(tag, "st", j) for j in range(NJ)] +
                 ([(tag, "st2", j) for j in range(NJ)] if out_dram2 is not None else []))


def build_B_full():
    mk = MK()
    io = {
        "xT": mk.dram("xT", [D, TOK], F32, kind="ExternalInput").ap(),
        "gwT": mk.dram("gwT", [D, NE], F32, kind="ExternalInput").ap(),
        "gb": mk.dram("gb", [NE, 1], F32, kind="ExternalInput").ap(),
        "w1": mk.dram("w1", [NE, D, 2048], F32, kind="ExternalInput").ap(),
        "b1r": mk.dram("b1r", [128, NE * NHC], F32, kind="ExternalInput").ap(),
        "w2": mk.dram("w2", [NE, 2048, D], F32, kind="ExternalInput").ap(),
        "b2": mk.dram("b2", [NE, D], F32, kind="ExternalInput").ap(),
        "x2_out": mk.dram("x2T", [D, TOK], F32, kind="ExternalOutput").ap(),
    }
    eyef_in = mk.dram("eyef", [128, 128], F32, kind="ExternalInput").ap()
    mk.psum_pool()
    consts = emit_consts(mk)
    eyef = mk.sbuf("eyef_s", [128, 128], F32)
    mk.emit("Yq", lambda e: e.dma_start(out=eyef[:], in_=eyef_in[:, :]), writes=[("c", "eyef")])
    mk.init_arena()
    emit_phase_B(mk, consts, eyef, io)
    return mk



def emit_phase_C1(mk, consts, eyef, io):
    xT_in, wq_in, cos_in, sin_in, prot_in = (io["xT"], io["wqkv"], io["cos2"],
                                             io["sin2"], io["prot"])
    qT_out, kT_out, v_out = io["qT"], io["kT"], io["v"]
    cos2 = mk.asbuf("cos2_s", [128, TOK], F32)
    sin2 = mk.asbuf("sin2_s", [128, TOK], F32)
    protb = mk.asbuf("prot_s", [128, 128], BF16)
    mk.emit("Yq", lambda e: e.dma_start(out=cos2[:], in_=cos_in[:, :]), writes=[("c", "cos")])
    mk.emit("Yq", lambda e: e.dma_start(out=sin2[:], in_=sin_in[:, :]), writes=[("c", "sin")])
    mk.emit("Gq", lambda e: e.dma_start(out=protb[:], in_=prot_in[:, :]), writes=[("c", "prot")])
    wq = mk.asbuf("wq_s", [128, 24 * NJ * 128], BF16)
    for ot in range(24):
        mk.emit("Gq", (lambda ot=ot: lambda e: e.dma_start(
            out=wq[:, ot * 1024:(ot + 1) * 1024].rearrange("p (j f) -> p j f", f=128),
            in_=wq_in[:, ot * 128:(ot + 1) * 128].rearrange("(j p) f -> p j f", p=128)))(),
            writes=[("wq", ot)])

    xln = mk.asbuf("xln_c1", [128, NJ * TOK], BF16)
    qbuf = mk.asbuf("qbuf", [128, 2 * TOK], BF16)
    kbuf = mk.asbuf("kbuf", [128, 2 * TOK], BF16)
    vbuf = mk.asbuf("vbuf", [128, 2 * TOK], BF16)
    braw = mk.asbuf("braw", [128, 2 * 512], BF16)
    t1f = mk.asbuf("t1f", [128, 2 * 512], F32)
    t2f = mk.asbuf("t2f", [128, 2 * 512], F32)
    vf32 = mk.asbuf("vf32", [128, 2 * 512], F32)
    ARN = mk.asbuf("arn_c1", [128, 24576], BF16)
    lnb = {
        "xb": ARN[:, 0:4096],
        "xq": ARN[:, 4096:8192],
        "absf": ARN[:, 0:8192].bitcast(F32),
        "arow": ARN[:, 8192:12288].bitcast(F32)[0:1, 0:TOK],
        "brow": ARN[:, 12288:16384].bitcast(F32)[0:1, 0:TOK],
        "t1a": ARN[:, 16384:18432].bitcast(F32)[:, 0:512],
        "t1b": ARN[:, 16384:18432].bitcast(F32)[:, 512:1024],
        "resb": ARN[:, 18432:22528].bitcast(F32),
    }
    emit_ln_fm_stream(mk, consts, "lnc", xT_in, xln, lnb)
    xk = lambda j, c: ("lnc", "xln", j, c)

    for ot in range(24):
        kind = 0 if ot < 8 else (1 if ot < 16 else 2)
        jd = ot % 8
        for c in range(NC4):
            cpar = c % 2

            def pmm(e, ot=ot, c=c, cpar=cpar):
                last = None
                for j in range(NJ):
                    last = e.matmul(mk.pb(cpar),
                                    wq[:, ot * 1024 + j * 128: ot * 1024 + j * 128 + 128],
                                    xln[:, j * TOK + c * 512: j * TOK + c * 512 + 512],
                                    start=(j == 0), stop=(j == NJ - 1))
                return last
            mk.emit("T", pmm, reads=[("wq", ot)] + [xk(j, c) for j in range(NJ)],
                    writes=[("pb", cpar)])
            if kind < 2:
                dst = qbuf if kind == 0 else kbuf
                mk.emit("S", (lambda cpar=cpar: lambda e: e.copy(
                    braw[:, cpar * 512:(cpar + 1) * 512], mk.pb(cpar)))(),
                    reads=[("pb", cpar)], writes=[("braw", cpar)])
                mk.emit("T", (lambda cpar=cpar: lambda e: e.matmul(
                    mk.pb(2 + cpar), protb[:],
                    braw[:, cpar * 512:(cpar + 1) * 512], start=True, stop=True))(),
                    reads=[("braw", cpar), ("c", "prot")], writes=[("pb", 2 + cpar)])
                mk.emit("V", (lambda c=c, cpar=cpar: lambda e: e.tensor_mul(
                    t1f[:, cpar * 512:(cpar + 1) * 512],
                    braw[:, cpar * 512:(cpar + 1) * 512],
                    cos2[:, c * 512:(c + 1) * 512]))(),
                    reads=[("braw", cpar), ("c", "cos")], writes=[("t1", cpar)])
                mk.emit("V", (lambda c=c, cpar=cpar: lambda e: e.tensor_mul(
                    t2f[:, cpar * 512:(cpar + 1) * 512], mk.pb(2 + cpar),
                    sin2[:, c * 512:(c + 1) * 512]))(),
                    reads=[("pb", 2 + cpar), ("c", "sin")], writes=[("t2", cpar)])
                jp = jd % 2
                mk.emit("V", (lambda dst=dst, jp=jp, c=c, cpar=cpar: lambda e: e.tensor_add(
                    dst[:, jp * TOK + c * 512: jp * TOK + c * 512 + 512],
                    t1f[:, cpar * 512:(cpar + 1) * 512],
                    t2f[:, cpar * 512:(cpar + 1) * 512]))(),
                    reads=[("t1", cpar), ("t2", cpar)], writes=[("qk", kind, jp, c)])
            else:
                mk.emit("V", (lambda cpar=cpar: lambda e: e.tensor_copy(
                    vf32[:, cpar * 512:(cpar + 1) * 512], mk.pb(cpar)))(),
                    reads=[("pb", cpar)], writes=[("vf32", cpar)])

                def vtr(e, cpar=cpar):
                    last = None
                    for i in range(4):
                        last = e.transpose(
                            mk.pool[:, 2048 + cpar * 1024 + i * 128: 2048 + cpar * 1024 + i * 128 + 128],
                            vf32[:, cpar * 512 + i * 128: cpar * 512 + i * 128 + 128], eyef[:])
                    return last
                mk.emit("T", vtr, reads=[("vf32", cpar), ("c", "eyef")],
                        writes=[("pb", 4 + 2 * cpar), ("pb", 5 + 2 * cpar)])

                jp = jd % 2

                def vev(e, jp=jp, c=c, cpar=cpar):
                    last = None
                    for i in range(4):
                        tt = c * 4 + i
                        last = e.copy(vbuf[:, jp * TOK + tt * 128: jp * TOK + tt * 128 + 128],
                                      mk.pool[:, 2048 + cpar * 1024 + i * 128: 2048 + cpar * 1024 + i * 128 + 128])
                    return last
                mk.emit("S", vev, reads=[("pb", 4 + 2 * cpar), ("pb", 5 + 2 * cpar)],
                        writes=[("qk", 2, jp, c)])
        jp = jd % 2
        if kind == 0:
            mk.emit("Yq", (lambda jd=jd, jp=jp: lambda e: e.dma_start(
                out=qT_out[jd * 128:(jd + 1) * 128, :], in_=qbuf[:, jp * TOK:(jp + 1) * TOK]))(),
                reads=[("qk", 0, jp, c) for c in range(NC4)], writes=[("qk", 0, jp, "st")])
        elif kind == 1:
            mk.emit("Yq", (lambda jd=jd, jp=jp: lambda e: e.dma_start(
                out=kT_out[jd * 128:(jd + 1) * 128, :], in_=kbuf[:, jp * TOK:(jp + 1) * TOK]))(),
                reads=[("qk", 1, jp, c) for c in range(NC4)], writes=[("qk", 1, jp, "st")])
        else:
            mk.emit("Yq", (lambda jd=jd, jp=jp: lambda e: e.dma_start(
                out=v_out.rearrange("(tt p) d -> p tt d", p=128)[:, :, jd * 128:(jd + 1) * 128],
                in_=vbuf[:, jp * TOK:(jp + 1) * TOK].rearrange("p (tt f) -> p tt f", f=128)))(),
                reads=[("qk", 2, jp, c) for c in range(NC4)], writes=[("qk", 2, jp, "st")])
    mk.wait_only("Y", [("qk", k, jp, "st") for k in range(3) for jp in range(2)])


def build_C1():
    mk = MK()
    io = {
        "xT": mk.dram("xT", [D, TOK], F32, kind="ExternalInput").ap(),
        "wqkv": mk.dram("wqkv", [D, 3 * D], F32, kind="ExternalInput").ap(),
        "cos2": mk.dram("cos2", [128, TOK], F32, kind="ExternalInput").ap(),
        "sin2": mk.dram("sin2", [128, TOK], F32, kind="ExternalInput").ap(),
        "prot": mk.dram("prot", [128, 128], F32, kind="ExternalInput").ap(),
        "qT": mk.dram("qT", [D, TOK], BF16, kind="ExternalOutput").ap(),
        "kT": mk.dram("kT", [D, TOK], BF16, kind="ExternalOutput").ap(),
        "v": mk.dram("v", [TOK, D], BF16, kind="ExternalOutput").ap(),
    }
    eyef_in = mk.dram("eyef", [128, 128], F32, kind="ExternalInput").ap()
    mk.psum_pool()
    consts = emit_consts(mk)
    eyef = mk.sbuf("eyef_s", [128, 128], F32)
    mk.emit("Yq", lambda e: e.dma_start(out=eyef[:], in_=eyef_in[:, :]), writes=[("c", "eyef")])
    mk.init_arena()
    emit_phase_C1(mk, consts, eyef, io)
    return mk



def emit_phase_C2(mk, consts, eyef, io):
    xT_in, qT_in, kTf_in, vf_in, wo_in = (io["xT"], io["qT"], io["kTf"], io["vf"], io["wout"])
    x3_out = io["x3_out"]
    x3_out2 = io.get("x3_out2")
    wo = mk.asbuf("wo_s", [128, NJ * 1024], BF16)
    for j in range(NJ):
        mk.emit("Gq", (lambda j=j: lambda e: e.dma_start(
            out=wo[:, j * 1024:(j + 1) * 1024], in_=wo_in[j * 128:(j + 1) * 128, :]))(),
            writes=[("wo", j)])

    fattn = mk.asbuf("fattn", [128, NJ * TOK], BF16)
    ftm = mk.asbuf("ftm_c2", [128, NTT * 1024], BF16)
    pT = mk.asbuf("pT_c2", [128, 3 * 1024], BF16)
    denr = mk.asbuf("denr_c2", [1, 1024], F32)
    bsb = mk.asbuf("bsb_c2", [128, 4 * 512], F32)
    ARN = mk.asbuf("arn_c2", [128, 36864], BF16)
    kh = ARN[:, 0:8192]
    vh = ARN[:, 8192:16512]
    qh = ARN[:, 16512:18560]
    mk.emit("G", lambda e: e.memset(vh[:, :], 1.0), writes=[("vhones",)])

    SC = 1.0 / np.sqrt(64.0)
    for hp in range(NJ):
        mk.emit("Yq", (lambda hp=hp: lambda e: e.dma_start(
            out=kh[:, :], in_=kTf_in[hp * 128:(hp + 1) * 128, :]))(),
            reads=[("kh",)], writes=[("kh",)])
        for hh in range(2):
            mk.emit("Yq", (lambda hp=hp, hh=hh: lambda e: e.dma_start(
                out=vh[:, :].rearrange("p (kt g) -> p kt g", g=130)[:, :, hh * 65:hh * 65 + 64],
                in_=vf_in.rearrange("(kt p) d -> p kt d", p=128)[:, :, hp * 128 + hh * 64: hp * 128 + hh * 64 + 64]))(),
                reads=[("vh",), ("vhones",)], writes=[("vh",)])
        mk.emit("Yq", (lambda hp=hp: lambda e: e.dma_start(
            out=qh[:, :], in_=qT_in[hp * 128:(hp + 1) * 128, :]))(),
            reads=[("qh",)], writes=[("qh",)])
        for qc in range(NC4):
            qpar = qc % 2
            obank = 6 + qpar
            def emit_scexp(kt, qc=qc):
                spar = kt % 2
                sb0 = spar * 2
                ppar = kt % 3

                def scmm(e, qc=qc, kt=kt, sb0=sb0):
                    last = None
                    for hh in range(2):
                        last = e.matmul(mk.pb(sb0 + hh),
                                        kh[hh * 64:hh * 64 + 64, kt * 128: kt * 128 + 128],
                                        qh[hh * 64:hh * 64 + 64, qc * 512: qc * 512 + 512],
                                        start=True, stop=True, tile_position=(hh * 64, 0))
                    return last
                mk.emit("T", scmm, reads=[("kh",), ("qh",), ("pb", sb0), ("pb", sb0 + 1)],
                        writes=[("pb", sb0), ("pb", sb0 + 1)])

                def expop(e, sb0=sb0, ppar=ppar):
                    return e.activation(pT[:, ppar * 1024:(ppar + 1) * 1024],
                                        mk.pool[:, sb0 * 512:sb0 * 512 + 1024],
                                        AF.Exp, scale=SC)
                mk.emit("S", expop, reads=[("pb", sb0), ("pb", sb0 + 1)], writes=[("pT", ppar)])

            def emit_pv(kt, qpar=qpar):
                ppar = kt % 3
                ob0 = 4 + qpar
                ob1 = 6 + qpar

                def pv(e, kt=kt, ppar=ppar, ob0=ob0, ob1=ob1):
                    last = None
                    for hh, ob in ((0, ob0), (1, ob1)):
                        pslice = pT[:, ppar * 1024 + hh * 512: ppar * 1024 + hh * 512 + 512]
                        last = e.matmul(mk.pool[0:65, ob * 512: ob * 512 + 512],
                                        vh[:, kt * 130 + hh * 65: kt * 130 + hh * 65 + 65],
                                        pslice, start=(kt == 0), stop=(kt == NKT - 1))
                    return last
                mk.emit("T", pv, reads=[("pT", ppar), ("vh",), ("pb", ob0), ("pb", ob1)],
                        writes=[("pb", ob0), ("pb", ob1)])

            for kt in range(NKT + 1):
                if kt < NKT:
                    emit_scexp(kt)
                if kt >= 1:
                    emit_pv(kt - 1)
            ob0 = 4 + qpar
            ob1 = 6 + qpar
            mk.emit("V", (lambda ob0=ob0: lambda e: e.reciprocal(
                denr[0:1, 0:512], mk.pool[64:65, ob0 * 512: ob0 * 512 + 512]))(),
                reads=[("pb", ob0)], writes=[("denr", 0)])
            mk.emit("V", (lambda ob1=ob1: lambda e: e.reciprocal(
                denr[0:1, 512:1024], mk.pool[64:65, ob1 * 512: ob1 * 512 + 512]))(),
                reads=[("pb", ob1)], writes=[("denr", 1)])
            sb0 = 0 if qpar == 0 else 2

            def dbc(e, sb0=sb0):
                e.matmul(mk.pool[0:64, sb0 * 512: sb0 * 512 + 512], consts["ones1f"][0:1, 0:64],
                         denr[0:1, 0:512], start=True, stop=True)
                return e.matmul(mk.pool[64:128, sb0 * 512: sb0 * 512 + 512], consts["ones1f"][0:1, 0:64],
                                denr[0:1, 512:1024], start=True, stop=True)
            mk.emit("T", dbc, reads=[("denr", 0), ("denr", 1), ("c", "ones1f"),
                                     ("pb", sb0), ("pb", sb0 + 1)],
                    writes=[("pb", sb0)])
            s0 = (qpar * 2) * 512
            s1 = (qpar * 2 + 1) * 512
            mk.emit("V", (lambda sb0=sb0, s0=s0: lambda e: e.tensor_copy(
                bsb[0:64, s0:s0 + 512], mk.pool[0:64, sb0 * 512: sb0 * 512 + 512]))(),
                reads=[("pb", sb0)], writes=[("bsb", qpar, 0)])
            mk.emit("V", (lambda sb0=sb0, s1=s1: lambda e: e.tensor_copy(
                bsb[0:64, s1:s1 + 512], mk.pool[64:128, sb0 * 512: sb0 * 512 + 512]))(),
                reads=[("pb", sb0)], writes=[("bsb", qpar, 1)])
            mk.emit("V", (lambda hp=hp, qc=qc, ob0=ob0, s0=s0: lambda e: e.tensor_mul(
                fattn[:, hp * TOK + qc * 512: hp * TOK + qc * 512 + 512][0:64, :],
                mk.pool[0:64, ob0 * 512: ob0 * 512 + 512],
                bsb[0:64, s0:s0 + 512]))(),
                reads=[("bsb", qpar, 0), ("pb", ob0)], writes=[("fa", hp, qc)])
            mk.emit("V", (lambda hp=hp, qc=qc, ob1=ob1, s1=s1: lambda e: e.tensor_mul(
                fattn[:, hp * TOK + qc * 512: hp * TOK + qc * 512 + 512][64:128, :],
                mk.pool[0:64, ob1 * 512: ob1 * 512 + 512],
                bsb[0:64, s1:s1 + 512]))(),
                reads=[("bsb", qpar, 1), ("pb", ob1), ("fa", hp, qc)], writes=[("fa", hp, qc)])

    for tt in range(NTT):
        for oc in range(2):
            pbk = (tt * 2 + oc) % 2

            def womm(e, tt=tt, oc=oc, pbk=pbk):
                last = None
                for j in range(NJ):
                    last = e.matmul(mk.pb(pbk),
                                    fattn[:, j * TOK + tt * 128: j * TOK + tt * 128 + 128],
                                    wo[:, j * 1024 + oc * 512: j * 1024 + oc * 512 + 512],
                                    start=(j == 0), stop=(j == NJ - 1))
                return last
            mk.emit("T", womm,
                    reads=[("fa", j, tt // 4) for j in range(NJ)] + [("wo", j) for j in range(NJ)] +
                          [("pb", pbk)],
                    writes=[("pb", pbk)])
            mk.emit("S", (lambda tt=tt, oc=oc, pbk=pbk: lambda e: e.copy(
                ftm[:, tt * 1024 + oc * 512: tt * 1024 + oc * 512 + 512], mk.pb(pbk)))(),
                reads=[("pb", pbk)], writes=[("ftm", tt, oc)])

    mk.barrier()
    av = {
        "sqt": ARN[:, 18560:20608][:, 0:1024],
        "lnf": ARN[:, 20608:24704].bitcast(F32)[:, 0:1024],
        "svec": ARN[:, 24704:24832].bitcast(F32)[:, 0:48],
        "lnfT": ARN[:, 0:16384],
        "resb": ARN[:, 24832:28928].bitcast(F32)[:, 0:TOK],
        "outb": ARN[:, 28928:33024].bitcast(F32)[:, 0:TOK],
    }
    emit_postln_tm(mk, consts, eyef, ftm,
                   lambda tt: [("ftm", tt, 0), ("ftm", tt, 1)],
                   av, xT_in, x3_out, tag="plC", out_dram2=x3_out2)


def build_C2():
    mk = MK()
    io = {
        "xT": mk.dram("xT", [D, TOK], F32, kind="ExternalInput").ap(),
        "qT": mk.dram("qT", [D, TOK], BF16, kind="ExternalInput").ap(),
        "kTf": mk.dram("kTf", [D, S], BF16, kind="ExternalInput").ap(),
        "vf": mk.dram("vf", [S, D], BF16, kind="ExternalInput").ap(),
        "wout": mk.dram("wout", [D, D], F32, kind="ExternalInput").ap(),
        "x3_out": mk.dram("x3T", [D, TOK], F32, kind="ExternalOutput").ap(),
    }
    eyef_in = mk.dram("eyef", [128, 128], F32, kind="ExternalInput").ap()
    mk.psum_pool()
    consts = emit_consts(mk)
    eyef = mk.sbuf("eyef_s", [128, 128], F32)
    mk.emit("Yq", lambda e: e.dma_start(out=eyef[:], in_=eyef_in[:, :]), writes=[("c", "eyef")])
    mk.init_arena()
    emit_phase_C2(mk, consts, eyef, io)
    return mk




NLP = 8


def emit_phase_D1(mk, consts, eyef, io):
    xT_in, nwT_in, nb_in, w1_in, b1_in, w2_in, b2_in, selm_in = (
        io["xT"], io["nwT3"], io["nb3"], io["w1f"], io["b1f"], io["w2f"], io["b2f"], io["selm"])
    qT_out, kT_out, v_out = io["qT"], io["kT"], io["v"]
    selm = mk.asbuf("selm_s", [16, 16 * 64], BF16)
    mk.emit("Gq", lambda e: e.dma_start(out=selm[:], in_=selm_in[:, :]), writes=[("c", "selm")])
    nwT = mk.asbuf("nwT_s", [128, 3 * NJ * 15], BF16)
    mk.emit("Gq", lambda e: e.dma_start(out=nwT[:, :].rearrange("p (x j r) -> p x j r", x=3, j=NJ),
                                        in_=nwT_in.rearrange("x (j p) r -> p x j r", p=128)),
            writes=[("c", "nwT")])
    nbs = mk.asbuf("nb_s", [15, 3], F32)
    mk.emit("Yq", lambda e: e.dma_start(out=nbs[:, :], in_=nb_in[:, :]),
            writes=[("c", "nb")])
    b1s = mk.asbuf("b1d_s", [128, 3 * NLP], F32)
    mk.emit("Yq", lambda e: e.dma_start(out=b1s[:, :].rearrange("p (x l) -> p x l", x=3),
                                        in_=b1_in.rearrange("x p l -> p x l")),
            writes=[("c", "b1")])
    b2s = mk.asbuf("b2d_s", [16, 3 * D], BF16)
    mk.emit("Gq", lambda e: e.dma_start(out=b2s[:, :].rearrange("p (x o) -> p x o", x=3),
                                        in_=b2_in.rearrange("x p o -> p x o")),
            writes=[("c", "b2")])

    xln = mk.asbuf("xln_d1", [128, NJ * TOK], BF16)
    mH = mk.asbuf("mH", [128, NLP * TOK], BF16)
    selT = mk.asbuf("selT", [16, TOK], BF16)
    sTM = mk.asbuf("sTM", [128, NTT * 16], F32)
    sel3f = mk.asbuf("sel3f", [128, NTT * 16], F32)
    tmpT = mk.asbuf("tmpT", [128, NTT * 16], F32)
    hp_ = mk.asbuf("hp_", [128, 2 * 512], BF16)
    qkbuf = mk.asbuf("qkbuf", [128, 2 * TOK], BF16)
    vtm = mk.asbuf("vtm_d1", [128, NTT * 1024], BF16)
    w1w = mk.asbuf("w1w", [128, NLP * 1024], BF16)
    w2w = mk.asbuf("w2w", [128, NLP * 1024], BF16)
    ARN = mk.asbuf("arn_d1", [128, 22528], BF16)
    lnb = {
        "xb": ARN[:, 0:4096],
        "xq": ARN[:, 4096:8192],
        "absf": ARN[:, 0:8192].bitcast(F32),
        "arow": ARN[:, 8192:12288].bitcast(F32)[0:1, 0:TOK],
        "brow": ARN[:, 12288:16384].bitcast(F32)[0:1, 0:TOK],
        "t1a": ARN[:, 16384:18432].bitcast(F32)[:, 0:512],
        "t1b": ARN[:, 16384:18432].bitcast(F32)[:, 512:1024],
        "resb": ARN[:, 18432:22528].bitcast(F32),
    }

    def load_wf(f):
        for lp in range(NLP):
            mk.emit("Gq", (lambda f=f, lp=lp: lambda e: e.dma_start(
                out=w1w[:, lp * 1024:(lp + 1) * 1024].rearrange("p (j g) -> p j g", g=128),
                in_=w1_in[f, lp].rearrange("(j p) g -> p j g", p=128)))(),
                reads=[("w1w", lp)], writes=[("w1w", lp)])
            mk.emit("Gq", (lambda f=f, lp=lp: lambda e: e.dma_start(
                out=w2w[:, lp * 1024:(lp + 1) * 1024], in_=w2_in[f, lp]))(),
                reads=[("w2w", lp)], writes=[("w2w", lp)])

    load_wf(0)
    emit_ln_fm_stream(mk, consts, "lnd", xT_in, xln, lnb)
    xk = lambda j, c: ("lnd", "xln", j, c)
    mk.barrier()
    sT = ARN[:, 0:4096].bitcast(F32)[0:15, 0:TOK]

    for f in range(3):
        def nsmm(e, f=f):
            last = None
            for c in range(NC4):
                for j in range(NJ):
                    last = e.matmul(mk.pool[0:15, 2048 + c * 512:2048 + c * 512 + 512],
                                    nwT[:, (f * NJ + j) * 15:(f * NJ + j) * 15 + 15],
                                    xln[:, j * TOK + c * 512: j * TOK + c * 512 + 512],
                                    start=(j == 0), stop=(j == NJ - 1))
            return last
        mk.emit("T", nsmm, reads=[("c", "nwT")] + [xk(j, c) for j in range(NJ) for c in range(NC4)],
                writes=[("pb", 4), ("pb", 5), ("pb", 6), ("pb", 7)])
        mk.emit("V", (lambda f=f: lambda e: e.tensor_scalar(
            sT[:, :], mk.pool[0:15, 2048:2048 + TOK], nbs[0:15, f:f + 1], None, ALU.add))(),
            reads=[("pb", i) for i in range(4, 8)] + [("c", "nb")], writes=[("sT",)])

        def str_(e):
            last = None
            for tt in range(NTT):
                last = e.transpose(mk.pool[:, tt * 16: tt * 16 + 15],
                                   sT[:, tt * 128:(tt + 1) * 128], eyef[0:15, 0:15])
            return last
        mk.emit("T", str_, reads=[("sT",), ("c", "eyef")],
                writes=[("pb", 0), ("pb", 1), ("pb", 2), ("pb", 3)])
        mk.emit("V", lambda e: e.tensor_copy(sTM[:, :].rearrange("p (t g) -> p t g", g=16)[:, :, 0:15],
                                             mk.pool[:, 0:NTT * 16].rearrange("p (t g) -> p t g", g=16)[:, :, 0:15]),
                reads=[("pb", 0), ("pb", 1), ("pb", 2), ("pb", 3)], writes=[("sTM",)])
        mk.emit("V", lambda e: e.tensor_single_scalar(sTM[:, :], sTM[:, :], 0.0, ALU.is_ge),
                reads=[("sTM",)], writes=[("sg",)])
        s3 = sTM[:, :].rearrange("p (t g) -> p t g", g=16)
        sel3 = sel3f[:, :].rearrange("p (t g) -> p t g", g=16)
        tp3 = tmpT[:, :].rearrange("p (t g) -> p t g", g=16)

        def bc(ap):
            return ap.broadcast_to([128, NTT, ap.shape[2]]) if False else ap

        mk.emit("V", lambda e: e.tensor_scalar(sel3[:, :, 0:8],
                                               s3[:, :, 0:1].broadcast_to([128, NTT, 8]),
                                               -1.0, 1.0, ALU.mult, ALU.add),
                reads=[("sg",)], writes=[("sel",)])
        mk.emit("V", lambda e: e.tensor_copy(sel3[:, :, 8:16],
                                             s3[:, :, 0:1].broadcast_to([128, NTT, 8])),
                reads=[("sg",), ("sel",)], writes=[("sel",)])
        for lev, base, wid in ((1, 1, 4), (2, 3, 2), (3, 7, 1)):
            ncomb = 2 ** lev
            for comb in range(ncomb):
                node = base + comb
                for b_ in range(2):
                    lo = (comb * 2 + b_) * wid
                    if b_ == 1:
                        mk.emit("V", (lambda node=node, lo=lo, wid=wid: lambda e: e.tensor_copy(
                            tp3[:, :, lo:lo + wid],
                            s3[:, :, node:node + 1].broadcast_to([128, NTT, wid])))(),
                            reads=[("sg",), ("tmpT",)], writes=[("tmpT",)])
                    else:
                        mk.emit("V", (lambda node=node, lo=lo, wid=wid: lambda e: e.tensor_scalar(
                            tp3[:, :, lo:lo + wid],
                            s3[:, :, node:node + 1].broadcast_to([128, NTT, wid]),
                            -1.0, 1.0, ALU.mult, ALU.add))(),
                            reads=[("sg",), ("tmpT",)], writes=[("tmpT",)])
            mk.emit("V", lambda e: e.tensor_mul(sel3f[:, :], sel3f[:, :], tmpT[:, :]),
                    reads=[("sel",), ("tmpT",)], writes=[("sel",)])

        def seltr(e):
            last = None
            for tt in range(NTT):
                last = e.transpose(mk.pool[0:16, tt * 128: tt * 128 + 128],
                                   sel3f[:, tt * 16:(tt + 1) * 16], eyef[:])
            return last
        mk.emit("T", seltr, reads=[("sel",), ("c", "eyef")],
                writes=[("pb", 0), ("pb", 1), ("pb", 2), ("pb", 3)])
        mk.emit("S", lambda e: e.copy(selT[:, :], mk.pool[0:16, 0:TOK]),
                reads=[("pb", 0), ("pb", 1), ("pb", 2), ("pb", 3)], writes=[("selT",)])

        for lp in range(NLP):
            for c in range(NC4):
                cb = c % 2

                def h1mm(e, lp=lp, c=c, cb=cb):
                    last = None
                    for j in range(NJ):
                        last = e.matmul(mk.pb(cb),
                                        w1w[:, lp * 1024 + j * 128: lp * 1024 + j * 128 + 128],
                                        xln[:, j * TOK + c * 512: j * TOK + c * 512 + 512],
                                        start=(j == 0), stop=(j == NJ - 1))
                    return last
                mk.emit("T", h1mm, reads=[("w1w", lp)] + [xk(j, c) for j in range(NJ)],
                        writes=[("pb", cb)])
                mk.emit("S", (lambda f=f, lp=lp, cb=cb: lambda e: e.activation(
                    hp_[:, cb * 512:(cb + 1) * 512], mk.pb(cb), AF.Relu,
                    bias=b1s[:, f * NLP + lp: f * NLP + lp + 1]))(),
                    reads=[("pb", cb), ("c", "b1")], writes=[("hp", cb)])

                def mmm(e, lp=lp, c=c, cb=cb):
                    e.matmul(mk.pool[0:64, (2 + cb) * 512:(2 + cb) * 512 + 512],
                             selm[:, (2 * lp) * 64:(2 * lp) * 64 + 64],
                             selT[:, c * 512:(c + 1) * 512], start=True, stop=True)
                    return e.matmul(mk.pool[64:128, (2 + cb) * 512:(2 + cb) * 512 + 512],
                                    selm[:, (2 * lp + 1) * 64:(2 * lp + 1) * 64 + 64],
                                    selT[:, c * 512:(c + 1) * 512], start=True, stop=True)
                mk.emit("T", mmm, reads=[("selT",), ("c", "selm"), ("pb", 2 + cb)],
                        writes=[("pb", 2 + cb)])
                mk.emit("V", (lambda lp=lp, c=c, cb=cb: lambda e: e.tensor_mul(
                    mH[:, lp * TOK + c * 512: lp * TOK + c * 512 + 512],
                    hp_[:, cb * 512:(cb + 1) * 512], mk.pb(2 + cb)))(),
                    reads=[("hp", cb), ("pb", 2 + cb)], writes=[("mH", lp, c)])

        if f < 2:
            dst_out = qT_out if f == 0 else kT_out
            for ot in range(NJ):
                op = ot % 2
                for c in range(NC4):
                    cb = 4 + (c % 2)

                    def w2mm(e, f=f, ot=ot, c=c, cb=cb):
                        for lp in range(NLP):
                            e.matmul(mk.pb(cb),
                                     w2w[:, lp * 1024 + ot * 128: lp * 1024 + ot * 128 + 128],
                                     mH[:, lp * TOK + c * 512: lp * TOK + c * 512 + 512],
                                     start=(lp == 0), stop=False)
                        return e.matmul(mk.pb(cb), b2s[0:16, f * D + ot * 128: f * D + ot * 128 + 128],
                                        selT[:, c * 512:(c + 1) * 512], start=False, stop=True)
                    mk.emit("T", w2mm,
                            reads=[("mH", lp, c) for lp in range(NLP)] +
                                  [("w2w", lp) for lp in range(NLP)] +
                                  [("selT",), ("c", "b2"), ("pb", cb)],
                            writes=[("pb", cb)])
                    mk.emit("S", (lambda ot=ot, op=op, c=c, cb=cb: lambda e: e.copy(
                        qkbuf[:, op * TOK + c * 512: op * TOK + c * 512 + 512], mk.pb(cb)))(),
                        reads=[("pb", cb)], writes=[("qk", op, c)])
                mk.emit("Yq", (lambda dst_out=dst_out, ot=ot, op=op: lambda e: e.dma_start(
                    out=dst_out[ot * 128:(ot + 1) * 128, :],
                    in_=qkbuf[:, op * TOK:(op + 1) * TOK]))(),
                    reads=[("qk", op, c) for c in range(NC4)], writes=[("qkst", op)])
        else:
            for c in range(NC4):
                for ttl in range(4):
                    tt = c * 4 + ttl
                    for oc in range(2):
                        cb = 4 + ((ttl * 2 + oc) % 2)

                        def vmm(e, c=c, ttl=ttl, oc=oc, cb=cb):
                            for lp in range(NLP):
                                e.matmul(mk.pb(cb),
                                         mH[:, lp * TOK + c * 512 + ttl * 128: lp * TOK + c * 512 + ttl * 128 + 128],
                                         w2w[:, lp * 1024 + oc * 512: lp * 1024 + oc * 512 + 512],
                                         start=(lp == 0), stop=False)
                            return e.matmul(mk.pb(cb),
                                            selT[:, c * 512 + ttl * 128: c * 512 + ttl * 128 + 128],
                                            b2s[0:16, 2 * D + oc * 512: 2 * D + oc * 512 + 512],
                                            start=False, stop=True)
                        mk.emit("T", vmm,
                                reads=[("mH", lp, c) for lp in range(NLP)] +
                                      [("w2w", lp) for lp in range(NLP)] +
                                      [("selT",), ("c", "b2"), ("pb", cb)],
                                writes=[("pb", cb)])
                        mk.emit("S", (lambda tt=tt, oc=oc, cb=cb: lambda e: e.copy(
                            vtm[:, tt * 1024 + oc * 512: tt * 1024 + oc * 512 + 512], mk.pb(cb)))(),
                            reads=[("pb", cb)], writes=[("vt", tt, oc)])
        if f < 2:
            load_wf(f + 1)
    mk.emit("Yq", lambda e: e.dma_start(
        out=v_out.rearrange("(tt p) d -> p tt d", p=128),
        in_=vtm[:, :].rearrange("p (tt d) -> p tt d", d=1024)),
        reads=[("vt", tt, oc) for tt in range(NTT) for oc in range(2)], writes=[("vst",)])
    mk.wait_only("Y", [("vst",), ("qkst", 0), ("qkst", 1)])


def build_D1():
    mk = MK()
    io = {
        "xT": mk.dram("xT", [D, TOK], F32, kind="ExternalInput").ap(),
        "nwT3": mk.dram("nwT3", [3, D, 15], F32, kind="ExternalInput").ap(),
        "nb3": mk.dram("nb3", [15, 3], F32, kind="ExternalInput").ap(),
        "w1f": mk.dram("w1f", [3, NLP, D, 128], F32, kind="ExternalInput").ap(),
        "b1f": mk.dram("b1f", [3, 128, NLP], F32, kind="ExternalInput").ap(),
        "w2f": mk.dram("w2f", [3, NLP, 128, D], F32, kind="ExternalInput").ap(),
        "b2f": mk.dram("b2f", [3, 16, D], F32, kind="ExternalInput").ap(),
        "selm": mk.dram("selm", [16, 16 * 64], F32, kind="ExternalInput").ap(),
        "qT": mk.dram("qT", [D, TOK], BF16, kind="ExternalOutput").ap(),
        "kT": mk.dram("kT", [D, TOK], BF16, kind="ExternalOutput").ap(),
        "v": mk.dram("v", [TOK, D], BF16, kind="ExternalOutput").ap(),
    }
    eyef_in = mk.dram("eyef", [128, 128], F32, kind="ExternalInput").ap()
    mk.psum_pool()
    consts = emit_consts(mk)
    eyef = mk.sbuf("eyef_s", [128, 128], F32)
    mk.emit("Yq", lambda e: e.dma_start(out=eyef[:], in_=eyef_in[:, :]), writes=[("c", "eyef")])
    mk.init_arena()
    emit_phase_D1(mk, consts, eyef, io)
    return mk



def build_D2():
    mk = MK()
    xT_in = mk.dram("xT", [D, TOK], F32, kind="ExternalInput").ap()
    qT_in = mk.dram("qT", [D, TOK], BF16, kind="ExternalInput").ap()
    kTf_in = mk.dram("kTf", [D, S], BF16, kind="ExternalInput").ap()
    vf_in = mk.dram("vf", [S, D], BF16, kind="ExternalInput").ap()
    eyef_in = mk.dram("eyef", [128, 128], F32, kind="ExternalInput").ap()
    out_tm = mk.dram("xout", [TOK, D], F32, kind="ExternalOutput").ap()

    mk.psum_pool()
    consts = emit_consts(mk)
    eyef = mk.sbuf("eyef_s", [128, 128], F32)
    mk.emit("Yq", lambda e: e.dma_start(out=eyef[:], in_=eyef_in[:, :]), writes=[("c", "eyef")])
    qT = mk.sbuf("qT_s", [128, NJ * TOK], BF16)
    for j in range(NJ):
        mk.emit("Yq", (lambda j=j: lambda e: e.dma_start(
            out=qT[:, j * TOK:(j + 1) * TOK], in_=qT_in[j * 128:(j + 1) * 128, :]))(),
            writes=[("qT", j)])
    ftm = mk.sbuf("ftm", [128, NTT * 1024], BF16)
    denr = mk.sbuf("denr", [1, 2 * 256], F32)
    rcols = mk.sbuf("rcols", [128, 2 * 2], F32)
    onesc = mk.sbuf("onesc", [1, 1], F32)
    mk.emit("G", lambda e: e.memset(onesc[:], 1.0), writes=[("c", "o1")])
    ARN = mk.sbuf("arn", [128, 40960], BF16)
    ppan = ARN[:, 0:32768]
    kbb = ARN[:, 32768:36864]
    vbb = ARN[:, 36864:40960]

    SC = 1.0 / np.sqrt(1024.0)
    QCN = TOK // 256

    def p1_load(qc, ktp):
        par = (ktp // 2) % 2
        mk.emit("Yq", (lambda ktp=ktp, par=par: lambda e: e.dma_start(
            out=kbb[:, par * 2048:(par + 1) * 2048].rearrange("p (j i f) -> p j i f", j=NJ, i=2),
            in_=kTf_in[:, ktp * 128:(ktp + 2) * 128].rearrange("(j p) (i f) -> p j i f", p=128, i=2)))(),
            reads=[("kbb", par)], writes=[("kbb", par)])

    def p1_sc(qc, kt):
        pp = qc % 2
        par = (kt // 2) % 2
        i = kt % 2
        sb = 4 + (kt % 2)

        def scmm(e, qc=qc, par=par, i=i, sb=sb):
            last = None
            for j in range(NJ):
                last = e.matmul(mk.pool[:, sb * 512: sb * 512 + 256],
                                kbb[:, par * 2048 + j * 256 + i * 128: par * 2048 + j * 256 + i * 128 + 128],
                                qT[:, j * TOK + (qc // 2) * 512 + (qc % 2) * 256:
                                   j * TOK + (qc // 2) * 512 + (qc % 2) * 256 + 256],
                                start=(j == 0), stop=(j == NJ - 1))
            return last
        mk.emit("T", scmm, reads=[("kbb", par), ("pb2", sb)] + [("qT", j) for j in range(NJ)],
                writes=[("pb2", sb)])
        mk.emit("S", (lambda qc=qc, kt=kt, sb=sb, pp=pp: lambda e: e.activation(
            ppan[:, pp * 16384 + kt * 256: pp * 16384 + kt * 256 + 256],
            mk.pool[:, sb * 512: sb * 512 + 256], AF.Exp, scale=SC))(),
            reads=[("pb2", sb)], writes=[("pp", pp, kt)])
        mk.emit("T", (lambda kt=kt, pp=pp, qc=qc: lambda e: e.matmul(
            mk.pool[0:1, 6 * 512 + (qc % 2) * 256: 6 * 512 + (qc % 2) * 256 + 256],
            consts["onesb"][:], ppan[:, pp * 16384 + kt * 256: pp * 16384 + kt * 256 + 256],
            start=(kt == 0), stop=(kt == NKT - 1)))(),
            reads=[("pp", pp, kt), ("c", "onesb"), ("pbden", qc % 2)], writes=[("pbden", qc % 2)])

    def p1_fin(qc):
        qp = qc % 2
        mk.emit("V", (lambda qp=qp: lambda e: e.reciprocal(
            denr[0:1, qp * 256:(qp + 1) * 256],
            mk.pool[0:1, 6 * 512 + qp * 256: 6 * 512 + qp * 256 + 256]))(),
            reads=[("pbden", qp)], writes=[("denr", qp)])

        def rtr(e, qp=qp):
            last = None
            for qt in range(2):
                last = e.matmul(mk.pool[:, 7 * 512 + qp * 2 + qt: 7 * 512 + qp * 2 + qt + 1],
                                denr[0:1, qp * 256 + qt * 128: qp * 256 + (qt + 1) * 128],
                                onesc[0:1, 0:1], start=True, stop=True)
            return last
        mk.emit("T", rtr, reads=[("denr", qp), ("c", "o1"), ("pbr", qp)], writes=[("pbr", qp)])
        mk.emit("V", (lambda qp=qp: lambda e: e.tensor_copy(
            rcols[:, qp * 2:(qp + 1) * 2], mk.pool[:, 7 * 512 + qp * 2: 7 * 512 + qp * 2 + 2]))(),
            reads=[("pbr", qp)], writes=[("rcols", qp)])

    def p2_load(qc, ktp):
        par = (ktp // 2) % 2
        mk.emit("Yq", (lambda ktp=ktp, par=par: lambda e: e.dma_start(
            out=vbb[:, par * 2048:(par + 1) * 2048].rearrange("p (i f) -> p i f", i=2),
            in_=vf_in.rearrange("(kt p) d -> p kt d", p=128)[:, ktp:ktp + 2, :]))(),
            reads=[("vbb", par)], writes=[("vbb", par)])

    def p2_pv(qc, kt):
        pp = qc % 2
        par = (kt // 2) % 2
        i = kt % 2

        def pvmm(e, qc=qc, kt=kt, pp=pp, par=par, i=i):
            last = None
            for qt in range(2):
                for oc in range(2):
                    ob = qt * 2 + oc
                    last = e.matmul(
                        mk.pool[:, ob * 512: ob * 512 + 512],
                        ppan[:, pp * 16384 + kt * 256 + qt * 128: pp * 16384 + kt * 256 + qt * 128 + 128],
                        vbb[:, par * 2048 + i * 1024 + oc * 512: par * 2048 + i * 1024 + oc * 512 + 512],
                        start=(kt == 0), stop=(kt == NKT - 1))
            return last
        mk.emit("T", pvmm,
                reads=[("vbb", par), ("pp", pp, kt)] + [("pbo", b) for b in range(4)],
                writes=[("pbo", b) for b in range(4)])

    def p2_fin(qc):
        pp = qc % 2
        for qt in range(2):
            tt = qc * 2 + qt
            mk.emit("S", (lambda tt=tt, qt=qt, pp=pp: lambda e: e.activation(
                ftm[:, tt * 1024:(tt + 1) * 1024],
                mk.pool[:, qt * 1024: qt * 1024 + 1024],
                AF.Copy, scale=rcols[:, pp * 2 + qt: pp * 2 + qt + 1]))(),
                reads=[("pbo", qt * 2), ("pbo", qt * 2 + 1), ("rcols", pp)],
                writes=[("ftm", tt)])

    for qc in range(QCN + 1):
        for kt in range(NKT):
            if qc < QCN:
                if kt % 2 == 0:
                    p1_load(qc, kt)
                p1_sc(qc, kt)
            if qc >= 1:
                if kt % 2 == 0:
                    p2_load(qc - 1, kt)
                p2_pv(qc - 1, kt)
        if qc < QCN:
            p1_fin(qc)
        if qc >= 1:
            p2_fin(qc - 1)

    mk.barrier()
    sqt = ARN[:, 0:1024]
    lnf = ARN[:, 1024:3072].bitcast(F32)[:, 0:1024]
    svec = ARN[:, 3072:3200].bitcast(F32)[:, 0:48]
    rjb = ARN[:, 3200:5248].bitcast(F32)[:, 0:1024]
    outb = ARN[:, 5248:9344].bitcast(F32)[:, 0:2048]
    mvec = svec[:, 0:16]
    vvec = svec[:, 16:32]
    rvec = svec[:, 32:48]
    tag = "po"
    for tt in range(NTT):
        at = ftm[:, tt * 1024:(tt + 1) * 1024]
        mk.emit("V", (lambda at=at, tt=tt: lambda e: e.reduce_sum(
            mvec[:, tt:tt + 1], at, axis=AX))(),
            reads=[("ftm", tt)], writes=[(tag, "m", tt)])
        mk.emit("S", (lambda at=at: lambda e: e.activation(sqt[:, 0:1024], at, AF.Square))(),
            reads=[("ftm", tt)], writes=[(tag, "sq")])
        mk.emit("V", (lambda tt=tt: lambda e: e.reduce_sum(
            vvec[:, tt:tt + 1], sqt[:, 0:1024], axis=AX))(),
            reads=[(tag, "sq")], writes=[(tag, "v", tt)])
    mk.emit("V", lambda e: e.tensor_scalar_mul(mvec[:, :], mvec[:, :], 1.0 / D),
            reads=[(tag, "m", tt) for tt in range(NTT)], writes=[(tag, "mv")])
    mk.emit("V", lambda e: e.tensor_scalar_mul(vvec[:, :], vvec[:, :], 1.0 / D),
            reads=[(tag, "v", tt) for tt in range(NTT)], writes=[(tag, "vv")])
    mk.emit("V", lambda e: e.scalar_tensor_tensor(rvec[:, :], mvec[:, :], -1.0, mvec[:, :],
                                                  ALU.mult, ALU.mult),
            reads=[(tag, "mv")], writes=[(tag, "rv")])
    mk.emit("V", lambda e: e.tensor_add(vvec[:, :], vvec[:, :], rvec[:, :]),
            reads=[(tag, "vv"), (tag, "rv")], writes=[(tag, "vv")])
    mk.emit("S", lambda e: e.activation(vvec[:, :], vvec[:, :], AF.Ln, bias=consts["epscol"][:, 0:1]),
            reads=[(tag, "vv"), ("c", "epsc")], writes=[(tag, "vv")])
    mk.emit("S", lambda e: e.activation(rvec[:, :], vvec[:, :], AF.Exp, scale=-0.5),
            reads=[(tag, "vv")], writes=[(tag, "rv")])
    for tt in range(NTT):
        mk.emit("Yq", (lambda tt=tt: lambda e: e.dma_start(
            out=rjb[:, 0:1024].rearrange("p (j f) -> p j f", f=128),
            in_=xT_in[:, tt * 128:(tt + 1) * 128].rearrange("(j p) f -> p j f", p=128)))(),
            reads=[(tag, "rjb")], writes=[(tag, "rjbw")])

        def rtt(e, tt=tt):
            last = None
            for j in range(NJ):
                last = e.transpose(mk.pool[:, 2048 + j * 128: 2048 + j * 128 + 128],
                                   rjb[:, j * 128:(j + 1) * 128], eyef[:])
            return last
        mk.emit("T", rtt, reads=[(tag, "rjbw"), ("c", "eyef")],
                writes=[("pb", 4), ("pb", 5), ("pb", 6), ("pb", 7)])
        at = ftm[:, tt * 1024:(tt + 1) * 1024]
        mk.emit("V", (lambda at=at, tt=tt: lambda e: e.tensor_scalar(
            lnf[:, 0:1024], at, mvec[:, tt:tt + 1], rvec[:, tt:tt + 1],
            ALU.subtract, ALU.mult))(),
            reads=[("ftm", tt), (tag, "mv"), (tag, "rv")],
            writes=[(tag, "lnf")])
        mk.emit("V", (lambda tt=tt: lambda e: e.tensor_add(
            outb[:, 0:1024], lnf[:, 0:1024], mk.pool[:, 2048:3072]))(),
            reads=[(tag, "lnf"), ("pb", 4), ("pb", 5), ("pb", 6), ("pb", 7), (tag, "outb")],
            writes=[(tag, "outbw"), (tag, "rjb")])
        mk.emit("Yq", (lambda tt=tt: lambda e: e.dma_start(
            out=out_tm[tt * 128:(tt + 1) * 128, :], in_=outb[:, 0:1024]))(),
            reads=[(tag, "outbw")], writes=[(tag, "outb"), (tag, "st", tt)])
    mk.wait_only("Y", [(tag, "st", tt) for tt in range(NTT)])
    return mk



def emit_phase_A0(mk, consts, eyef, io):
    x_in = io["x"]
    vT_in, uTb_in = io["vT3"], io["uTb3"]
    qT_out, kTf_out, vf_out, xT_out = io["qTs"], io["kTfs"], io["vfs"], io["xTs"]

    vTs = mk.asbuf("vTs", [128, 3 * NJ * 32], BF16)
    uTbs = mk.asbuf("uTbs", [33, 3 * NJ * 128], BF16)
    mk.emit("Gq", lambda e: e.dma_start(out=vTs[:, :].rearrange("p (x j r) -> p x j r", x=3, j=NJ),
                                        in_=vT_in.rearrange("x j p r -> p x j r")),
            writes=[("c", "vTs")])
    mk.emit("Gq", lambda e: e.dma_start(out=uTbs[:, :].rearrange("p (x j f) -> p x j f", x=3, j=NJ),
                                        in_=uTb_in.rearrange("x j p f -> p x j f")),
            writes=[("c", "uTbs")])

    res = mk.asbuf("res_a0", [128, NJ * TOK], F32)
    xln = mk.asbuf("xln_a0", [128, NJ * TOK], BF16)
    ln_tmp = alloc_ln_tmp(mk)
    taug = mk.asbuf("taug", [33, 2 * 512], BF16)
    mk.emit("G", lambda e: e.memset(taug[:, :], 1.0), writes=[("taug", 0), ("taug", 1)])
    qbuf = mk.asbuf("qbuf_a0", [128, 2 * TOK], BF16)
    kbuf = mk.asbuf("kbuf_a0", [128, 2 * TOK], BF16)
    vbuf = mk.asbuf("vbuf_a0", [128, 2 * TOK], BF16)
    bufs = {0: qbuf, 1: kbuf, 2: vbuf}
    xtm0 = mk.asbuf("xtm_a0", [128, 2 * 1024], F32)

    for Q in range(4):
        rk = lambda j, c, Q=Q: ("res", Q, j, c)
        emit_loadx_fm(mk, x_in[Q * TOK:(Q + 1) * TOK, :], res, rk, eyef, "lx%d" % Q, xtm=xtm0)
        emit_ln_fm(mk, consts, ln_tmp, "lnA%d" % Q, res, rk,
                   dst_fn=lambda j, c: (xln[:, j * TOK + c * 512: j * TOK + c * 512 + 512],
                                        ("xlnA", Q, j, c)))
        if Q == 0:
            for j in range(NJ):
                mk.emit("Yq", (lambda j=j, rk=rk: lambda e: e.dma_start(
                    out=xT_out[j * 128:(j + 1) * 128, :], in_=res[:, j * TOK:(j + 1) * TOK]))(),
                    reads=[rk(j, c) for c in range(NC4)], writes=[("resst", j)])
        for h in range(NJ):
            hpar = h % 2
            projs = (0, 1, 2) if Q == 0 else (1, 2)
            for p in projs:
                for c in range(NC4):
                    cpar = c % 2
                    xkeys = [("xlnA", Q, h, c)]
                    mk.emit("T", (lambda h=h, p=p, c=c, cpar=cpar: lambda e: e.matmul(
                        mk.pool[0:32, cpar * 512:cpar * 512 + 512],
                        vTs[:, (p * NJ + h) * 32:(p * NJ + h) * 32 + 32],
                        xln[:, h * TOK + c * 512: h * TOK + c * 512 + 512],
                        start=True, stop=True))(),
                        reads=xkeys + [("c", "vTs")], writes=[("pb", cpar)])
                    mk.emit("V", (lambda cpar=cpar: lambda e: e.tensor_copy(
                        taug[0:32, cpar * 512:cpar * 512 + 512],
                        mk.pool[0:32, cpar * 512:cpar * 512 + 512]))(),
                        reads=[("pb", cpar)], writes=[("taug", cpar)])
                    if p < 2:
                        mk.emit("T", (lambda h=h, p=p, cpar=cpar: lambda e: e.matmul(
                            mk.pb(2 + cpar),
                            uTbs[0:33, (p * NJ + h) * 128:(p * NJ + h) * 128 + 128],
                            taug[0:33, cpar * 512:cpar * 512 + 512],
                            start=True, stop=True))(),
                            reads=[("taug", cpar), ("c", "uTbs")], writes=[("pb", 2 + cpar)])
                        mk.emit("S", (lambda p=p, c=c, cpar=cpar, hpar=hpar: lambda e: e.copy(
                            bufs[p][:, hpar * TOK + c * 512: hpar * TOK + c * 512 + 512],
                            mk.pb(2 + cpar)))(),
                            reads=[("pb", 2 + cpar)], writes=[("ob", p, hpar, c)])
                    else:
                        def vmm(e, h=h, cpar=cpar):
                            last = None
                            for i in range(4):
                                last = e.matmul(
                                    mk.pool[:, 2048 + cpar * 1024 + i * 128: 2048 + cpar * 1024 + i * 128 + 128],
                                    taug[0:33, cpar * 512 + i * 128: cpar * 512 + i * 128 + 128],
                                    uTbs[0:33, (2 * NJ + h) * 128:(2 * NJ + h) * 128 + 128],
                                    start=True, stop=True)
                            return last
                        mk.emit("T", vmm, reads=[("taug", cpar), ("c", "uTbs")],
                                writes=[("pb", 4 + 2 * cpar), ("pb", 5 + 2 * cpar)])

                        def vevac(e, c=c, cpar=cpar, hpar=hpar):
                            last = None
                            for i in range(4):
                                tt = c * 4 + i
                                last = e.copy(
                                    bufs[2][:, hpar * TOK + tt * 128: hpar * TOK + tt * 128 + 128],
                                    mk.pool[:, 2048 + cpar * 1024 + i * 128: 2048 + cpar * 1024 + i * 128 + 128])
                            return last
                        mk.emit("S", vevac, reads=[("pb", 4 + 2 * cpar), ("pb", 5 + 2 * cpar)],
                                writes=[("ob", 2, hpar, c)])
            for p in projs:
                if p == 0:
                    mk.emit("Yq", (lambda h=h, hpar=hpar: lambda e: e.dma_start(
                        out=qT_out[h * 128:(h + 1) * 128, :],
                        in_=bufs[0][:, hpar * TOK:(hpar + 1) * TOK]))(),
                        reads=[("ob", 0, hpar, c) for c in range(NC4)], writes=[("obst", 0, Q, h)])
                elif p == 1:
                    mk.emit("Yq", (lambda h=h, hpar=hpar, Q=Q: lambda e: e.dma_start(
                        out=kTf_out[h * 128:(h + 1) * 128, Q * TOK:(Q + 1) * TOK],
                        in_=bufs[1][:, hpar * TOK:(hpar + 1) * TOK]))(),
                        reads=[("ob", 1, hpar, c) for c in range(NC4)], writes=[("obst", 1, Q, h)])
                else:
                    mk.emit("Yq", (lambda h=h, hpar=hpar, Q=Q: lambda e: e.dma_start(
                        out=vf_out[Q * TOK:(Q + 1) * TOK, :].rearrange("(tt p) d -> p tt d", p=128)[:, :, h * 128:(h + 1) * 128],
                        in_=bufs[2][:, hpar * TOK:(hpar + 1) * TOK].rearrange("p (tt f) -> p tt f", f=128)))(),
                        reads=[("ob", 2, hpar, c) for c in range(NC4)], writes=[("obst", 2, Q, h)])
    mk.wait_only("Y", [("obst", p, Q, h) for p in range(3) for Q in range(4) for h in range(NJ)
                       if (p != 0 or Q == 0)] + [("resst", j) for j in range(NJ)])



def build_M1():
    mk = MK()
    io_0 = {
        "x": mk.dram("x", [S, D], F32, kind="ExternalInput").ap(),
        "vT3": mk.dram("vT3", [3, NJ, 128, 32], F32, kind="ExternalInput").ap(),
        "uTb3": mk.dram("uTb3", [3, NJ, 33, 128], F32, kind="ExternalInput").ap(),
        "qTs": mk.dram("qTs", [D, TOK], BF16).ap(),
        "kTfs": mk.dram("kTfs", [D, S], BF16).ap(),
        "vfs": mk.dram("vfs", [S, D], BF16).ap(),
        "xTs": mk.dram("xTs", [D, TOK], F32).ap(),
    }
    io_a = {
        "xT": io_0["xTs"],
        "qT": io_0["qTs"],
        "kTf": io_0["kTfs"],
        "vf": io_0["vfs"],
        "x1_out": mk.dram("x1s", [D, TOK], F32).ap(),
    }
    x2s = mk.dram("x2s", [D, TOK], F32).ap()
    io_b = {
        "xT": io_a["x1_out"],
        "gwT": mk.dram("gwT", [D, NE], F32, kind="ExternalInput").ap(),
        "gb": mk.dram("gb", [NE, 1], F32, kind="ExternalInput").ap(),
        "w1": mk.dram("w1", [NE, D, 2048], F32, kind="ExternalInput").ap(),
        "b1r": mk.dram("b1r", [128, NE * NHC], F32, kind="ExternalInput").ap(),
        "w2": mk.dram("w2", [NE, 2048, D], F32, kind="ExternalInput").ap(),
        "b2": mk.dram("b2", [NE, D], F32, kind="ExternalInput").ap(),
        "x2_out": x2s,
        "x2_out2": mk.dram("x2T", [D, TOK], F32, kind="ExternalOutput").ap(),
    }
    io_c = {
        "xT": x2s,
        "wqkv": mk.dram("wqkv", [D, 3 * D], F32, kind="ExternalInput").ap(),
        "cos2": mk.dram("cos2", [128, TOK], F32, kind="ExternalInput").ap(),
        "sin2": mk.dram("sin2", [128, TOK], F32, kind="ExternalInput").ap(),
        "prot": mk.dram("prot", [128, 128], F32, kind="ExternalInput").ap(),
        "qT": mk.dram("qT1", [D, TOK], BF16, kind="ExternalOutput").ap(),
        "kT": mk.dram("kT1", [D, TOK], BF16, kind="ExternalOutput").ap(),
        "v": mk.dram("v1", [TOK, D], BF16, kind="ExternalOutput").ap(),
    }
    eyef_in = mk.dram("eyef", [128, 128], F32, kind="ExternalInput").ap()
    mk.psum_pool()
    consts = emit_consts(mk)
    eyef = mk.sbuf("eyef_s", [128, 128], F32)
    mk.emit("Yq", lambda e: e.dma_start(out=eyef[:], in_=eyef_in[:, :]), writes=[("c", "eyef")])
    mk.init_arena()
    emit_phase_A0(mk, consts, eyef, io_0)
    mk.arena_reset()
    emit_phase_A2(mk, consts, io_a)
    mk.arena_reset()
    emit_phase_B(mk, consts, eyef, io_b)
    mk.arena_reset()
    emit_phase_C1(mk, consts, eyef, io_c)
    return mk




def build_M2():
    mk = MK()
    x3s = mk.dram("x3s", [D, TOK], F32).ap()
    io_c = {
        "xT": mk.dram("xT", [D, TOK], F32, kind="ExternalInput").ap(),
        "qT": mk.dram("qT", [D, TOK], BF16, kind="ExternalInput").ap(),
        "kTf": mk.dram("kTf", [D, S], BF16, kind="ExternalInput").ap(),
        "vf": mk.dram("vf", [S, D], BF16, kind="ExternalInput").ap(),
        "wout": mk.dram("wout", [D, D], F32, kind="ExternalInput").ap(),
        "x3_out": x3s,
        "x3_out2": mk.dram("x3T", [D, TOK], F32, kind="ExternalOutput").ap(),
    }
    io_d = {
        "xT": x3s,
        "nwT3": mk.dram("nwT3", [3, D, 15], F32, kind="ExternalInput").ap(),
        "nb3": mk.dram("nb3", [15, 3], F32, kind="ExternalInput").ap(),
        "w1f": mk.dram("w1f", [3, NLP, D, 128], F32, kind="ExternalInput").ap(),
        "b1f": mk.dram("b1f", [3, 128, NLP], F32, kind="ExternalInput").ap(),
        "w2f": mk.dram("w2f", [3, NLP, 128, D], F32, kind="ExternalInput").ap(),
        "b2f": mk.dram("b2f", [3, 16, D], F32, kind="ExternalInput").ap(),
        "selm": mk.dram("selm", [16, 16 * 64], F32, kind="ExternalInput").ap(),
        "qT": mk.dram("qT1", [D, TOK], BF16, kind="ExternalOutput").ap(),
        "kT": mk.dram("kT1", [D, TOK], BF16, kind="ExternalOutput").ap(),
        "v": mk.dram("v1", [TOK, D], BF16, kind="ExternalOutput").ap(),
    }
    eyef_in = mk.dram("eyef", [128, 128], F32, kind="ExternalInput").ap()
    mk.psum_pool()
    consts = emit_consts(mk)
    eyef = mk.sbuf("eyef_s", [128, 128], F32)
    mk.emit("Yq", lambda e: e.dma_start(out=eyef[:], in_=eyef_in[:, :]), writes=[("c", "eyef")])
    mk.init_arena()
    emit_phase_C2(mk, consts, eyef, io_c)
    mk.arena_reset()
    emit_phase_D1(mk, consts, eyef, io_d)
    return mk



_NC_CACHE = {}


def _get_nc(name, builder):
    if name not in _NC_CACHE:
        _NC_CACHE[name] = builder().finalize()
    return _NC_CACHE[name]


def phaseA(inp):
    x = np.asarray(inp["L0_attn_x"], np.float32)
    eye = np.eye(128, dtype=np.float32)
    vT3 = np.stack([np.asarray(inp["L0_attn_V" + p], np.float32).transpose(0, 2, 1) for p in "qkv"])
    uTb3 = np.stack([
        np.concatenate([np.asarray(inp["L0_attn_U" + p], np.float32).transpose(0, 2, 1),
                        np.asarray(inp["L0_attn_b" + p], np.float32)[:, None, :]], axis=1)
        for p in "qkv"])
    nc1 = _get_nc("A1", build_A1)
    in_maps = []
    for c in range(NCORES):
        b, q0 = c // 4, (c % 4) * TOK
        in_maps.append({"x": np.ascontiguousarray(x[b, q0:q0 + TOK]),
                        "vT3": vT3, "uTb3": uTb3, "eyef": eye})
    r1 = run_bass_kernel_spmd(nc1, in_maps, core_ids=list(range(NCORES))).results
    kTf = [np.ascontiguousarray(np.concatenate([r1[4 * b + i]["kT"] for i in range(4)], axis=1)) for b in range(B)]
    vf = [np.ascontiguousarray(np.concatenate([r1[4 * b + i]["v"] for i in range(4)], axis=0)) for b in range(B)]
    nc2 = _get_nc("A2", build_A2)
    in_maps2 = []
    for c in range(NCORES):
        b = c // 4
        in_maps2.append({"xT": r1[c]["xT"], "qT": r1[c]["qT"], "kTf": kTf[b], "vf": vf[b]})
    r2 = run_bass_kernel_spmd(nc2, in_maps2, core_ids=list(range(NCORES))).results
    return [r2[c]["x1T"] for c in range(NCORES)]


def phaseB(inp, x1):
    eye = np.eye(128, dtype=np.float32)
    gwT = np.ascontiguousarray(np.asarray(inp["L0_ffn_gate_w"], np.float32).T)
    gb = np.asarray(inp["L0_ffn_gate_b"], np.float32).reshape(NE, 1)
    w1 = np.ascontiguousarray(np.asarray(inp["L0_ffn_w1"], np.float32))
    b1r = np.ascontiguousarray(np.asarray(inp["L0_ffn_b1"], np.float32)
                               .reshape(NE, NHC, 128).transpose(2, 0, 1).reshape(128, NE * NHC))
    w2 = np.ascontiguousarray(np.asarray(inp["L0_ffn_w2"], np.float32))
    b2 = np.ascontiguousarray(np.asarray(inp["L0_ffn_b2"], np.float32))
    ncb = _get_nc("B", build_B_full)
    in_maps = [{"xT": x1[c], "gwT": gwT, "gb": gb, "w1": w1, "b1r": b1r,
                "w2": w2, "b2": b2, "eyef": eye} for c in range(NCORES)]
    r = run_bass_kernel_spmd(ncb, in_maps, core_ids=list(range(NCORES))).results
    return [r[c]["x2T"] for c in range(NCORES)]


def _rope_tables():
    inv = 1.0 / (10000.0 ** (np.arange(0, 64, 2, dtype=np.float32) / 64.0))
    t = np.arange(S, dtype=np.float32)
    fr = np.outer(t, inv)
    emb = np.concatenate([fr, fr], axis=-1)
    return np.cos(emb).T.astype(np.float32), np.sin(emb).T.astype(np.float32)


def _prot():
    P = np.zeros((128, 128), np.float32)
    for base in (0, 64):
        for i in range(32):
            P[base + 32 + i, base + i] = -1.0
            P[base + i, base + 32 + i] = 1.0
    return P


def phaseC(inp, x2):
    eye = np.eye(128, dtype=np.float32)
    wqkv = np.ascontiguousarray(np.asarray(inp["L1_attn_Wqkv"], np.float32))
    wout = np.ascontiguousarray(np.asarray(inp["L1_attn_Wout"], np.float32))
    cosT, sinT = _rope_tables()
    prot = _prot()
    nc1 = _get_nc("C1", build_C1)
    in_maps = []
    for c in range(NCORES):
        q0 = (c % 4) * TOK
        cos2 = np.ascontiguousarray(np.concatenate([cosT[:, q0:q0 + TOK]] * 2, axis=0))
        sin2 = np.ascontiguousarray(np.concatenate([sinT[:, q0:q0 + TOK]] * 2, axis=0))
        in_maps.append({"xT": x2[c], "wqkv": wqkv, "cos2": cos2, "sin2": sin2,
                        "prot": prot, "eyef": eye})
    r1 = run_bass_kernel_spmd(nc1, in_maps, core_ids=list(range(NCORES))).results
    kTf = [np.ascontiguousarray(np.concatenate([r1[4 * b + i]["kT"] for i in range(4)], axis=1)) for b in range(B)]
    vf = [np.ascontiguousarray(np.concatenate([r1[4 * b + i]["v"] for i in range(4)], axis=0)) for b in range(B)]
    nc2 = _get_nc("C2", build_C2)
    in_maps2 = []
    for c in range(NCORES):
        b = c // 4
        in_maps2.append({"xT": x2[c], "qT": r1[c]["qT"], "kTf": kTf[b], "vf": vf[b],
                         "wout": wout, "eyef": eye})
    r2 = run_bass_kernel_spmd(nc2, in_maps2, core_ids=list(range(NCORES))).results
    return [r2[c]["x3T"] for c in range(NCORES)]


def phaseD(inp, x3):
    eye = np.eye(128, dtype=np.float32)
    f32 = np.float32
    nwT3 = np.stack([np.ascontiguousarray(np.asarray(inp["L1_ffn_%s_node_w" % p], f32).T) for p in "qkv"])
    nb3 = np.stack([np.asarray(inp["L1_ffn_%s_node_b" % p], f32) for p in "qkv"], axis=1)
    w1f = np.stack([np.asarray(inp["L1_ffn_%s_w1" % p], f32).reshape(NLP, 2, D, 64)
                    .transpose(0, 2, 1, 3).reshape(NLP, D, 128) for p in "qkv"])
    b1f = np.stack([np.asarray(inp["L1_ffn_%s_b1" % p], f32).reshape(NLP, 128).T for p in "qkv"])
    w2f = np.stack([np.asarray(inp["L1_ffn_%s_w2" % p], f32).reshape(NLP, 128, D) for p in "qkv"])
    b2f = np.stack([np.asarray(inp["L1_ffn_%s_b2" % p], f32) for p in "qkv"])
    selm = np.zeros((16, 16 * 64), f32)
    for l in range(16):
        selm[l, l * 64:(l + 1) * 64] = 1.0
    nc1 = _get_nc("D1", build_D1)
    in_maps = [{"xT": x3[c], "nwT3": nwT3, "nb3": np.ascontiguousarray(nb3),
                "w1f": np.ascontiguousarray(w1f), "b1f": np.ascontiguousarray(b1f),
                "w2f": np.ascontiguousarray(w2f), "b2f": np.ascontiguousarray(b2f),
                "selm": selm, "eyef": eye} for c in range(NCORES)]
    r1 = run_bass_kernel_spmd(nc1, in_maps, core_ids=list(range(NCORES))).results
    kTf = [np.ascontiguousarray(np.concatenate([r1[4 * b + i]["kT"] for i in range(4)], axis=1)) for b in range(B)]
    vf = [np.ascontiguousarray(np.concatenate([r1[4 * b + i]["v"] for i in range(4)], axis=0)) for b in range(B)]
    nc2 = _get_nc("D2", build_D2)
    in_maps2 = []
    for c in range(NCORES):
        b = c // 4
        in_maps2.append({"xT": x3[c], "qT": r1[c]["qT"], "kTf": kTf[b], "vf": vf[b], "eyef": eye})
    r2 = run_bass_kernel_spmd(nc2, in_maps2, core_ids=list(range(NCORES))).results
    return [r2[c]["xout"] for c in range(NCORES)]


_ALL_INPUT_NAMES = (
    "L0_attn_x", "L0_attn_Uq", "L0_attn_Vq", "L0_attn_bq", "L0_attn_Uk", "L0_attn_Vk",
    "L0_attn_bk", "L0_attn_Uv", "L0_attn_Vv", "L0_attn_bv", "L0_ffn_gate_w",
    "L0_ffn_gate_b", "L0_ffn_w1", "L0_ffn_b1", "L0_ffn_w2", "L0_ffn_b2",
    "L1_attn_Wqkv", "L1_attn_Wout",
    "L1_ffn_q_node_w", "L1_ffn_q_node_b", "L1_ffn_q_w1", "L1_ffn_q_b1", "L1_ffn_q_w2",
    "L1_ffn_q_b2", "L1_ffn_k_node_w", "L1_ffn_k_node_b", "L1_ffn_k_w1", "L1_ffn_k_b1",
    "L1_ffn_k_w2", "L1_ffn_k_b2", "L1_ffn_v_node_w", "L1_ffn_v_node_b", "L1_ffn_v_w1",
    "L1_ffn_v_b1", "L1_ffn_v_w2", "L1_ffn_v_b2",
)


def run_M1(inp):
    x = np.asarray(inp["L0_attn_x"], np.float32)
    vT3 = np.stack([np.asarray(inp["L0_attn_V" + p], np.float32).transpose(0, 2, 1) for p in "qkv"])
    uTb3 = np.stack([
        np.concatenate([np.asarray(inp["L0_attn_U" + p], np.float32).transpose(0, 2, 1),
                        np.asarray(inp["L0_attn_b" + p], np.float32)[:, None, :]], axis=1)
        for p in "qkv"])
    eye = np.eye(128, dtype=np.float32)
    gwT = np.ascontiguousarray(np.asarray(inp["L0_ffn_gate_w"], np.float32).T)
    gb = np.asarray(inp["L0_ffn_gate_b"], np.float32).reshape(NE, 1)
    w1 = np.ascontiguousarray(np.asarray(inp["L0_ffn_w1"], np.float32))
    b1r = np.ascontiguousarray(np.asarray(inp["L0_ffn_b1"], np.float32)
                               .reshape(NE, NHC, 128).transpose(2, 0, 1).reshape(128, NE * NHC))
    w2 = np.ascontiguousarray(np.asarray(inp["L0_ffn_w2"], np.float32))
    b2 = np.ascontiguousarray(np.asarray(inp["L0_ffn_b2"], np.float32))
    wqkv = np.ascontiguousarray(np.asarray(inp["L1_attn_Wqkv"], np.float32))
    cosT, sinT = _rope_tables()
    prot = _prot()
    ncm = _get_nc("M1", build_M1)
    in_maps = []
    for c in range(NCORES):
        b, q0 = c // 4, (c % 4) * TOK
        cos2 = np.ascontiguousarray(np.concatenate([cosT[:, q0:q0 + TOK]] * 2, axis=0))
        sin2 = np.ascontiguousarray(np.concatenate([sinT[:, q0:q0 + TOK]] * 2, axis=0))
        x_rot = np.ascontiguousarray(np.concatenate([x[b, q0:], x[b, :q0]], axis=0))
        in_maps.append({"x": x_rot, "vT3": vT3, "uTb3": uTb3,
                        "gwT": gwT, "gb": gb, "w1": w1, "b1r": b1r, "w2": w2, "b2": b2,
                        "wqkv": wqkv, "cos2": cos2, "sin2": sin2, "prot": prot, "eyef": eye})
    return run_bass_kernel_spmd(ncm, in_maps, core_ids=list(range(NCORES))).results


def kernel_merged(inputs):
    inp = inputs
    rm = run_M1(inp)
    x2 = [rm[c]["x2T"] for c in range(NCORES)]
    r1b = [{"qT": rm[c]["qT1"], "kT": rm[c]["kT1"], "v": rm[c]["v1"]} for c in range(NCORES)]
    rm2 = run_M2(inp, x2, r1b)
    x3 = [rm2[c]["x3T"] for c in range(NCORES)]
    r2b = [{"qT": rm2[c]["qT1"], "kT": rm2[c]["kT1"], "v": rm2[c]["v1"]} for c in range(NCORES)]
    x4 = run_D2(inp, x3, r2b)
    out = np.zeros((B, S, D), np.float32)
    for c in range(NCORES):
        b, q0 = c // 4, (c % 4) * TOK
        out[b, q0:q0 + TOK] = x4[c]
    return out


def run_M2(inp, x2, r1):
    eye = np.eye(128, dtype=np.float32)
    f32 = np.float32
    wout = np.ascontiguousarray(np.asarray(inp["L1_attn_Wout"], f32))
    kTf = [np.ascontiguousarray(np.concatenate([r1[4 * b + i]["kT"] for i in range(4)], axis=1)) for b in range(B)]
    vf = [np.ascontiguousarray(np.concatenate([r1[4 * b + i]["v"] for i in range(4)], axis=0)) for b in range(B)]
    nwT3 = np.stack([np.ascontiguousarray(np.asarray(inp["L1_ffn_%s_node_w" % p], f32).T) for p in "qkv"])
    nb3 = np.ascontiguousarray(np.stack([np.asarray(inp["L1_ffn_%s_node_b" % p], f32) for p in "qkv"], axis=1))
    w1f = np.ascontiguousarray(np.stack([np.asarray(inp["L1_ffn_%s_w1" % p], f32).reshape(NLP, 2, D, 64)
                                         .transpose(0, 2, 1, 3).reshape(NLP, D, 128) for p in "qkv"]))
    b1f = np.ascontiguousarray(np.stack([np.asarray(inp["L1_ffn_%s_b1" % p], f32).reshape(NLP, 128).T for p in "qkv"]))
    w2f = np.ascontiguousarray(np.stack([np.asarray(inp["L1_ffn_%s_w2" % p], f32).reshape(NLP, 128, D) for p in "qkv"]))
    b2f = np.ascontiguousarray(np.stack([np.asarray(inp["L1_ffn_%s_b2" % p], f32) for p in "qkv"]))
    selm = np.zeros((16, 16 * 64), f32)
    for l in range(16):
        selm[l, l * 64:(l + 1) * 64] = 1.0
    ncm = _get_nc("M2", build_M2)
    in_maps = []
    for c in range(NCORES):
        b = c // 4
        in_maps.append({"xT": x2[c], "qT": r1[c]["qT"], "kTf": kTf[b], "vf": vf[b], "wout": wout,
                        "nwT3": nwT3, "nb3": nb3, "w1f": w1f, "b1f": b1f, "w2f": w2f, "b2f": b2f,
                        "selm": selm, "eyef": eye})
    return run_bass_kernel_spmd(ncm, in_maps, core_ids=list(range(NCORES))).results


def run_D2(inp, x3, r1):
    eye = np.eye(128, dtype=np.float32)
    kTf = [np.ascontiguousarray(np.concatenate([r1[4 * b + i]["kT"] for i in range(4)], axis=1)) for b in range(B)]
    vf = [np.ascontiguousarray(np.concatenate([r1[4 * b + i]["v"] for i in range(4)], axis=0)) for b in range(B)]
    nc2 = _get_nc("D2", build_D2)
    in_maps2 = []
    for c in range(NCORES):
        b = c // 4
        in_maps2.append({"xT": x3[c], "qT": r1[c]["qT"], "kTf": kTf[b], "vf": vf[b], "eyef": eye})
    r2 = run_bass_kernel_spmd(nc2, in_maps2, core_ids=list(range(NCORES))).results
    return [r2[c]["xout"] for c in range(NCORES)]


def phaseC2_only(inp, x2, r1):
    eye = np.eye(128, dtype=np.float32)
    wout = np.ascontiguousarray(np.asarray(inp["L1_attn_Wout"], np.float32))
    kTf = [np.ascontiguousarray(np.concatenate([r1[4 * b + i]["kT"] for i in range(4)], axis=1)) for b in range(B)]
    vf = [np.ascontiguousarray(np.concatenate([r1[4 * b + i]["v"] for i in range(4)], axis=0)) for b in range(B)]
    nc2 = _get_nc("C2", build_C2)
    in_maps2 = []
    for c in range(NCORES):
        b = c // 4
        in_maps2.append({"xT": x2[c], "qT": r1[c]["qT"], "kTf": kTf[b], "vf": vf[b],
                         "wout": wout, "eyef": eye})
    r2 = run_bass_kernel_spmd(nc2, in_maps2, core_ids=list(range(NCORES))).results
    return [r2[c]["x3T"] for c in range(NCORES)]


def kernel(**inputs):
    assert all(n in inputs for n in _ALL_INPUT_NAMES)
    return kernel_merged(inputs)


def kernel_old(**inputs):
    assert all(n in inputs for n in _ALL_INPUT_NAMES)
    x1 = phaseA(inputs)
    x2 = phaseB(inputs, x1)
    x3 = phaseC(inputs, x2)
    x4 = phaseD(inputs, x3)
    out = np.zeros((B, S, D), np.float32)
    for c in range(NCORES):
        b, q0 = c // 4, (c % 4) * TOK
        out[b, q0:q0 + TOK] = x4[c]
    return out


# version 16
# speedup vs baseline: 1.1056x; 1.0141x over previous
import numpy as np
import ml_dtypes
from contextlib import ExitStack
import concourse.bass as bass
import concourse.mybir as mybir
from concourse.bass_utils import run_bass_kernel_spmd

F32 = mybir.dt.float32
BF16 = mybir.dt.bfloat16
AF = mybir.ActivationFunctionType
ALU = mybir.AluOpType
AX = mybir.AxisListType.X
BF = ml_dtypes.bfloat16

NCORES = 8
B = 2
S = 8192
D = 1024
TOK = 2048
NJ = 8
NTT = TOK // 128
NC4 = TOK // 512
NKT = S // 128
LN_EPS = 1e-5

CH2ENG = {"T": "tensor", "V": "vector", "S": "scalar", "G": "gpsimd",
          "Y": "sync", "Gq": "gpsimd", "Yq": "sync", "Sq": "scalar"}
DMACH = ("Gq", "Yq", "Sq")


class MK:

    def __init__(self):
        self.nc = bass.Bass("TRN2", target_bir_lowering=False)
        self.es = ExitStack()
        self.ops = {e: [] for e in ("tensor", "vector", "scalar", "gpsimd", "sync")}
        self.cnt = {c: 0 for c in CH2ENG}
        self.sems = {}
        self.regions = {}
        self.marks = {}

    def dram(self, name, shape, dtype, kind="Internal"):
        return self.nc.dram_tensor(name, shape, dtype, kind=kind)

    def sbuf(self, name, shape, dtype):
        return self.es.enter_context(self.nc.sbuf_tensor(name, shape, dtype))

    def psum_pool(self):
        self.pool = self.es.enter_context(self.nc.psum_tensor("pool", [128, 4096], F32))
        return self.pool

    def init_arena(self, cols_bf16=104448):
        self.arena = self.sbuf("ARENA", [128, cols_bf16], BF16)
        self.acols = cols_bf16
        self.aoff = 0

    def asbuf(self, name, shape, dtype):
        cols = shape[1] * (2 if dtype == F32 else 1)
        cols = (cols + 31) & ~31
        assert self.aoff + cols <= self.acols, (name, self.aoff, cols)
        ap = self.arena[:, self.aoff:self.aoff + cols]
        self.aoff += cols
        if dtype == F32:
            ap = ap.bitcast(F32)
        return ap[0:shape[0], 0:shape[1]]

    def arena_reset(self):
        self.barrier()
        self.aoff = 0

    def pb(self, bank, width=512, rows=128, off=0):
        return self.pool[0:rows, bank * 512 + off: bank * 512 + off + width]

    def _sem(self, ch):
        if ch not in self.sems:
            self.sems[ch] = self.es.enter_context(self.nc.semaphore("sem_" + ch))
        return self.sems[ch]

    def emit(self, ch, fn, reads=(), writes=(), ndma=1):
        eng = CH2ENG[ch]
        is_dma = ch in DMACH
        waits = {}

        def dep(d):
            if d is None:
                return
            dch, dval = d
            if dch == "T" and ch == "T":
                return
            if self.marks.get((eng, dch), 0) >= dval:
                return
            waits[dch] = max(waits.get(dch, 0), dval)

        for k in reads:
            r = self.regions.get(k)
            if r is not None:
                dep(r[0])
        for k in writes:
            r = self.regions.get(k)
            if r is not None:
                dep(r[0])
                for rd in r[1]:
                    dep(rd)
        for dch, dval in waits.items():
            self.marks[(eng, dch)] = dval
        inc_total = (16 * ndma) if is_dma else (0 if fn is None else 1)
        self.cnt[ch] += inc_total
        val = self.cnt[ch]
        for k in writes:
            self.regions[k] = [(ch, val), []]
        for k in reads:
            self.regions.setdefault(k, [None, []])[1].append((ch, val))
        self._sem(ch)
        self.ops[eng].append((ch, fn, sorted(waits.items()), is_dma))
        return (ch, val)

    def wait_only(self, ch, reads):
        self.emit(ch, None, reads=reads, ndma=0)

    def barrier(self):
        snap = [(ch, v) for ch, v in self.cnt.items() if v > 0]
        for eng in self.ops:
            if not self.ops[eng]:
                continue
            chs = [(ch, v) for (ch, v) in snap
                   if CH2ENG[ch] != eng and self.marks.get((eng, ch), 0) < v]
            for ch, v in chs:
                self.marks[(eng, ch)] = v
                self._sem(ch)
            mych = [c for c, e in CH2ENG.items() if e == eng and c not in DMACH]
            mych = mych[0] if mych else "Y"
            self.ops[eng].append((mych, None, sorted(chs), False))

    def finalize(self):
        nc = self.nc
        with nc.Block() as blk:
            for eng in self.ops:
                if not self.ops[eng]:
                    continue

                def mk_body(eng):
                    oplist = self.ops[eng]

                    def body(e):
                        for (ch, fn, waits, is_dma) in oplist:
                            for dch, dval in waits:
                                e.wait_ge(self.sems[dch], dval)
                            if fn is None:
                                continue
                            r = fn(e)
                            if is_dma:
                                insts = r if isinstance(r, (list, tuple)) else [r]
                                for i_ in insts:
                                    i_.then_inc(self.sems[ch], 16)
                            else:
                                r.then_inc(self.sems[ch], 1)
                    return body
                getattr(blk, eng)(mk_body(eng))
        return self.nc


PB = [("pb", i) for i in range(8)]


def emit_consts(mk):
    c = {}
    c["onesb"] = mk.sbuf("c_onesb", [128, 1], BF16)
    c["ones1f"] = mk.sbuf("c_ones1f", [1, 128], F32)
    c["epsrow"] = mk.sbuf("c_epsrow", [1, 1], F32)
    c["epscol"] = mk.sbuf("c_epscol", [128, 1], F32)
    c["ones1b"] = mk.sbuf("c_ones1b", [1, 128], BF16)
    c["ones8f"] = mk.sbuf("c_ones8f", [8, 1], F32)
    mk.emit("G", lambda e: e.memset(c["ones1b"][:], 1.0), writes=[("c", "ones1b")])
    mk.emit("G", lambda e: e.memset(c["ones8f"][:], 1.0), writes=[("c", "ones8f")])
    mk.emit("G", lambda e: e.memset(c["onesb"][:], 1.0), writes=[("c", "onesb")])
    mk.emit("G", lambda e: e.memset(c["ones1f"][:], 1.0), writes=[("c", "ones1f")])
    mk.emit("G", lambda e: e.memset(c["epsrow"][:], LN_EPS), writes=[("c", "eps")])
    mk.emit("G", lambda e: e.memset(c["epscol"][:], LN_EPS), writes=[("c", "epsc")])
    return c


def alloc_ln_tmp(mk, t_tok=TOK, with_xb=True):
    return {
        **({"xb": mk.asbuf("ln_xb", [128, 2 * t_tok], BF16)} if with_xb else {}),
        "xq": mk.asbuf("ln_xq", [128, 2 * t_tok], BF16),
        "arow": mk.asbuf("ln_arow", [1, t_tok], F32),
        "brow": mk.asbuf("ln_brow", [1, t_tok], F32),
        "trow": mk.asbuf("ln_trow", [1, t_tok], F32),
        "mrow": mk.asbuf("ln_mrow", [1, t_tok], F32),
        "absb": mk.asbuf("ln_absb", [128, 1024], F32),
        "t1a": mk.asbuf("ln_t1a", [128, 512], F32),
        "t1b": mk.asbuf("ln_t1b", [128, 512], F32),
    }


def emit_ln_fm(mk, consts, tmp, tag, src, src_key, dst_fn, res=None, res_key=None, src_bf16=False):
    for j in range(NJ):
        par = j % 2
        if not src_bf16:
            mk.emit("S", (lambda j=j, par=par: lambda e: e.activation(
                tmp["xb"][:, par * TOK:(par + 1) * TOK],
                src[:, j * TOK:(j + 1) * TOK], AF.Copy))(),
                reads=[src_key(j, c) for c in range(NC4)], writes=[(tag, "xb", par)])
        mk.emit("S", (lambda j=j, par=par: lambda e: e.activation(
            tmp["xq"][:, par * TOK:(par + 1) * TOK],
            src[:, j * TOK:(j + 1) * TOK], AF.Square))(),
            reads=[src_key(j, c) for c in range(NC4)], writes=[(tag, "xq", par)])

        def mmstats(e, j=j, par=par):
            last = None
            for c in range(NC4):
                xbap = (src[:, j * TOK + c * 512: j * TOK + c * 512 + 512] if src_bf16
                        else tmp["xb"][:, par * TOK + c * 512: par * TOK + c * 512 + 512])
                last = e.matmul(mk.pool[0:1, c * 512:c * 512 + 512], consts["onesb"][:],
                                xbap, start=(j == 0), stop=(j == NJ - 1))
                last = e.matmul(mk.pool[0:1, 2048 + c * 512:2048 + c * 512 + 512], consts["onesb"][:],
                                tmp["xq"][:, par * TOK + c * 512: par * TOK + c * 512 + 512],
                                start=(j == 0), stop=(j == NJ - 1))
            return last
        rd = [(tag, "xq", par), ("c", "onesb")]
        rd += [src_key(j, c) for c in range(NC4)] if src_bf16 else [(tag, "xb", par)]
        mk.emit("T", mmstats, reads=rd, writes=PB)
    mk.emit("V", lambda e: e.tensor_scalar_mul(tmp["mrow"][0:1, :], mk.pool[0:1, 0:TOK], 1.0 / D),
            reads=[("pb", i) for i in range(4)], writes=[(tag, "mrow")])
    mk.emit("V", lambda e: e.tensor_scalar_mul(tmp["trow"][0:1, :], mk.pool[0:1, 2048:2048 + TOK], 1.0 / D),
            reads=[("pb", i) for i in range(4, 8)], writes=[(tag, "trow")])
    mk.emit("V", lambda e: e.scalar_tensor_tensor(tmp["arow"][0:1, :], tmp["mrow"][0:1, :], -1.0,
                                                  tmp["mrow"][0:1, :], ALU.mult, ALU.mult),
            reads=[(tag, "mrow")], writes=[(tag, "arow")])
    mk.emit("V", lambda e: e.tensor_add(tmp["trow"][0:1, :], tmp["trow"][0:1, :], tmp["arow"][0:1, :]),
            reads=[(tag, "trow"), (tag, "arow")], writes=[(tag, "trow")])
    mk.emit("S", lambda e: e.activation(tmp["trow"][0:1, :], tmp["trow"][0:1, :], AF.Ln, bias=consts["epsrow"][0:1, 0:1]),
            reads=[(tag, "trow"), ("c", "eps")], writes=[(tag, "trow")])
    mk.emit("S", lambda e: e.activation(tmp["arow"][0:1, :], tmp["trow"][0:1, :], AF.Exp, scale=-0.5),
            reads=[(tag, "trow")], writes=[(tag, "arow")])
    mk.emit("V", lambda e: e.scalar_tensor_tensor(tmp["brow"][0:1, :], tmp["mrow"][0:1, :], -1.0,
                                                  tmp["arow"][0:1, :], ALU.mult, ALU.mult),
            reads=[(tag, "mrow"), (tag, "arow")], writes=[(tag, "brow")])
    for c in range(NC4):
        def bcast(e, c=c):
            e.matmul(mk.pb(0), consts["ones1f"][:], tmp["arow"][0:1, c * 512:c * 512 + 512],
                     start=True, stop=True)
            return e.matmul(mk.pb(1), consts["ones1f"][:], tmp["brow"][0:1, c * 512:c * 512 + 512],
                            start=True, stop=True)
        mk.emit("T", bcast, reads=[(tag, "arow"), (tag, "brow"), ("c", "ones1f")],
                writes=[("pb", 0), ("pb", 1)])
        mk.emit("V", lambda e: e.tensor_copy(tmp["absb"][:, 0:1024], mk.pool[:, 0:1024]),
                reads=[("pb", 0), ("pb", 1)], writes=[(tag, "absb")])
        for j in range(NJ):
            ch = "V" if j % 2 == 0 else "G"
            tk = (tag, "t1a") if ch == "V" else (tag, "t1b")
            tb = tmp["t1a"] if ch == "V" else tmp["t1b"]
            off = j * TOK + c * 512
            mk.emit(ch, (lambda off=off, tb=tb: lambda e: e.tensor_mul(
                tb[:, 0:512], src[:, off:off + 512], tmp["absb"][:, 0:512]))(),
                reads=[src_key(j, c), (tag, "absb")], writes=[tk])
            if res is None:
                dap, dkey = dst_fn(j, c)
                mk.emit(ch, (lambda dap=dap, tb=tb: lambda e: e.tensor_add(
                    dap, tb[:, 0:512], tmp["absb"][:, 512:1024]))(),
                    reads=[tk, (tag, "absb")], writes=[dkey])
            else:
                mk.emit(ch, (lambda tb=tb: lambda e: e.tensor_add(
                    tb[:, 0:512], tb[:, 0:512], tmp["absb"][:, 512:1024]))(),
                    reads=[tk, (tag, "absb")], writes=[tk])
                mk.emit(ch, (lambda off=off, tb=tb: lambda e: e.tensor_add(
                    res[:, off:off + 512], res[:, off:off + 512], tb[:, 0:512]))(),
                    reads=[tk, res_key(j, c)], writes=[res_key(j, c)])


def emit_loadx_fm(mk, x_dram, res, res_key, eyef, tag, xtm=None):
    if xtm is None:
        xtm = mk.sbuf(tag + "_xtm", [128, 2 * 1024], F32)
    for tt in range(NTT):
        par = tt % 2
        mk.emit("Yq", (lambda tt=tt, par=par: lambda e: e.dma_start(
            out=xtm[:, par * 1024:(par + 1) * 1024],
            in_=x_dram[tt * 128:(tt + 1) * 128, :]))(),
            reads=[(tag, "xtm", par)], writes=[(tag, "xtmw", par)])

        def trans(e, par=par):
            last = None
            for j in range(NJ):
                last = e.transpose(mk.pool[:, 2048 + par * 1024 + j * 128: 2048 + par * 1024 + j * 128 + 128],
                                   xtm[:, par * 1024 + j * 128: par * 1024 + j * 128 + 128], eyef[:])
            return last
        mk.emit("T", trans, reads=[(tag, "xtmw", par), ("c", "eyef")],
                writes=[("pb", 4 + 2 * par), ("pb", 5 + 2 * par)])

        def evac(e, tt=tt, par=par):
            last = None
            for j in range(NJ):
                srcap = mk.pool[:, 2048 + par * 1024 + j * 128: 2048 + par * 1024 + j * 128 + 128]
                dstap = res[:, j * TOK + tt * 128: j * TOK + tt * 128 + 128]
                last = e.tensor_copy(dstap, srcap) if tt % 2 == 0 else e.copy(dstap, srcap)
            return last
        ch = "V" if tt % 2 == 0 else "S"
        mk.emit(ch, evac, reads=[("pb", 4 + 2 * par), ("pb", 5 + 2 * par)],
                writes=[res_key(j, tt // 4) for j in range(NJ)] + [(tag, "xtm", par)])



def build_A1():
    mk = MK()
    x_in = mk.dram("x", [TOK, D], F32, kind="ExternalInput").ap()
    vT_in = mk.dram("vT3", [3, NJ, 128, 32], F32, kind="ExternalInput").ap()
    uTb_in = mk.dram("uTb3", [3, NJ, 33, 128], F32, kind="ExternalInput").ap()
    eyef_in = mk.dram("eyef", [128, 128], F32, kind="ExternalInput").ap()
    qT_out = mk.dram("qT", [D, TOK], BF16, kind="ExternalOutput").ap()
    kT_out = mk.dram("kT", [D, TOK], BF16, kind="ExternalOutput").ap()
    v_out = mk.dram("v", [TOK, D], BF16, kind="ExternalOutput").ap()
    xT_out = mk.dram("xT", [D, TOK], F32, kind="ExternalOutput").ap()

    mk.psum_pool()
    consts = emit_consts(mk)
    mk.init_arena(32768)
    eyef = mk.sbuf("eyef_s", [128, 128], F32)
    mk.emit("Yq", lambda e: e.dma_start(out=eyef[:], in_=eyef_in[:, :]), writes=[("c", "eyef")])
    vTs = mk.sbuf("vTs", [128, 3 * NJ * 32], BF16)
    uTbs = mk.sbuf("uTbs", [33, 3 * NJ * 128], BF16)
    mk.emit("Gq", lambda e: e.dma_start(out=vTs[:, :].rearrange("p (x j r) -> p x j r", x=3, j=NJ),
                                        in_=vT_in.rearrange("x j p r -> p x j r")),
            writes=[("c", "vTs")])
    mk.emit("Gq", lambda e: e.dma_start(out=uTbs[:, :].rearrange("p (x j f) -> p x j f", x=3, j=NJ),
                                        in_=uTb_in.rearrange("x j p f -> p x j f")),
            writes=[("c", "uTbs")])

    res = mk.sbuf("res", [128, NJ * TOK], F32)
    xln = mk.sbuf("xln", [128, NJ * TOK], BF16)
    rk = lambda j, c: ("res", j, c)
    emit_loadx_fm(mk, x_in, res, rk, eyef, "lx")

    ln_tmp = alloc_ln_tmp(mk)
    emit_ln_fm(mk, consts, ln_tmp, "ln1", res, rk,
               dst_fn=lambda j, c: (xln[:, j * TOK + c * 512: j * TOK + c * 512 + 512], ("xln", j, c)))

    for j in range(NJ):
        mk.emit("Yq", (lambda j=j: lambda e: e.dma_start(
            out=xT_out[j * 128:(j + 1) * 128, :], in_=res[:, j * TOK:(j + 1) * TOK]))(),
            reads=[rk(j, c) for c in range(NC4)], writes=[("resst", j)])

    taug = mk.sbuf("taug", [33, 2 * 512], BF16)
    mk.emit("G", lambda e: e.memset(taug[:], 1.0), writes=[("taug", 0), ("taug", 1)])
    qbuf = mk.sbuf("qbuf", [128, 2 * TOK], BF16)
    kbuf = mk.sbuf("kbuf", [128, 2 * TOK], BF16)
    vbuf = mk.sbuf("vbuf", [128, 2 * TOK], BF16)
    bufs = {0: qbuf, 1: kbuf, 2: vbuf}
    outs = {0: qT_out, 1: kT_out}
    for h in range(NJ):
        hpar = h % 2
        for p in range(3):
            for c in range(NC4):
                cpar = c % 2
                xkeys = [("xln", h, c)]
                mk.emit("T", (lambda h=h, p=p, c=c, cpar=cpar: lambda e: e.matmul(
                    mk.pool[0:32, cpar * 512:cpar * 512 + 512],
                    vTs[:, (p * NJ + h) * 32:(p * NJ + h) * 32 + 32],
                    xln[:, h * TOK + c * 512: h * TOK + c * 512 + 512],
                    start=True, stop=True))(),
                    reads=xkeys + [("c", "vTs")], writes=[("pb", cpar)])
                mk.emit("V", (lambda cpar=cpar: lambda e: e.tensor_copy(
                    taug[0:32, cpar * 512:cpar * 512 + 512],
                    mk.pool[0:32, cpar * 512:cpar * 512 + 512]))(),
                    reads=[("pb", cpar)], writes=[("taug", cpar)])
                if p < 2:
                    mk.emit("T", (lambda h=h, p=p, cpar=cpar: lambda e: e.matmul(
                        mk.pb(2 + cpar),
                        uTbs[0:33, (p * NJ + h) * 128:(p * NJ + h) * 128 + 128],
                        taug[0:33, cpar * 512:cpar * 512 + 512],
                        start=True, stop=True))(),
                        reads=[("taug", cpar), ("c", "uTbs")], writes=[("pb", 2 + cpar)])
                    mk.emit("S", (lambda p=p, c=c, cpar=cpar, hpar=hpar: lambda e: e.copy(
                        bufs[p][:, hpar * TOK + c * 512: hpar * TOK + c * 512 + 512],
                        mk.pb(2 + cpar)))(),
                        reads=[("pb", 2 + cpar)], writes=[("ob", p, hpar, c)])
                else:
                    def vmm(e, h=h, cpar=cpar):
                        last = None
                        for i in range(4):
                            last = e.matmul(
                                mk.pool[:, 2048 + cpar * 1024 + i * 128: 2048 + cpar * 1024 + i * 128 + 128],
                                taug[0:33, cpar * 512 + i * 128: cpar * 512 + i * 128 + 128],
                                uTbs[0:33, (2 * NJ + h) * 128:(2 * NJ + h) * 128 + 128],
                                start=True, stop=True)
                        return last
                    mk.emit("T", vmm, reads=[("taug", cpar), ("c", "uTbs")],
                            writes=[("pb", 4 + 2 * cpar), ("pb", 5 + 2 * cpar)])

                    def vevac(e, c=c, cpar=cpar, hpar=hpar):
                        last = None
                        for i in range(4):
                            tt = c * 4 + i
                            last = e.copy(
                                bufs[2][:, hpar * TOK + tt * 128: hpar * TOK + tt * 128 + 128],
                                mk.pool[:, 2048 + cpar * 1024 + i * 128: 2048 + cpar * 1024 + i * 128 + 128])
                        return last
                    mk.emit("S", vevac, reads=[("pb", 4 + 2 * cpar), ("pb", 5 + 2 * cpar)],
                            writes=[("ob", 2, hpar, c)])
        for p in range(3):
            if p < 2:
                mk.emit("Yq", (lambda p=p, h=h, hpar=hpar: lambda e: e.dma_start(
                    out=outs[p][h * 128:(h + 1) * 128, :],
                    in_=bufs[p][:, hpar * TOK:(hpar + 1) * TOK]))(),
                    reads=[("ob", p, hpar, c) for c in range(NC4)], writes=[("obst", p, h)])
            else:
                mk.emit("Yq", (lambda h=h, hpar=hpar: lambda e: e.dma_start(
                    out=v_out.rearrange("(tt p) d -> p tt d", p=128)[:, :, h * 128:(h + 1) * 128],
                    in_=bufs[2][:, hpar * TOK:(hpar + 1) * TOK].rearrange("p (tt f) -> p tt f", f=128)))(),
                    reads=[("ob", 2, hpar, c) for c in range(NC4)], writes=[("obst", 2, h)])
    mk.wait_only("Y", [("obst", p, h) for p in range(3) for h in range(NJ)] +
                 [("resst", j) for j in range(NJ)])
    return mk



def emit_attention_l0(mk, consts, qT_src, kTf_in, vf_in, fsb, nheads=NJ, scale=1.0 / np.sqrt(128.0)):
    kh = mk.asbuf("kh", [128, S], BF16)
    vh = mk.asbuf("vh", [128, NKT * 130], BF16)
    qh = mk.asbuf("qh", [128, TOK], BF16)
    pT = mk.asbuf("pT", [128, 3 * 1024], BF16)
    denr = mk.asbuf("denr", [1, 512], F32)
    bsb = mk.asbuf("bsb", [128, 4 * 512], F32)
    mk.emit("G", lambda e: e.memset(vh[:, :], 1.0), writes=[("vhones",)])
    for h in range(nheads):
        mk.emit("Yq", (lambda h=h: lambda e: e.dma_start(
            out=kh[:], in_=kTf_in[h * 128:(h + 1) * 128, :]))(),
            reads=[("kh",)], writes=[("kh",)])
        for hh in range(2):
            mk.emit("Yq", (lambda h=h, hh=hh: lambda e: e.dma_start(
                out=vh[:, :].rearrange("p (kt g) -> p kt g", g=130)[:, :, hh * 65:hh * 65 + 64],
                in_=vf_in.rearrange("(kt p) d -> p kt d", p=128)[:, :, h * 128 + hh * 64: h * 128 + hh * 64 + 64]))(),
                reads=[("vh",), ("vhones",)], writes=[("vh",)])
        mk.emit("Yq", (lambda h=h: lambda e: e.dma_start(
            out=qh[:], in_=qT_src[h * 128:(h + 1) * 128, :]))(),
            reads=[("qh",)], writes=[("qh",)])
        for qc in range(NC4):
            qpar = qc % 2
            dbank = 4 + qpar
            obank = 6 + qpar
            def emit_scexp(ktp, qc=qc):
                spar = (ktp // 2) % 2
                sb0 = spar * 2
                ppar = (ktp // 2) % 3

                def scmm(e, qc=qc, ktp=ktp, sb0=sb0):
                    last = None
                    for i in range(2):
                        kt = ktp + i
                        last = e.matmul(mk.pb(sb0 + i),
                                        kh[:, kt * 128: kt * 128 + 128],
                                        qh[:, qc * 512: qc * 512 + 512],
                                        start=True, stop=True)
                    return last
                mk.emit("T", scmm, reads=[("kh",), ("qh",), ("pb", sb0), ("pb", sb0 + 1)],
                        writes=[("pb", sb0), ("pb", sb0 + 1)])

                def expop(e, sb0=sb0, ppar=ppar):
                    return e.activation(pT[:, ppar * 1024:(ppar + 1) * 1024],
                                        mk.pool[:, sb0 * 512:sb0 * 512 + 1024],
                                        AF.Exp, scale=scale)
                mk.emit("S", expop, reads=[("pb", sb0), ("pb", sb0 + 1)], writes=[("pT", ppar)])

            def emit_pv(ktp, qpar=qpar):
                ppar = (ktp // 2) % 3
                ob0 = 4 + qpar
                ob1 = 6 + qpar

                def pv(e, ktp=ktp, ppar=ppar, ob0=ob0, ob1=ob1):
                    last = None
                    for i in range(2):
                        kt = ktp + i
                        pslice = pT[:, ppar * 1024 + i * 512: ppar * 1024 + i * 512 + 512]
                        for hh, ob in ((0, ob0), (1, ob1)):
                            last = e.matmul(mk.pool[0:65, ob * 512: ob * 512 + 512],
                                            vh[:, kt * 130 + hh * 65: kt * 130 + hh * 65 + 65],
                                            pslice, start=(kt == 0), stop=(kt == NKT - 1))
                    return last
                mk.emit("T", pv, reads=[("pT", ppar), ("vh",), ("pb", ob0), ("pb", ob1)],
                        writes=[("pb", ob0), ("pb", ob1)])

            for ktp in range(0, NKT + 2, 2):
                if ktp < NKT:
                    emit_scexp(ktp)
                if ktp >= 2:
                    emit_pv(ktp - 2)
            ob0 = 4 + qpar
            ob1 = 6 + qpar
            mk.emit("V", (lambda ob0=ob0: lambda e: e.reciprocal(
                denr[0:1, 0:512], mk.pool[64:65, ob0 * 512: ob0 * 512 + 512]))(),
                reads=[("pb", ob0)], writes=[("denr",)])
            sb0 = 0 if qpar == 0 else 2

            def dbc(e, sb0=sb0):
                return e.matmul(mk.pool[0:64, sb0 * 512: sb0 * 512 + 512],
                                consts["ones1f"][0:1, 0:64],
                                denr[0:1, 0:512], start=True, stop=True)
            mk.emit("T", dbc, reads=[("denr",), ("c", "ones1f"), ("pb", sb0), ("pb", sb0 + 1)],
                    writes=[("pb", sb0)])
            s0 = qpar * 512
            mk.emit("V", (lambda sb0=sb0, s0=s0: lambda e: e.tensor_copy(
                bsb[0:64, s0:s0 + 512], mk.pool[0:64, sb0 * 512: sb0 * 512 + 512]))(),
                reads=[("pb", sb0)], writes=[("bsb", qpar)])
            mk.emit("V", (lambda h=h, qc=qc, ob0=ob0, s0=s0: lambda e: e.tensor_mul(
                fsb[:, h * TOK + qc * 512: h * TOK + qc * 512 + 512][0:64, :],
                mk.pool[0:64, ob0 * 512: ob0 * 512 + 512],
                bsb[0:64, s0:s0 + 512]))(),
                reads=[("bsb", qpar), ("pb", ob0)], writes=[("f", h, qc)])
            mk.emit("V", (lambda h=h, qc=qc, ob1=ob1, s0=s0: lambda e: e.tensor_mul(
                fsb[:, h * TOK + qc * 512: h * TOK + qc * 512 + 512][64:128, :],
                mk.pool[0:64, ob1 * 512: ob1 * 512 + 512],
                bsb[0:64, s0:s0 + 512]))(),
                reads=[("bsb", qpar), ("pb", ob1), ("f", h, qc)], writes=[("f", h, qc)])


def emit_phase_A2(mk, consts, io):
    xT_in, qT_in, kTf_in, vf_in, x1_out = io["xT"], io["qT"], io["kTf"], io["vf"], io["x1_out"]
    res = mk.asbuf("res", [128, NJ * TOK], F32)
    fsb = mk.asbuf("fsb", [128, NJ * TOK], BF16)
    rk = lambda j, c: ("res", j, c)
    for j in range(NJ):
        mk.emit("Yq", (lambda j=j: lambda e: e.dma_start(
            out=res[:, j * TOK:(j + 1) * TOK], in_=xT_in[j * 128:(j + 1) * 128, :]))(),
            writes=[rk(j, c) for c in range(NC4)])

    emit_attention_l0(mk, consts, qT_in, kTf_in, vf_in, fsb)

    ln_tmp = alloc_ln_tmp(mk, with_xb=False)
    fkey = lambda j, c: ("f", j, c)
    emit_ln_fm(mk, consts, ln_tmp, "ln2", fsb, fkey, dst_fn=None, res=res, res_key=rk, src_bf16=True)
    for j in range(NJ):
        mk.emit("Yq", (lambda j=j: lambda e: e.dma_start(
            out=x1_out[j * 128:(j + 1) * 128, :], in_=res[:, j * TOK:(j + 1) * TOK]))(),
            reads=[rk(j, c) for c in range(NC4)], writes=[("x1st", j)])
    mk.wait_only("Y", [("x1st", j) for j in range(NJ)])


def build_A2():
    mk = MK()
    io = {
        "xT": mk.dram("xT", [D, TOK], F32, kind="ExternalInput").ap(),
        "qT": mk.dram("qT", [D, TOK], BF16, kind="ExternalInput").ap(),
        "kTf": mk.dram("kTf", [D, S], BF16, kind="ExternalInput").ap(),
        "vf": mk.dram("vf", [S, D], BF16, kind="ExternalInput").ap(),
        "x1_out": mk.dram("x1T", [D, TOK], F32, kind="ExternalOutput").ap(),
    }
    mk.psum_pool()
    consts = emit_consts(mk)
    mk.init_arena()
    emit_phase_A2(mk, consts, io)
    return mk




def emit_ln_fm_stream(mk, consts, tag, src_dram, xln, bufs):
    for j in range(NJ):
        par = j % 2
        mk.emit("Yq", (lambda j=j: lambda e: e.dma_start(
            out=bufs["resb"], in_=src_dram[j * 128:(j + 1) * 128, :]))(),
            reads=[(tag, "resb")], writes=[(tag, "resbw")])
        mk.emit("S", (lambda par=par: lambda e: e.activation(
            bufs["xb"][:, par * TOK:(par + 1) * TOK], bufs["resb"], AF.Copy))(),
            reads=[(tag, "resbw")], writes=[(tag, "xb", par)])
        mk.emit("S", (lambda par=par: lambda e: e.activation(
            bufs["xq"][:, par * TOK:(par + 1) * TOK], bufs["resb"], AF.Square))(),
            reads=[(tag, "resbw")], writes=[(tag, "xq", par), (tag, "resb")])

        def mmstats(e, j=j, par=par):
            last = None
            for c in range(NC4):
                last = e.matmul(mk.pool[0:1, c * 512:c * 512 + 512], consts["onesb"][:],
                                bufs["xb"][:, par * TOK + c * 512: par * TOK + c * 512 + 512],
                                start=(j == 0), stop=(j == NJ - 1))
                last = e.matmul(mk.pool[0:1, 2048 + c * 512:2048 + c * 512 + 512], consts["onesb"][:],
                                bufs["xq"][:, par * TOK + c * 512: par * TOK + c * 512 + 512],
                                start=(j == 0), stop=(j == NJ - 1))
            return last
        mk.emit("T", mmstats, reads=[(tag, "xb", par), (tag, "xq", par), ("c", "onesb")],
                writes=PB)
    tmp = bufs
    prow = mk.pool[0:1, 0:TOK]
    mk.emit("V", lambda e: e.tensor_scalar_mul(tmp["brow"], mk.pool[0:1, 0:TOK], 1.0 / D),
            reads=[("pb", i) for i in range(4)], writes=[(tag, "brow")])
    mk.emit("V", lambda e: e.tensor_scalar_mul(tmp["arow"], mk.pool[0:1, 2048:2048 + TOK], 1.0 / D),
            reads=[("pb", i) for i in range(4, 8)], writes=[(tag, "arow")])
    mk.emit("V", lambda e: e.tensor_mul(prow, tmp["brow"], tmp["brow"]),
            reads=[(tag, "brow")] + [("pb", i) for i in range(4)],
            writes=[("pb", 0), ("pb", 1), ("pb", 2), ("pb", 3)])
    mk.emit("V", lambda e: e.tensor_sub(tmp["arow"], tmp["arow"], prow),
            reads=[(tag, "arow")] + [("pb", i) for i in range(4)], writes=[(tag, "arow")])
    mk.emit("S", lambda e: e.activation(tmp["arow"], tmp["arow"], AF.Ln, bias=consts["epsrow"][0:1, 0:1]),
            reads=[(tag, "arow"), ("c", "eps")], writes=[(tag, "arow")])
    mk.emit("S", lambda e: e.activation(tmp["arow"], tmp["arow"], AF.Exp, scale=-0.5),
            reads=[(tag, "arow")], writes=[(tag, "arow")])
    mk.emit("V", lambda e: e.scalar_tensor_tensor(tmp["brow"], tmp["brow"], -1.0,
                                                  tmp["arow"], ALU.mult, ALU.mult),
            reads=[(tag, "brow"), (tag, "arow")], writes=[(tag, "brow")])
    mk.barrier()
    for c in range(NC4):
        def bcast(e, c=c):
            e.matmul(mk.pb(0), consts["ones1f"][:], tmp["arow"][0:1, c * 512:c * 512 + 512],
                     start=True, stop=True)
            return e.matmul(mk.pb(1), consts["ones1f"][:], tmp["brow"][0:1, c * 512:c * 512 + 512],
                            start=True, stop=True)
        mk.emit("T", bcast, reads=[(tag, "arow"), (tag, "brow"), ("c", "ones1f")],
                writes=[("pb", 0), ("pb", 1)])
        mk.emit("V", (lambda c=c: lambda e: e.tensor_copy(
            bufs["absf"][:, c * 512:c * 512 + 512], mk.pb(0)))(),
            reads=[("pb", 0)], writes=[(tag, "absf", c)])
        mk.emit("V", (lambda c=c: lambda e: e.tensor_copy(
            bufs["absf"][:, TOK + c * 512:TOK + c * 512 + 512], mk.pb(1)))(),
            reads=[("pb", 1)], writes=[(tag, "absf", c)])
    for j in range(NJ):
        mk.emit("Yq", (lambda j=j: lambda e: e.dma_start(
            out=bufs["resb"], in_=src_dram[j * 128:(j + 1) * 128, :]))(),
            reads=[(tag, "resb")], writes=[(tag, "resbw")])
        for c in range(NC4):
            ch = "V" if c % 2 == 0 else "G"
            tk = (tag, "t1a") if ch == "V" else (tag, "t1b")
            tb = bufs["t1a"] if ch == "V" else bufs["t1b"]
            mk.emit(ch, (lambda c=c, tb=tb: lambda e: e.tensor_mul(
                tb, bufs["resb"][:, c * 512:c * 512 + 512],
                bufs["absf"][:, c * 512:c * 512 + 512]))(),
                reads=[(tag, "resbw"), (tag, "absf", c)], writes=[tk])
            mk.emit(ch, (lambda j=j, c=c, tb=tb: lambda e: e.tensor_add(
                xln[:, j * TOK + c * 512: j * TOK + c * 512 + 512], tb,
                bufs["absf"][:, TOK + c * 512:TOK + c * 512 + 512]))(),
                reads=[tk, (tag, "absf", c)], writes=[(tag, "xln", j, c)])
    return



NE = 8
NHC = 16


def emit_phase_B(mk, consts, eyef, io, debug_stop=None):
    xT_in, x2_out = io["xT"], io["x2_out"]
    gwT_in, gb_in, w1_in, b1_in, w2_in, b2_in = (io["gwT"], io["gb"], io["w1"],
                                                 io["b1r"], io["w2"], io["b2"])
    x2_out2 = io.get("x2_out2")
    gwT = mk.asbuf("gwT_s", [128, NJ * NE], BF16)
    mk.emit("Gq", lambda e: e.dma_start(out=gwT[:, :].rearrange("p (j e) -> p j e", j=NJ),
                                        in_=gwT_in.rearrange("(j p) e -> p j e", p=128)),
            writes=[("c", "gwT")])
    gbs = mk.asbuf("gb_s", [NE, 1], F32)
    mk.emit("Yq", lambda e: e.dma_start(out=gbs[:], in_=gb_in[:, :]), writes=[("c", "gb")])
    b1s = mk.asbuf("b1_s", [128, NE * NHC], F32)
    mk.emit("Yq", lambda e: e.dma_start(out=b1s[:, :], in_=b1_in[:, :]),
            writes=[("c", "b1")])
    b2s = mk.asbuf("b2_s", [1, NE * D], BF16)
    mk.emit("Gq", lambda e: e.dma_start(out=b2s[:], in_=b2_in.rearrange("e o -> (e o)")[None, :]),
            writes=[("c", "b2")])
    zer = mk.asbuf("zer_s", [128, 512], BF16)
    mk.emit("G", lambda e: e.memset(zer[:], 0.0), writes=[("c", "zer")])

    xln = mk.asbuf("xln", [128, NJ * TOK], BF16)
    acc = mk.asbuf("acc", [128, NTT * 1024], BF16)
    gtm = mk.asbuf("gtm", [128, NTT * NE], F32)
    tmp2 = mk.asbuf("tmp2", [128, 2 * 512], BF16)
    ARN = mk.asbuf("arn", [128, 40960], BF16)

    lnb = {
        "xb": ARN[:, 16384:20480],
        "xq": ARN[:, 20480:24576],
        "absf": ARN[:, 16384:24576].bitcast(F32),
        "arow": ARN[:, 24576:28672].bitcast(F32)[0:1, 0:TOK],
        "brow": ARN[:, 28672:32768].bitcast(F32)[0:1, 0:TOK],
        "t1a": ARN[:, 32768:34816].bitcast(F32)[:, 0:512],
        "t1b": ARN[:, 32768:34816].bitcast(F32)[:, 512:1024],
        "resb": ARN[:, 34816:38912].bitcast(F32),
    }
    w1s = ARN[:, 0:16384]
    w2s = ARN[:, 16384:32768]
    htc = ARN[:, 32768:40960]

    def load_w1(e_, first=False):
        for hc in range(NHC):
            mk.emit("Gq", (lambda e_=e_, hc=hc: lambda e: e.dma_start(
                out=w1s[:, hc * 1024:(hc + 1) * 1024].rearrange("p (j f) -> p j f", f=128),
                in_=w1_in[e_, :, hc * 128:(hc + 1) * 128].rearrange("(j p) f -> p j f", p=128)))(),
                reads=[("w1s", hc)], writes=[("w1s", hc)])

    def load_w2(e_):
        for hc in range(NHC):
            mk.emit("Gq", (lambda e_=e_, hc=hc: lambda e: e.dma_start(
                out=w2s[:, hc * 1024:(hc + 1) * 1024],
                in_=w2_in[e_, hc * 128:(hc + 1) * 128, :]))(),
                reads=[("w2s", hc)], writes=[("w2s", hc)])

    load_w1(0, first=True)
    emit_ln_fm_stream(mk, consts, "lnb", xT_in, xln, lnb)
    xk = lambda j, c: ("lnb", "xln", j, c)
    if debug_stop == "ln":
        return

    logT = ARN[:, 16384:20480].bitcast(F32)[0:8, 0:TOK]
    expT = ARN[:, 20480:24576].bitcast(F32)[0:8, 0:TOK]
    prbT = ARN[:, 24576:28672].bitcast(F32)[0:8, 0:TOK]
    denB = ARN[:, 28672:32768].bitcast(F32)[0:1, 0:TOK]
    ptm = ARN[:, 32768:33280].bitcast(F32)[:, 0:128]
    sc1 = ARN[:, 33280:33792].bitcast(F32)[:, 0:128]
    sc2 = ARN[:, 33792:34304].bitcast(F32)[:, 0:128]
    scm = ARN[:, 34304:34816].bitcast(F32)[:, 0:32]

    def gmm(e, c=None):
        last = None
        for c_ in range(NC4):
            for j in range(NJ):
                last = e.matmul(mk.pool[0:8, 2048 + c_ * 512:2048 + c_ * 512 + 512],
                                gwT[:, j * NE:(j + 1) * NE],
                                xln[:, j * TOK + c_ * 512: j * TOK + c_ * 512 + 512],
                                start=(j == 0), stop=(j == NJ - 1))
        return last
    mk.emit("T", gmm, reads=[("c", "gwT")] + [xk(j, c) for j in range(NJ) for c in range(NC4)],
            writes=[("pb", 4), ("pb", 5), ("pb", 6), ("pb", 7)])
    mk.emit("V", lambda e: e.tensor_scalar(logT, mk.pool[0:8, 2048:2048 + TOK],
                                           gbs[:, 0:1], None, ALU.add),
            reads=[("pb", i) for i in range(4, 8)] + [("c", "gb")], writes=[("g", "log")])
    mk.emit("S", lambda e: e.activation(expT, logT, AF.Exp),
            reads=[("g", "log")], writes=[("g", "exp")])

    def gden(e):
        last = None
        for c_ in range(NC4):
            last = e.matmul(mk.pool[0:1, c_ * 512:c_ * 512 + 512], consts["ones8f"][:, :],
                            expT[:, c_ * 512:c_ * 512 + 512], start=True, stop=True)
        return last
    mk.emit("T", gden, reads=[("g", "exp"), ("c", "ones8f")], writes=[("pb", 0), ("pb", 1)])
    mk.emit("V", lambda e: e.reciprocal(denB, mk.pool[0:1, 0:TOK]),
            reads=[("pb", 0), ("pb", 1)], writes=[("g", "den")])

    def gbc(e):
        last = None
        for c_ in range(NC4):
            last = e.matmul(mk.pool[0:8, c_ * 512:c_ * 512 + 512], consts["ones1f"][0:1, 0:8],
                            denB[0:1, c_ * 512:c_ * 512 + 512], start=True, stop=True)
        return last
    mk.emit("T", gbc, reads=[("g", "den"), ("c", "ones1f")], writes=[("pb", 0), ("pb", 1)])
    mk.emit("V", lambda e: e.tensor_mul(prbT, expT, mk.pool[0:8, 0:TOK]),
            reads=[("pb", 0), ("pb", 1), ("g", "exp")], writes=[("g", "prb")])

    def gtr(e):
        last = None
        for tt in range(NTT):
            last = e.transpose(mk.pool[:, 2048 + tt * 8: 2048 + tt * 8 + 8],
                               prbT[:, tt * 128:(tt + 1) * 128], eyef[0:8, 0:8])
        return last
    mk.emit("T", gtr, reads=[("g", "prb"), ("c", "eyef")], writes=[("pb", 4)])
    mk.emit("V", lambda e: e.tensor_copy(ptm[:, :], mk.pool[:, 2048:2048 + 128]),
            reads=[("pb", 4)], writes=[("g", "ptm")])
    p3 = ptm.rearrange("p (t e) -> p t e", e=NE)
    s13 = sc1.rearrange("p (t e) -> p t e", e=NE)
    s23 = sc2.rearrange("p (t e) -> p t e", e=NE)
    m3 = scm.rearrange("p (t o) -> p t o", o=2)
    mk.emit("V", lambda e: e.reduce_max(m3[:, :, 0:1], p3, axis=AX),
            reads=[("g", "ptm")], writes=[("g", "m1")])
    mk.emit("V", lambda e: e.tensor_tensor(s13, p3, m3[:, :, 0:1].broadcast_to([128, NTT, NE]), ALU.is_ge),
            reads=[("g", "ptm"), ("g", "m1")], writes=[("g", "is1")])
    mk.emit("V", lambda e: e.tensor_mul(s23, p3, s13),
            reads=[("g", "ptm"), ("g", "is1")], writes=[("g", "t")])
    mk.emit("V", lambda e: e.tensor_sub(s23, p3, s23),
            reads=[("g", "ptm"), ("g", "t")], writes=[("g", "pm")])
    mk.emit("V", lambda e: e.reduce_max(m3[:, :, 1:2], s23, axis=AX),
            reads=[("g", "pm")], writes=[("g", "m2")])
    mk.emit("V", lambda e: e.tensor_tensor(s23, s23, m3[:, :, 1:2].broadcast_to([128, NTT, NE]), ALU.is_ge),
            reads=[("g", "pm"), ("g", "m2")], writes=[("g", "is2")])
    mk.emit("V", lambda e: e.tensor_add(s13, s13, s23),
            reads=[("g", "is1"), ("g", "is2")], writes=[("g", "sel")])
    mk.emit("V", lambda e: e.tensor_mul(gtm[:, :].rearrange("p (t e) -> p t e", e=NE), p3, s13),
            reads=[("g", "sel"), ("g", "ptm")], writes=[("c", "gtm")])

    if debug_stop == "gate":
        return
    mk.emit("G", lambda e: e.memset(acc[:, :], 0.0),
            writes=[("acc", tt, oc) for tt in range(NTT) for oc in range(2)])
    mk.barrier()
    load_w2(0)
    for e_ in range(NE):
        for tc in range(NC4):
            for hc in range(NHC):
                pbk = hc % 2

                def p1mm(e, e_=e_, tc=tc, hc=hc, pbk=pbk):
                    last = None
                    for j in range(NJ):
                        last = e.matmul(mk.pb(pbk),
                                        w1s[:, hc * 1024 + j * 128: hc * 1024 + j * 128 + 128],
                                        xln[:, j * TOK + tc * 512: j * TOK + tc * 512 + 512],
                                        start=(j == 0), stop=(j == NJ - 1))
                    return last
                mk.emit("T", p1mm, reads=[("w1s", hc)] + [xk(j, tc) for j in range(NJ)],
                        writes=[("pb", pbk)])
                if hc % 2 == 0:
                    mk.emit("S", (lambda e_=e_, hc=hc, pbk=pbk: lambda e: e.activation(
                        htc[:, hc * 512:(hc + 1) * 512], mk.pb(pbk), AF.Relu,
                        bias=b1s[:, e_ * NHC + hc: e_ * NHC + hc + 1]))(),
                        reads=[("pb", pbk), ("c", "b1")], writes=[("h", hc)])
                else:
                    mk.emit("V", (lambda e_=e_, hc=hc, pbk=pbk: lambda e: e.scalar_tensor_tensor(
                        htc[:, hc * 512:(hc + 1) * 512], mk.pb(pbk),
                        b1s[:, e_ * NHC + hc: e_ * NHC + hc + 1], zer[:, :],
                        ALU.add, ALU.max))(),
                        reads=[("pb", pbk), ("c", "b1"), ("c", "zer")], writes=[("h", hc)])
            for ttl in range(4):
                tt = tc * 4 + ttl
                for oc in range(2):
                    pbk = 2 + (ttl * 2 + oc) % 2

                    def p2mm(e, e_=e_, ttl=ttl, oc=oc, pbk=pbk):
                        last = None
                        for hc in range(NHC):
                            e.matmul(mk.pb(pbk),
                                     htc[:, hc * 512 + ttl * 128: hc * 512 + ttl * 128 + 128],
                                     w2s[:, hc * 1024 + oc * 512: hc * 1024 + oc * 512 + 512],
                                     start=(hc == 0), stop=False)
                        last = e.matmul(mk.pb(pbk), consts["ones1b"][0:1, :],
                                        b2s[0:1, e_ * D + oc * 512: e_ * D + oc * 512 + 512],
                                        start=False, stop=True)
                        return last
                    mk.emit("T", p2mm,
                            reads=[("h", hc) for hc in range(NHC)] + [("w2s", hc) for hc in range(NHC)] +
                                  [("c", "b2"), ("c", "ones1b")],
                            writes=[("pb", pbk)])
                    tpar = (ttl * 2 + oc) % 2
                    mk.emit("V", (lambda e_=e_, tt=tt, pbk=pbk, tpar=tpar: lambda e: e.tensor_scalar(
                        tmp2[:, tpar * 512:(tpar + 1) * 512], mk.pb(pbk),
                        gtm[:, tt * NE + e_: tt * NE + e_ + 1], None, ALU.mult))(),
                        reads=[("pb", pbk), ("c", "gtm")], writes=[("tmp2", tpar)])
                    mk.emit("G", (lambda tt=tt, oc=oc, tpar=tpar: lambda e: e.tensor_add(
                        acc[:, tt * 1024 + oc * 512: tt * 1024 + oc * 512 + 512],
                        acc[:, tt * 1024 + oc * 512: tt * 1024 + oc * 512 + 512],
                        tmp2[:, tpar * 512:(tpar + 1) * 512]))(),
                        reads=[("tmp2", tpar), ("acc", tt, oc)], writes=[("acc", tt, oc)])
        if e_ + 1 < NE:
            load_w1(e_ + 1)
            load_w2(e_ + 1)
    if debug_stop == "experts":
        return
    mk.barrier()
    av = {
        "sqt": ARN[:, 32768:34816][:, 0:1024],
        "lnf": ARN[:, 34816:38912].bitcast(F32)[:, 0:1024],
        "svec": ARN[:, 38912:39040].bitcast(F32)[:, 0:48],
        "lnfT": ARN[:, 0:16384],
        "resb": ARN[:, 16384:20480].bitcast(F32),
        "outb": ARN[:, 20480:24576].bitcast(F32),
    }
    emit_postln_tm(mk, consts, eyef, acc,
                   lambda tt: [("acc", tt, 0), ("acc", tt, 1)],
                   av, xT_in, x2_out, out_dram2=x2_out2, tag="plB")


def emit_postln_tm(mk, consts, eyef, src, skey, av, xT_in, out_dram, tag="pl", out_dram2=None):
    sqt, lnf, svec, lnfT, resb, outb = av["sqt"], av["lnf"], av["svec"], av["lnfT"], av["resb"], av["outb"]
    mvec = svec[:, 0:16]
    vvec = svec[:, 16:32]
    rvec = svec[:, 32:48]
    for tt in range(NTT):
        at = src[:, tt * 1024:(tt + 1) * 1024]
        mk.emit("V", (lambda at=at, tt=tt: lambda e: e.reduce_sum(
            mvec[:, tt:tt + 1], at, axis=AX))(),
            reads=skey(tt), writes=[(tag, "m", tt)])
        mk.emit("S", (lambda at=at: lambda e: e.activation(sqt[:, 0:1024], at, AF.Square))(),
            reads=skey(tt), writes=[(tag, "sq")])
        mk.emit("V", (lambda tt=tt: lambda e: e.reduce_sum(
            vvec[:, tt:tt + 1], sqt[:, 0:1024], axis=AX))(),
            reads=[(tag, "sq")], writes=[(tag, "v", tt)])
    mk.emit("V", lambda e: e.tensor_scalar_mul(mvec[:, :], mvec[:, :], 1.0 / D),
            reads=[(tag, "m", tt) for tt in range(NTT)], writes=[(tag, "mv")])
    mk.emit("V", lambda e: e.tensor_scalar_mul(vvec[:, :], vvec[:, :], 1.0 / D),
            reads=[(tag, "v", tt) for tt in range(NTT)], writes=[(tag, "vv")])
    mk.emit("V", lambda e: e.scalar_tensor_tensor(rvec[:, :], mvec[:, :], -1.0, mvec[:, :],
                                                  ALU.mult, ALU.mult),
            reads=[(tag, "mv")], writes=[(tag, "rv")])
    mk.emit("V", lambda e: e.tensor_add(vvec[:, :], vvec[:, :], rvec[:, :]),
            reads=[(tag, "vv"), (tag, "rv")], writes=[(tag, "vv")])
    mk.emit("S", lambda e: e.activation(vvec[:, :], vvec[:, :], AF.Ln, bias=consts["epscol"][:, 0:1]),
            reads=[(tag, "vv"), ("c", "epsc")], writes=[(tag, "vv")])
    mk.emit("S", lambda e: e.activation(rvec[:, :], vvec[:, :], AF.Exp, scale=-0.5),
            reads=[(tag, "vv")], writes=[(tag, "rv")])
    for tt in range(NTT):
        at = src[:, tt * 1024:(tt + 1) * 1024]
        mk.emit("V", (lambda at=at, tt=tt: lambda e: e.tensor_scalar(
            lnf[:, 0:1024], at, mvec[:, tt:tt + 1], rvec[:, tt:tt + 1],
            ALU.subtract, ALU.mult))(),
            reads=skey(tt) + [(tag, "mv"), (tag, "rv")],
            writes=[(tag, "lnf")])

        def ttr(e, tt=tt):
            last = None
            for j in range(NJ):
                last = e.transpose(mk.pool[:, 2048 + j * 128: 2048 + j * 128 + 128],
                                   lnf[:, j * 128:(j + 1) * 128], eyef[:])
            return last
        mk.emit("T", ttr, reads=[(tag, "lnf"), ("c", "eyef")],
                writes=[("pb", 4), ("pb", 5), ("pb", 6), ("pb", 7)])

        def tev(e, tt=tt):
            last = None
            for j in range(NJ):
                last = e.copy(lnfT[:, j * TOK + tt * 128: j * TOK + tt * 128 + 128],
                              mk.pool[:, 2048 + j * 128: 2048 + j * 128 + 128])
            return last
        mk.emit("S", tev, reads=[("pb", 4), ("pb", 5), ("pb", 6), ("pb", 7), (tag, "lnf")],
                writes=[(tag, "lnfT", tt)])
    for j in range(NJ):
        mk.emit("Yq", (lambda j=j: lambda e: e.dma_start(
            out=resb[:, :], in_=xT_in[j * 128:(j + 1) * 128, :]))(),
            reads=[(tag, "resb")], writes=[(tag, "resbw")])
        mk.emit("V", (lambda j=j: lambda e: e.tensor_add(
            outb[:, :], resb[:, :], lnfT[:, j * TOK:(j + 1) * TOK]))(),
            reads=[(tag, "resbw"), (tag, "outb")] + [(tag, "lnfT", tt) for tt in range(NTT)],
            writes=[(tag, "outbw"), (tag, "resb")])
        mk.emit("Yq", (lambda j=j: lambda e: e.dma_start(
            out=out_dram[j * 128:(j + 1) * 128, :], in_=outb[:, :]))(),
            reads=[(tag, "outbw")], writes=[(tag, "st", j)])
        if out_dram2 is not None:
            mk.emit("Gq", (lambda j=j: lambda e: e.dma_start(
                out=out_dram2[j * 128:(j + 1) * 128, :], in_=outb[:, :]))(),
                reads=[(tag, "outbw")], writes=[(tag, "st2", j)])
        mk.regions[(tag, "outb")] = [mk.regions[(tag, "st", j)][0], []]
        if out_dram2 is not None:
            mk.regions[(tag, "outb")][1].append(mk.regions[(tag, "st2", j)][0])
    mk.wait_only("Y", [(tag, "st", j) for j in range(NJ)] +
                 ([(tag, "st2", j) for j in range(NJ)] if out_dram2 is not None else []))


def build_B_full():
    mk = MK()
    io = {
        "xT": mk.dram("xT", [D, TOK], F32, kind="ExternalInput").ap(),
        "gwT": mk.dram("gwT", [D, NE], F32, kind="ExternalInput").ap(),
        "gb": mk.dram("gb", [NE, 1], F32, kind="ExternalInput").ap(),
        "w1": mk.dram("w1", [NE, D, 2048], F32, kind="ExternalInput").ap(),
        "b1r": mk.dram("b1r", [128, NE * NHC], F32, kind="ExternalInput").ap(),
        "w2": mk.dram("w2", [NE, 2048, D], F32, kind="ExternalInput").ap(),
        "b2": mk.dram("b2", [NE, D], F32, kind="ExternalInput").ap(),
        "x2_out": mk.dram("x2T", [D, TOK], F32, kind="ExternalOutput").ap(),
    }
    eyef_in = mk.dram("eyef", [128, 128], F32, kind="ExternalInput").ap()
    mk.psum_pool()
    consts = emit_consts(mk)
    eyef = mk.sbuf("eyef_s", [128, 128], F32)
    mk.emit("Yq", lambda e: e.dma_start(out=eyef[:], in_=eyef_in[:, :]), writes=[("c", "eyef")])
    mk.init_arena()
    emit_phase_B(mk, consts, eyef, io)
    return mk



def emit_phase_C1(mk, consts, eyef, io):
    xT_in, wq_in, cos_in, sin_in, prot_in = (io["xT"], io["wqkv"], io["cos2"],
                                             io["sin2"], io["prot"])
    qT_out, kT_out, v_out = io["qT"], io["kT"], io["v"]
    cos2 = mk.asbuf("cos2_s", [128, TOK], F32)
    sin2 = mk.asbuf("sin2_s", [128, TOK], F32)
    protb = mk.asbuf("prot_s", [128, 128], BF16)
    mk.emit("Yq", lambda e: e.dma_start(out=cos2[:], in_=cos_in[:, :]), writes=[("c", "cos")])
    mk.emit("Yq", lambda e: e.dma_start(out=sin2[:], in_=sin_in[:, :]), writes=[("c", "sin")])
    mk.emit("Gq", lambda e: e.dma_start(out=protb[:], in_=prot_in[:, :]), writes=[("c", "prot")])
    wq = mk.asbuf("wq_s", [128, 24 * NJ * 128], BF16)
    for ot in range(24):
        mk.emit("Gq", (lambda ot=ot: lambda e: e.dma_start(
            out=wq[:, ot * 1024:(ot + 1) * 1024].rearrange("p (j f) -> p j f", f=128),
            in_=wq_in[:, ot * 128:(ot + 1) * 128].rearrange("(j p) f -> p j f", p=128)))(),
            writes=[("wq", ot)])

    xln = mk.asbuf("xln_c1", [128, NJ * TOK], BF16)
    qbuf = mk.asbuf("qbuf", [128, 2 * TOK], BF16)
    kbuf = mk.asbuf("kbuf", [128, 2 * TOK], BF16)
    vbuf = mk.asbuf("vbuf", [128, 2 * TOK], BF16)
    braw = mk.asbuf("braw", [128, 2 * 512], BF16)
    t1f = mk.asbuf("t1f", [128, 2 * 512], F32)
    t2f = mk.asbuf("t2f", [128, 2 * 512], F32)
    vf32 = mk.asbuf("vf32", [128, 2 * 512], F32)
    ARN = mk.asbuf("arn_c1", [128, 24576], BF16)
    lnb = {
        "xb": ARN[:, 0:4096],
        "xq": ARN[:, 4096:8192],
        "absf": ARN[:, 0:8192].bitcast(F32),
        "arow": ARN[:, 8192:12288].bitcast(F32)[0:1, 0:TOK],
        "brow": ARN[:, 12288:16384].bitcast(F32)[0:1, 0:TOK],
        "t1a": ARN[:, 16384:18432].bitcast(F32)[:, 0:512],
        "t1b": ARN[:, 16384:18432].bitcast(F32)[:, 512:1024],
        "resb": ARN[:, 18432:22528].bitcast(F32),
    }
    emit_ln_fm_stream(mk, consts, "lnc", xT_in, xln, lnb)
    xk = lambda j, c: ("lnc", "xln", j, c)

    for ot in range(24):
        kind = 0 if ot < 8 else (1 if ot < 16 else 2)
        jd = ot % 8
        for c in range(NC4):
            cpar = c % 2

            def pmm(e, ot=ot, c=c, cpar=cpar):
                last = None
                for j in range(NJ):
                    last = e.matmul(mk.pb(cpar),
                                    wq[:, ot * 1024 + j * 128: ot * 1024 + j * 128 + 128],
                                    xln[:, j * TOK + c * 512: j * TOK + c * 512 + 512],
                                    start=(j == 0), stop=(j == NJ - 1))
                return last
            mk.emit("T", pmm, reads=[("wq", ot)] + [xk(j, c) for j in range(NJ)],
                    writes=[("pb", cpar)])
            if kind < 2:
                dst = qbuf if kind == 0 else kbuf
                mk.emit("S", (lambda cpar=cpar: lambda e: e.copy(
                    braw[:, cpar * 512:(cpar + 1) * 512], mk.pb(cpar)))(),
                    reads=[("pb", cpar)], writes=[("braw", cpar)])
                mk.emit("T", (lambda cpar=cpar: lambda e: e.matmul(
                    mk.pb(2 + cpar), protb[:],
                    braw[:, cpar * 512:(cpar + 1) * 512], start=True, stop=True))(),
                    reads=[("braw", cpar), ("c", "prot")], writes=[("pb", 2 + cpar)])
                mk.emit("V", (lambda c=c, cpar=cpar: lambda e: e.tensor_mul(
                    t1f[:, cpar * 512:(cpar + 1) * 512],
                    braw[:, cpar * 512:(cpar + 1) * 512],
                    cos2[:, c * 512:(c + 1) * 512]))(),
                    reads=[("braw", cpar), ("c", "cos")], writes=[("t1", cpar)])
                mk.emit("V", (lambda c=c, cpar=cpar: lambda e: e.tensor_mul(
                    t2f[:, cpar * 512:(cpar + 1) * 512], mk.pb(2 + cpar),
                    sin2[:, c * 512:(c + 1) * 512]))(),
                    reads=[("pb", 2 + cpar), ("c", "sin")], writes=[("t2", cpar)])
                jp = jd % 2
                mk.emit("V", (lambda dst=dst, jp=jp, c=c, cpar=cpar: lambda e: e.tensor_add(
                    dst[:, jp * TOK + c * 512: jp * TOK + c * 512 + 512],
                    t1f[:, cpar * 512:(cpar + 1) * 512],
                    t2f[:, cpar * 512:(cpar + 1) * 512]))(),
                    reads=[("t1", cpar), ("t2", cpar)], writes=[("qk", kind, jp, c)])
            else:
                mk.emit("V", (lambda cpar=cpar: lambda e: e.tensor_copy(
                    vf32[:, cpar * 512:(cpar + 1) * 512], mk.pb(cpar)))(),
                    reads=[("pb", cpar)], writes=[("vf32", cpar)])

                def vtr(e, cpar=cpar):
                    last = None
                    for i in range(4):
                        last = e.transpose(
                            mk.pool[:, 2048 + cpar * 1024 + i * 128: 2048 + cpar * 1024 + i * 128 + 128],
                            vf32[:, cpar * 512 + i * 128: cpar * 512 + i * 128 + 128], eyef[:])
                    return last
                mk.emit("T", vtr, reads=[("vf32", cpar), ("c", "eyef")],
                        writes=[("pb", 4 + 2 * cpar), ("pb", 5 + 2 * cpar)])

                jp = jd % 2

                def vev(e, jp=jp, c=c, cpar=cpar):
                    last = None
                    for i in range(4):
                        tt = c * 4 + i
                        last = e.copy(vbuf[:, jp * TOK + tt * 128: jp * TOK + tt * 128 + 128],
                                      mk.pool[:, 2048 + cpar * 1024 + i * 128: 2048 + cpar * 1024 + i * 128 + 128])
                    return last
                mk.emit("S", vev, reads=[("pb", 4 + 2 * cpar), ("pb", 5 + 2 * cpar)],
                        writes=[("qk", 2, jp, c)])
        jp = jd % 2
        if kind == 0:
            mk.emit("Yq", (lambda jd=jd, jp=jp: lambda e: e.dma_start(
                out=qT_out[jd * 128:(jd + 1) * 128, :], in_=qbuf[:, jp * TOK:(jp + 1) * TOK]))(),
                reads=[("qk", 0, jp, c) for c in range(NC4)], writes=[("qk", 0, jp, "st")])
        elif kind == 1:
            mk.emit("Yq", (lambda jd=jd, jp=jp: lambda e: e.dma_start(
                out=kT_out[jd * 128:(jd + 1) * 128, :], in_=kbuf[:, jp * TOK:(jp + 1) * TOK]))(),
                reads=[("qk", 1, jp, c) for c in range(NC4)], writes=[("qk", 1, jp, "st")])
        else:
            mk.emit("Yq", (lambda jd=jd, jp=jp: lambda e: e.dma_start(
                out=v_out.rearrange("(tt p) d -> p tt d", p=128)[:, :, jd * 128:(jd + 1) * 128],
                in_=vbuf[:, jp * TOK:(jp + 1) * TOK].rearrange("p (tt f) -> p tt f", f=128)))(),
                reads=[("qk", 2, jp, c) for c in range(NC4)], writes=[("qk", 2, jp, "st")])
    mk.wait_only("Y", [("qk", k, jp, "st") for k in range(3) for jp in range(2)])


def build_C1():
    mk = MK()
    io = {
        "xT": mk.dram("xT", [D, TOK], F32, kind="ExternalInput").ap(),
        "wqkv": mk.dram("wqkv", [D, 3 * D], F32, kind="ExternalInput").ap(),
        "cos2": mk.dram("cos2", [128, TOK], F32, kind="ExternalInput").ap(),
        "sin2": mk.dram("sin2", [128, TOK], F32, kind="ExternalInput").ap(),
        "prot": mk.dram("prot", [128, 128], F32, kind="ExternalInput").ap(),
        "qT": mk.dram("qT", [D, TOK], BF16, kind="ExternalOutput").ap(),
        "kT": mk.dram("kT", [D, TOK], BF16, kind="ExternalOutput").ap(),
        "v": mk.dram("v", [TOK, D], BF16, kind="ExternalOutput").ap(),
    }
    eyef_in = mk.dram("eyef", [128, 128], F32, kind="ExternalInput").ap()
    mk.psum_pool()
    consts = emit_consts(mk)
    eyef = mk.sbuf("eyef_s", [128, 128], F32)
    mk.emit("Yq", lambda e: e.dma_start(out=eyef[:], in_=eyef_in[:, :]), writes=[("c", "eyef")])
    mk.init_arena()
    emit_phase_C1(mk, consts, eyef, io)
    return mk



def emit_phase_C2(mk, consts, eyef, io):
    xT_in, qT_in, kTf_in, vf_in, wo_in = (io["xT"], io["qT"], io["kTf"], io["vf"], io["wout"])
    x3_out = io["x3_out"]
    x3_out2 = io.get("x3_out2")
    wo = mk.asbuf("wo_s", [128, NJ * 1024], BF16)
    for j in range(NJ):
        mk.emit("Gq", (lambda j=j: lambda e: e.dma_start(
            out=wo[:, j * 1024:(j + 1) * 1024], in_=wo_in[j * 128:(j + 1) * 128, :]))(),
            writes=[("wo", j)])

    fattn = mk.asbuf("fattn", [128, NJ * TOK], BF16)
    ftm = mk.asbuf("ftm_c2", [128, NTT * 1024], BF16)
    pT = mk.asbuf("pT_c2", [128, 3 * 1024], BF16)
    denr = mk.asbuf("denr_c2", [1, 1024], F32)
    bsb = mk.asbuf("bsb_c2", [128, 4 * 512], F32)
    ARN = mk.asbuf("arn_c2", [128, 36864], BF16)
    kh = ARN[:, 0:8192]
    vh = ARN[:, 8192:16512]
    qh = ARN[:, 16512:18560]
    mk.emit("G", lambda e: e.memset(vh[:, :], 1.0), writes=[("vhones",)])

    SC = 1.0 / np.sqrt(64.0)
    for hp in range(NJ):
        mk.emit("Yq", (lambda hp=hp: lambda e: e.dma_start(
            out=kh[:, :], in_=kTf_in[hp * 128:(hp + 1) * 128, :]))(),
            reads=[("kh",)], writes=[("kh",)])
        for hh in range(2):
            mk.emit("Yq", (lambda hp=hp, hh=hh: lambda e: e.dma_start(
                out=vh[:, :].rearrange("p (kt g) -> p kt g", g=130)[:, :, hh * 65:hh * 65 + 64],
                in_=vf_in.rearrange("(kt p) d -> p kt d", p=128)[:, :, hp * 128 + hh * 64: hp * 128 + hh * 64 + 64]))(),
                reads=[("vh",), ("vhones",)], writes=[("vh",)])
        mk.emit("Yq", (lambda hp=hp: lambda e: e.dma_start(
            out=qh[:, :], in_=qT_in[hp * 128:(hp + 1) * 128, :]))(),
            reads=[("qh",)], writes=[("qh",)])
        for qc in range(NC4):
            qpar = qc % 2
            obank = 6 + qpar
            def emit_scexp(kt, qc=qc):
                spar = kt % 2
                sb0 = spar * 2
                ppar = kt % 3

                def scmm(e, qc=qc, kt=kt, sb0=sb0):
                    last = None
                    for hh in range(2):
                        last = e.matmul(mk.pb(sb0 + hh),
                                        kh[hh * 64:hh * 64 + 64, kt * 128: kt * 128 + 128],
                                        qh[hh * 64:hh * 64 + 64, qc * 512: qc * 512 + 512],
                                        start=True, stop=True, tile_position=(hh * 64, 0))
                    return last
                mk.emit("T", scmm, reads=[("kh",), ("qh",), ("pb", sb0), ("pb", sb0 + 1)],
                        writes=[("pb", sb0), ("pb", sb0 + 1)])

                def expop(e, sb0=sb0, ppar=ppar):
                    return e.activation(pT[:, ppar * 1024:(ppar + 1) * 1024],
                                        mk.pool[:, sb0 * 512:sb0 * 512 + 1024],
                                        AF.Exp, scale=SC)
                mk.emit("S", expop, reads=[("pb", sb0), ("pb", sb0 + 1)], writes=[("pT", ppar)])

            def emit_pv(kt, qpar=qpar):
                ppar = kt % 3
                ob0 = 4 + qpar
                ob1 = 6 + qpar

                def pv(e, kt=kt, ppar=ppar, ob0=ob0, ob1=ob1):
                    last = None
                    for hh, ob in ((0, ob0), (1, ob1)):
                        pslice = pT[:, ppar * 1024 + hh * 512: ppar * 1024 + hh * 512 + 512]
                        last = e.matmul(mk.pool[0:65, ob * 512: ob * 512 + 512],
                                        vh[:, kt * 130 + hh * 65: kt * 130 + hh * 65 + 65],
                                        pslice, start=(kt == 0), stop=(kt == NKT - 1))
                    return last
                mk.emit("T", pv, reads=[("pT", ppar), ("vh",), ("pb", ob0), ("pb", ob1)],
                        writes=[("pb", ob0), ("pb", ob1)])

            for kt in range(NKT + 1):
                if kt < NKT:
                    emit_scexp(kt)
                if kt >= 1:
                    emit_pv(kt - 1)
            ob0 = 4 + qpar
            ob1 = 6 + qpar
            mk.emit("V", (lambda ob0=ob0: lambda e: e.reciprocal(
                denr[0:1, 0:512], mk.pool[64:65, ob0 * 512: ob0 * 512 + 512]))(),
                reads=[("pb", ob0)], writes=[("denr", 0)])
            mk.emit("V", (lambda ob1=ob1: lambda e: e.reciprocal(
                denr[0:1, 512:1024], mk.pool[64:65, ob1 * 512: ob1 * 512 + 512]))(),
                reads=[("pb", ob1)], writes=[("denr", 1)])
            sb0 = 0 if qpar == 0 else 2

            def dbc(e, sb0=sb0):
                e.matmul(mk.pool[0:64, sb0 * 512: sb0 * 512 + 512], consts["ones1f"][0:1, 0:64],
                         denr[0:1, 0:512], start=True, stop=True)
                return e.matmul(mk.pool[64:128, sb0 * 512: sb0 * 512 + 512], consts["ones1f"][0:1, 0:64],
                                denr[0:1, 512:1024], start=True, stop=True)
            mk.emit("T", dbc, reads=[("denr", 0), ("denr", 1), ("c", "ones1f"),
                                     ("pb", sb0), ("pb", sb0 + 1)],
                    writes=[("pb", sb0)])
            s0 = (qpar * 2) * 512
            s1 = (qpar * 2 + 1) * 512
            mk.emit("V", (lambda sb0=sb0, s0=s0: lambda e: e.tensor_copy(
                bsb[0:64, s0:s0 + 512], mk.pool[0:64, sb0 * 512: sb0 * 512 + 512]))(),
                reads=[("pb", sb0)], writes=[("bsb", qpar, 0)])
            mk.emit("V", (lambda sb0=sb0, s1=s1: lambda e: e.tensor_copy(
                bsb[0:64, s1:s1 + 512], mk.pool[64:128, sb0 * 512: sb0 * 512 + 512]))(),
                reads=[("pb", sb0)], writes=[("bsb", qpar, 1)])
            mk.emit("V", (lambda hp=hp, qc=qc, ob0=ob0, s0=s0: lambda e: e.tensor_mul(
                fattn[:, hp * TOK + qc * 512: hp * TOK + qc * 512 + 512][0:64, :],
                mk.pool[0:64, ob0 * 512: ob0 * 512 + 512],
                bsb[0:64, s0:s0 + 512]))(),
                reads=[("bsb", qpar, 0), ("pb", ob0)], writes=[("fa", hp, qc)])
            mk.emit("V", (lambda hp=hp, qc=qc, ob1=ob1, s1=s1: lambda e: e.tensor_mul(
                fattn[:, hp * TOK + qc * 512: hp * TOK + qc * 512 + 512][64:128, :],
                mk.pool[0:64, ob1 * 512: ob1 * 512 + 512],
                bsb[0:64, s1:s1 + 512]))(),
                reads=[("bsb", qpar, 1), ("pb", ob1), ("fa", hp, qc)], writes=[("fa", hp, qc)])

    for tt in range(NTT):
        for oc in range(2):
            pbk = (tt * 2 + oc) % 2

            def womm(e, tt=tt, oc=oc, pbk=pbk):
                last = None
                for j in range(NJ):
                    last = e.matmul(mk.pb(pbk),
                                    fattn[:, j * TOK + tt * 128: j * TOK + tt * 128 + 128],
                                    wo[:, j * 1024 + oc * 512: j * 1024 + oc * 512 + 512],
                                    start=(j == 0), stop=(j == NJ - 1))
                return last
            mk.emit("T", womm,
                    reads=[("fa", j, tt // 4) for j in range(NJ)] + [("wo", j) for j in range(NJ)] +
                          [("pb", pbk)],
                    writes=[("pb", pbk)])
            mk.emit("S", (lambda tt=tt, oc=oc, pbk=pbk: lambda e: e.copy(
                ftm[:, tt * 1024 + oc * 512: tt * 1024 + oc * 512 + 512], mk.pb(pbk)))(),
                reads=[("pb", pbk)], writes=[("ftm", tt, oc)])

    mk.barrier()
    av = {
        "sqt": ARN[:, 18560:20608][:, 0:1024],
        "lnf": ARN[:, 20608:24704].bitcast(F32)[:, 0:1024],
        "svec": ARN[:, 24704:24832].bitcast(F32)[:, 0:48],
        "lnfT": ARN[:, 0:16384],
        "resb": ARN[:, 24832:28928].bitcast(F32)[:, 0:TOK],
        "outb": ARN[:, 28928:33024].bitcast(F32)[:, 0:TOK],
    }
    emit_postln_tm(mk, consts, eyef, ftm,
                   lambda tt: [("ftm", tt, 0), ("ftm", tt, 1)],
                   av, xT_in, x3_out, tag="plC", out_dram2=x3_out2)


def build_C2():
    mk = MK()
    io = {
        "xT": mk.dram("xT", [D, TOK], F32, kind="ExternalInput").ap(),
        "qT": mk.dram("qT", [D, TOK], BF16, kind="ExternalInput").ap(),
        "kTf": mk.dram("kTf", [D, S], BF16, kind="ExternalInput").ap(),
        "vf": mk.dram("vf", [S, D], BF16, kind="ExternalInput").ap(),
        "wout": mk.dram("wout", [D, D], F32, kind="ExternalInput").ap(),
        "x3_out": mk.dram("x3T", [D, TOK], F32, kind="ExternalOutput").ap(),
    }
    eyef_in = mk.dram("eyef", [128, 128], F32, kind="ExternalInput").ap()
    mk.psum_pool()
    consts = emit_consts(mk)
    eyef = mk.sbuf("eyef_s", [128, 128], F32)
    mk.emit("Yq", lambda e: e.dma_start(out=eyef[:], in_=eyef_in[:, :]), writes=[("c", "eyef")])
    mk.init_arena()
    emit_phase_C2(mk, consts, eyef, io)
    return mk




NLP = 8


def emit_phase_D1(mk, consts, eyef, io):
    xT_in, nwT_in, nb_in, w1_in, b1_in, w2_in, b2_in, selm_in = (
        io["xT"], io["nwT3"], io["nb3"], io["w1f"], io["b1f"], io["w2f"], io["b2f"], io["selm"])
    qT_out, kT_out, v_out = io["qT"], io["kT"], io["v"]
    selm = mk.asbuf("selm_s", [16, 16 * 64], BF16)
    mk.emit("Gq", lambda e: e.dma_start(out=selm[:], in_=selm_in[:, :]), writes=[("c", "selm")])
    nwT = mk.asbuf("nwT_s", [128, 3 * NJ * 15], BF16)
    mk.emit("Gq", lambda e: e.dma_start(out=nwT[:, :].rearrange("p (x j r) -> p x j r", x=3, j=NJ),
                                        in_=nwT_in.rearrange("x (j p) r -> p x j r", p=128)),
            writes=[("c", "nwT")])
    nbs = mk.asbuf("nb_s", [15, 3], F32)
    mk.emit("Yq", lambda e: e.dma_start(out=nbs[:, :], in_=nb_in[:, :]),
            writes=[("c", "nb")])
    b1s = mk.asbuf("b1d_s", [128, 3 * NLP], F32)
    mk.emit("Yq", lambda e: e.dma_start(out=b1s[:, :].rearrange("p (x l) -> p x l", x=3),
                                        in_=b1_in.rearrange("x p l -> p x l")),
            writes=[("c", "b1")])
    b2s = mk.asbuf("b2d_s", [16, 3 * D], BF16)
    mk.emit("Gq", lambda e: e.dma_start(out=b2s[:, :].rearrange("p (x o) -> p x o", x=3),
                                        in_=b2_in.rearrange("x p o -> p x o")),
            writes=[("c", "b2")])

    xln = mk.asbuf("xln_d1", [128, NJ * TOK], BF16)
    mH = mk.asbuf("mH", [128, NLP * TOK], BF16)
    selT = mk.asbuf("selT", [16, TOK], BF16)
    sTM = mk.asbuf("sTM", [128, NTT * 16], F32)
    sel3f = mk.asbuf("sel3f", [128, NTT * 16], F32)
    tmpT = mk.asbuf("tmpT", [128, NTT * 16], F32)
    hp_ = mk.asbuf("hp_", [128, 2 * 512], BF16)
    qkbuf = mk.asbuf("qkbuf", [128, 2 * TOK], BF16)
    vtm = mk.asbuf("vtm_d1", [128, NTT * 1024], BF16)
    w1w = mk.asbuf("w1w", [128, NLP * 1024], BF16)
    w2w = mk.asbuf("w2w", [128, NLP * 1024], BF16)
    ARN = mk.asbuf("arn_d1", [128, 22528], BF16)
    lnb = {
        "xb": ARN[:, 0:4096],
        "xq": ARN[:, 4096:8192],
        "absf": ARN[:, 0:8192].bitcast(F32),
        "arow": ARN[:, 8192:12288].bitcast(F32)[0:1, 0:TOK],
        "brow": ARN[:, 12288:16384].bitcast(F32)[0:1, 0:TOK],
        "t1a": ARN[:, 16384:18432].bitcast(F32)[:, 0:512],
        "t1b": ARN[:, 16384:18432].bitcast(F32)[:, 512:1024],
        "resb": ARN[:, 18432:22528].bitcast(F32),
    }

    def load_wf(f):
        for lp in range(NLP):
            mk.emit("Gq", (lambda f=f, lp=lp: lambda e: e.dma_start(
                out=w1w[:, lp * 1024:(lp + 1) * 1024].rearrange("p (j g) -> p j g", g=128),
                in_=w1_in[f, lp].rearrange("(j p) g -> p j g", p=128)))(),
                reads=[("w1w", lp)], writes=[("w1w", lp)])
            mk.emit("Gq", (lambda f=f, lp=lp: lambda e: e.dma_start(
                out=w2w[:, lp * 1024:(lp + 1) * 1024], in_=w2_in[f, lp]))(),
                reads=[("w2w", lp)], writes=[("w2w", lp)])

    load_wf(0)
    emit_ln_fm_stream(mk, consts, "lnd", xT_in, xln, lnb)
    xk = lambda j, c: ("lnd", "xln", j, c)
    mk.barrier()
    sT = ARN[:, 0:4096].bitcast(F32)[0:15, 0:TOK]

    for f in range(3):
        def nsmm(e, f=f):
            last = None
            for c in range(NC4):
                for j in range(NJ):
                    last = e.matmul(mk.pool[0:15, 2048 + c * 512:2048 + c * 512 + 512],
                                    nwT[:, (f * NJ + j) * 15:(f * NJ + j) * 15 + 15],
                                    xln[:, j * TOK + c * 512: j * TOK + c * 512 + 512],
                                    start=(j == 0), stop=(j == NJ - 1))
            return last
        mk.emit("T", nsmm, reads=[("c", "nwT")] + [xk(j, c) for j in range(NJ) for c in range(NC4)],
                writes=[("pb", 4), ("pb", 5), ("pb", 6), ("pb", 7)])
        mk.emit("V", (lambda f=f: lambda e: e.tensor_scalar(
            sT[:, :], mk.pool[0:15, 2048:2048 + TOK], nbs[0:15, f:f + 1], None, ALU.add))(),
            reads=[("pb", i) for i in range(4, 8)] + [("c", "nb")], writes=[("sT",)])

        def str_(e):
            last = None
            for tt in range(NTT):
                last = e.transpose(mk.pool[:, tt * 16: tt * 16 + 15],
                                   sT[:, tt * 128:(tt + 1) * 128], eyef[0:15, 0:15])
            return last
        mk.emit("T", str_, reads=[("sT",), ("c", "eyef")],
                writes=[("pb", 0), ("pb", 1), ("pb", 2), ("pb", 3)])
        mk.emit("V", lambda e: e.tensor_copy(sTM[:, :].rearrange("p (t g) -> p t g", g=16)[:, :, 0:15],
                                             mk.pool[:, 0:NTT * 16].rearrange("p (t g) -> p t g", g=16)[:, :, 0:15]),
                reads=[("pb", 0), ("pb", 1), ("pb", 2), ("pb", 3)], writes=[("sTM",)])
        mk.emit("V", lambda e: e.tensor_single_scalar(sTM[:, :], sTM[:, :], 0.0, ALU.is_ge),
                reads=[("sTM",)], writes=[("sg",)])
        s3 = sTM[:, :].rearrange("p (t g) -> p t g", g=16)
        sel3 = sel3f[:, :].rearrange("p (t g) -> p t g", g=16)
        tp3 = tmpT[:, :].rearrange("p (t g) -> p t g", g=16)

        def bc(ap):
            return ap.broadcast_to([128, NTT, ap.shape[2]]) if False else ap

        mk.emit("V", lambda e: e.tensor_scalar(sel3[:, :, 0:8],
                                               s3[:, :, 0:1].broadcast_to([128, NTT, 8]),
                                               -1.0, 1.0, ALU.mult, ALU.add),
                reads=[("sg",)], writes=[("sel",)])
        mk.emit("V", lambda e: e.tensor_copy(sel3[:, :, 8:16],
                                             s3[:, :, 0:1].broadcast_to([128, NTT, 8])),
                reads=[("sg",), ("sel",)], writes=[("sel",)])
        for lev, base, wid in ((1, 1, 4), (2, 3, 2), (3, 7, 1)):
            ncomb = 2 ** lev
            for comb in range(ncomb):
                node = base + comb
                for b_ in range(2):
                    lo = (comb * 2 + b_) * wid
                    if b_ == 1:
                        mk.emit("V", (lambda node=node, lo=lo, wid=wid: lambda e: e.tensor_copy(
                            tp3[:, :, lo:lo + wid],
                            s3[:, :, node:node + 1].broadcast_to([128, NTT, wid])))(),
                            reads=[("sg",), ("tmpT",)], writes=[("tmpT",)])
                    else:
                        mk.emit("V", (lambda node=node, lo=lo, wid=wid: lambda e: e.tensor_scalar(
                            tp3[:, :, lo:lo + wid],
                            s3[:, :, node:node + 1].broadcast_to([128, NTT, wid]),
                            -1.0, 1.0, ALU.mult, ALU.add))(),
                            reads=[("sg",), ("tmpT",)], writes=[("tmpT",)])
            mk.emit("V", lambda e: e.tensor_mul(sel3f[:, :], sel3f[:, :], tmpT[:, :]),
                    reads=[("sel",), ("tmpT",)], writes=[("sel",)])

        def seltr(e):
            last = None
            for tt in range(NTT):
                last = e.transpose(mk.pool[0:16, tt * 128: tt * 128 + 128],
                                   sel3f[:, tt * 16:(tt + 1) * 16], eyef[:])
            return last
        mk.emit("T", seltr, reads=[("sel",), ("c", "eyef")],
                writes=[("pb", 0), ("pb", 1), ("pb", 2), ("pb", 3)])
        mk.emit("S", lambda e: e.copy(selT[:, :], mk.pool[0:16, 0:TOK]),
                reads=[("pb", 0), ("pb", 1), ("pb", 2), ("pb", 3)], writes=[("selT",)])

        for lp in range(NLP):
            for c in range(NC4):
                cb = c % 2

                def h1mm(e, lp=lp, c=c, cb=cb):
                    last = None
                    for j in range(NJ):
                        last = e.matmul(mk.pb(cb),
                                        w1w[:, lp * 1024 + j * 128: lp * 1024 + j * 128 + 128],
                                        xln[:, j * TOK + c * 512: j * TOK + c * 512 + 512],
                                        start=(j == 0), stop=(j == NJ - 1))
                    return last
                mk.emit("T", h1mm, reads=[("w1w", lp)] + [xk(j, c) for j in range(NJ)],
                        writes=[("pb", cb)])
                mk.emit("S", (lambda f=f, lp=lp, cb=cb: lambda e: e.activation(
                    hp_[:, cb * 512:(cb + 1) * 512], mk.pb(cb), AF.Relu,
                    bias=b1s[:, f * NLP + lp: f * NLP + lp + 1]))(),
                    reads=[("pb", cb), ("c", "b1")], writes=[("hp", cb)])

                def mmm(e, lp=lp, c=c, cb=cb):
                    e.matmul(mk.pool[0:64, (2 + cb) * 512:(2 + cb) * 512 + 512],
                             selm[:, (2 * lp) * 64:(2 * lp) * 64 + 64],
                             selT[:, c * 512:(c + 1) * 512], start=True, stop=True)
                    return e.matmul(mk.pool[64:128, (2 + cb) * 512:(2 + cb) * 512 + 512],
                                    selm[:, (2 * lp + 1) * 64:(2 * lp + 1) * 64 + 64],
                                    selT[:, c * 512:(c + 1) * 512], start=True, stop=True)
                mk.emit("T", mmm, reads=[("selT",), ("c", "selm"), ("pb", 2 + cb)],
                        writes=[("pb", 2 + cb)])
                mk.emit("V", (lambda lp=lp, c=c, cb=cb: lambda e: e.tensor_mul(
                    mH[:, lp * TOK + c * 512: lp * TOK + c * 512 + 512],
                    hp_[:, cb * 512:(cb + 1) * 512], mk.pb(2 + cb)))(),
                    reads=[("hp", cb), ("pb", 2 + cb)], writes=[("mH", lp, c)])

        if f < 2:
            dst_out = qT_out if f == 0 else kT_out
            for ot in range(NJ):
                op = ot % 2
                for c in range(NC4):
                    cb = 4 + (c % 2)

                    def w2mm(e, f=f, ot=ot, c=c, cb=cb):
                        for lp in range(NLP):
                            e.matmul(mk.pb(cb),
                                     w2w[:, lp * 1024 + ot * 128: lp * 1024 + ot * 128 + 128],
                                     mH[:, lp * TOK + c * 512: lp * TOK + c * 512 + 512],
                                     start=(lp == 0), stop=False)
                        return e.matmul(mk.pb(cb), b2s[0:16, f * D + ot * 128: f * D + ot * 128 + 128],
                                        selT[:, c * 512:(c + 1) * 512], start=False, stop=True)
                    mk.emit("T", w2mm,
                            reads=[("mH", lp, c) for lp in range(NLP)] +
                                  [("w2w", lp) for lp in range(NLP)] +
                                  [("selT",), ("c", "b2"), ("pb", cb)],
                            writes=[("pb", cb)])
                    mk.emit("S", (lambda ot=ot, op=op, c=c, cb=cb: lambda e: e.copy(
                        qkbuf[:, op * TOK + c * 512: op * TOK + c * 512 + 512], mk.pb(cb)))(),
                        reads=[("pb", cb)], writes=[("qk", op, c)])
                mk.emit("Yq", (lambda dst_out=dst_out, ot=ot, op=op: lambda e: e.dma_start(
                    out=dst_out[ot * 128:(ot + 1) * 128, :],
                    in_=qkbuf[:, op * TOK:(op + 1) * TOK]))(),
                    reads=[("qk", op, c) for c in range(NC4)], writes=[("qkst", op)])
        else:
            for c in range(NC4):
                for ttl in range(4):
                    tt = c * 4 + ttl
                    for oc in range(2):
                        cb = 4 + ((ttl * 2 + oc) % 2)

                        def vmm(e, c=c, ttl=ttl, oc=oc, cb=cb):
                            for lp in range(NLP):
                                e.matmul(mk.pb(cb),
                                         mH[:, lp * TOK + c * 512 + ttl * 128: lp * TOK + c * 512 + ttl * 128 + 128],
                                         w2w[:, lp * 1024 + oc * 512: lp * 1024 + oc * 512 + 512],
                                         start=(lp == 0), stop=False)
                            return e.matmul(mk.pb(cb),
                                            selT[:, c * 512 + ttl * 128: c * 512 + ttl * 128 + 128],
                                            b2s[0:16, 2 * D + oc * 512: 2 * D + oc * 512 + 512],
                                            start=False, stop=True)
                        mk.emit("T", vmm,
                                reads=[("mH", lp, c) for lp in range(NLP)] +
                                      [("w2w", lp) for lp in range(NLP)] +
                                      [("selT",), ("c", "b2"), ("pb", cb)],
                                writes=[("pb", cb)])
                        mk.emit("S", (lambda tt=tt, oc=oc, cb=cb: lambda e: e.copy(
                            vtm[:, tt * 1024 + oc * 512: tt * 1024 + oc * 512 + 512], mk.pb(cb)))(),
                            reads=[("pb", cb)], writes=[("vt", tt, oc)])
        if f < 2:
            load_wf(f + 1)
    mk.emit("Yq", lambda e: e.dma_start(
        out=v_out.rearrange("(tt p) d -> p tt d", p=128),
        in_=vtm[:, :].rearrange("p (tt d) -> p tt d", d=1024)),
        reads=[("vt", tt, oc) for tt in range(NTT) for oc in range(2)], writes=[("vst",)])
    mk.wait_only("Y", [("vst",), ("qkst", 0), ("qkst", 1)])


def build_D1():
    mk = MK()
    io = {
        "xT": mk.dram("xT", [D, TOK], F32, kind="ExternalInput").ap(),
        "nwT3": mk.dram("nwT3", [3, D, 15], F32, kind="ExternalInput").ap(),
        "nb3": mk.dram("nb3", [15, 3], F32, kind="ExternalInput").ap(),
        "w1f": mk.dram("w1f", [3, NLP, D, 128], F32, kind="ExternalInput").ap(),
        "b1f": mk.dram("b1f", [3, 128, NLP], F32, kind="ExternalInput").ap(),
        "w2f": mk.dram("w2f", [3, NLP, 128, D], F32, kind="ExternalInput").ap(),
        "b2f": mk.dram("b2f", [3, 16, D], F32, kind="ExternalInput").ap(),
        "selm": mk.dram("selm", [16, 16 * 64], F32, kind="ExternalInput").ap(),
        "qT": mk.dram("qT", [D, TOK], BF16, kind="ExternalOutput").ap(),
        "kT": mk.dram("kT", [D, TOK], BF16, kind="ExternalOutput").ap(),
        "v": mk.dram("v", [TOK, D], BF16, kind="ExternalOutput").ap(),
    }
    eyef_in = mk.dram("eyef", [128, 128], F32, kind="ExternalInput").ap()
    mk.psum_pool()
    consts = emit_consts(mk)
    eyef = mk.sbuf("eyef_s", [128, 128], F32)
    mk.emit("Yq", lambda e: e.dma_start(out=eyef[:], in_=eyef_in[:, :]), writes=[("c", "eyef")])
    mk.init_arena()
    emit_phase_D1(mk, consts, eyef, io)
    return mk



def build_D2():
    mk = MK()
    xT_in = mk.dram("xT", [D, TOK], F32, kind="ExternalInput").ap()
    qT_in = mk.dram("qT", [D, TOK], BF16, kind="ExternalInput").ap()
    kTf_in = mk.dram("kTf", [D, S], BF16, kind="ExternalInput").ap()
    vf_in = mk.dram("vf", [S, D], BF16, kind="ExternalInput").ap()
    eyef_in = mk.dram("eyef", [128, 128], F32, kind="ExternalInput").ap()
    out_tm = mk.dram("xout", [TOK, D], F32, kind="ExternalOutput").ap()

    mk.psum_pool()
    consts = emit_consts(mk)
    eyef = mk.sbuf("eyef_s", [128, 128], F32)
    mk.emit("Yq", lambda e: e.dma_start(out=eyef[:], in_=eyef_in[:, :]), writes=[("c", "eyef")])
    qT = mk.sbuf("qT_s", [128, NJ * TOK], BF16)
    for j in range(NJ):
        mk.emit("Yq", (lambda j=j: lambda e: e.dma_start(
            out=qT[:, j * TOK:(j + 1) * TOK], in_=qT_in[j * 128:(j + 1) * 128, :]))(),
            writes=[("qT", j)])
    ftm = mk.sbuf("ftm", [128, NTT * 1024], BF16)
    denr = mk.sbuf("denr", [1, 2 * 256], F32)
    rcols = mk.sbuf("rcols", [128, 2 * 2], F32)
    onesc = mk.sbuf("onesc", [1, 1], F32)
    mk.emit("G", lambda e: e.memset(onesc[:], 1.0), writes=[("c", "o1")])
    ARN = mk.sbuf("arn", [128, 40960], BF16)
    ppan = ARN[:, 0:32768]
    kbb = ARN[:, 32768:36864]
    vbb = ARN[:, 36864:40960]

    SC = 1.0 / np.sqrt(1024.0)
    QCN = TOK // 256

    def p1_load(qc, ktp):
        par = (ktp // 2) % 2
        mk.emit("Yq", (lambda ktp=ktp, par=par: lambda e: e.dma_start(
            out=kbb[:, par * 2048:(par + 1) * 2048].rearrange("p (j i f) -> p j i f", j=NJ, i=2),
            in_=kTf_in[:, ktp * 128:(ktp + 2) * 128].rearrange("(j p) (i f) -> p j i f", p=128, i=2)))(),
            reads=[("kbb", par)], writes=[("kbb", par)])

    def p1_sc(qc, kt):
        pp = qc % 2
        par = (kt // 2) % 2
        i = kt % 2
        sb = 4 + (kt % 2)

        def scmm(e, qc=qc, par=par, i=i, sb=sb):
            last = None
            for j in range(NJ):
                last = e.matmul(mk.pool[:, sb * 512: sb * 512 + 256],
                                kbb[:, par * 2048 + j * 256 + i * 128: par * 2048 + j * 256 + i * 128 + 128],
                                qT[:, j * TOK + (qc // 2) * 512 + (qc % 2) * 256:
                                   j * TOK + (qc // 2) * 512 + (qc % 2) * 256 + 256],
                                start=(j == 0), stop=(j == NJ - 1))
            return last
        mk.emit("T", scmm, reads=[("kbb", par), ("pb2", sb)] + [("qT", j) for j in range(NJ)],
                writes=[("pb2", sb)])
        mk.emit("S", (lambda qc=qc, kt=kt, sb=sb, pp=pp: lambda e: e.activation(
            ppan[:, pp * 16384 + kt * 256: pp * 16384 + kt * 256 + 256],
            mk.pool[:, sb * 512: sb * 512 + 256], AF.Exp, scale=SC))(),
            reads=[("pb2", sb)], writes=[("pp", pp, kt)])
        mk.emit("T", (lambda kt=kt, pp=pp, qc=qc: lambda e: e.matmul(
            mk.pool[0:1, 6 * 512 + (qc % 2) * 256: 6 * 512 + (qc % 2) * 256 + 256],
            consts["onesb"][:], ppan[:, pp * 16384 + kt * 256: pp * 16384 + kt * 256 + 256],
            start=(kt == 0), stop=(kt == NKT - 1)))(),
            reads=[("pp", pp, kt), ("c", "onesb"), ("pbden", qc % 2)], writes=[("pbden", qc % 2)])

    def p1_fin(qc):
        qp = qc % 2
        mk.emit("V", (lambda qp=qp: lambda e: e.reciprocal(
            denr[0:1, qp * 256:(qp + 1) * 256],
            mk.pool[0:1, 6 * 512 + qp * 256: 6 * 512 + qp * 256 + 256]))(),
            reads=[("pbden", qp)], writes=[("denr", qp)])

        def rtr(e, qp=qp):
            last = None
            for qt in range(2):
                last = e.matmul(mk.pool[:, 7 * 512 + qp * 2 + qt: 7 * 512 + qp * 2 + qt + 1],
                                denr[0:1, qp * 256 + qt * 128: qp * 256 + (qt + 1) * 128],
                                onesc[0:1, 0:1], start=True, stop=True)
            return last
        mk.emit("T", rtr, reads=[("denr", qp), ("c", "o1"), ("pbr", qp)], writes=[("pbr", qp)])
        mk.emit("V", (lambda qp=qp: lambda e: e.tensor_copy(
            rcols[:, qp * 2:(qp + 1) * 2], mk.pool[:, 7 * 512 + qp * 2: 7 * 512 + qp * 2 + 2]))(),
            reads=[("pbr", qp)], writes=[("rcols", qp)])

    def p2_load(qc, ktp):
        par = (ktp // 2) % 2
        mk.emit("Yq", (lambda ktp=ktp, par=par: lambda e: e.dma_start(
            out=vbb[:, par * 2048:(par + 1) * 2048].rearrange("p (i f) -> p i f", i=2),
            in_=vf_in.rearrange("(kt p) d -> p kt d", p=128)[:, ktp:ktp + 2, :]))(),
            reads=[("vbb", par)], writes=[("vbb", par)])

    def p2_pv(qc, kt):
        pp = qc % 2
        par = (kt // 2) % 2
        i = kt % 2

        def pvmm(e, qc=qc, kt=kt, pp=pp, par=par, i=i):
            last = None
            for qt in range(2):
                for oc in range(2):
                    ob = qt * 2 + oc
                    last = e.matmul(
                        mk.pool[:, ob * 512: ob * 512 + 512],
                        ppan[:, pp * 16384 + kt * 256 + qt * 128: pp * 16384 + kt * 256 + qt * 128 + 128],
                        vbb[:, par * 2048 + i * 1024 + oc * 512: par * 2048 + i * 1024 + oc * 512 + 512],
                        start=(kt == 0), stop=(kt == NKT - 1))
            return last
        mk.emit("T", pvmm,
                reads=[("vbb", par), ("pp", pp, kt)] + [("pbo", b) for b in range(4)],
                writes=[("pbo", b) for b in range(4)])

    def p2_fin(qc):
        pp = qc % 2
        for qt in range(2):
            tt = qc * 2 + qt
            mk.emit("S", (lambda tt=tt, qt=qt, pp=pp: lambda e: e.activation(
                ftm[:, tt * 1024:(tt + 1) * 1024],
                mk.pool[:, qt * 1024: qt * 1024 + 1024],
                AF.Copy, scale=rcols[:, pp * 2 + qt: pp * 2 + qt + 1]))(),
                reads=[("pbo", qt * 2), ("pbo", qt * 2 + 1), ("rcols", pp)],
                writes=[("ftm", tt)])

    for qc in range(QCN + 1):
        for kt in range(NKT):
            if qc < QCN:
                if kt % 2 == 0:
                    p1_load(qc, kt)
                p1_sc(qc, kt)
            if qc >= 1:
                if kt % 2 == 0:
                    p2_load(qc - 1, kt)
                p2_pv(qc - 1, kt)
        if qc < QCN:
            p1_fin(qc)
        if qc >= 1:
            p2_fin(qc - 1)

    mk.barrier()
    sqt = ARN[:, 0:1024]
    lnf = ARN[:, 1024:3072].bitcast(F32)[:, 0:1024]
    svec = ARN[:, 3072:3200].bitcast(F32)[:, 0:48]
    rjb = ARN[:, 3200:5248].bitcast(F32)[:, 0:1024]
    outb = ARN[:, 5248:9344].bitcast(F32)[:, 0:2048]
    mvec = svec[:, 0:16]
    vvec = svec[:, 16:32]
    rvec = svec[:, 32:48]
    tag = "po"
    for tt in range(NTT):
        at = ftm[:, tt * 1024:(tt + 1) * 1024]
        mk.emit("V", (lambda at=at, tt=tt: lambda e: e.reduce_sum(
            mvec[:, tt:tt + 1], at, axis=AX))(),
            reads=[("ftm", tt)], writes=[(tag, "m", tt)])
        mk.emit("S", (lambda at=at: lambda e: e.activation(sqt[:, 0:1024], at, AF.Square))(),
            reads=[("ftm", tt)], writes=[(tag, "sq")])
        mk.emit("V", (lambda tt=tt: lambda e: e.reduce_sum(
            vvec[:, tt:tt + 1], sqt[:, 0:1024], axis=AX))(),
            reads=[(tag, "sq")], writes=[(tag, "v", tt)])
    mk.emit("V", lambda e: e.tensor_scalar_mul(mvec[:, :], mvec[:, :], 1.0 / D),
            reads=[(tag, "m", tt) for tt in range(NTT)], writes=[(tag, "mv")])
    mk.emit("V", lambda e: e.tensor_scalar_mul(vvec[:, :], vvec[:, :], 1.0 / D),
            reads=[(tag, "v", tt) for tt in range(NTT)], writes=[(tag, "vv")])
    mk.emit("V", lambda e: e.scalar_tensor_tensor(rvec[:, :], mvec[:, :], -1.0, mvec[:, :],
                                                  ALU.mult, ALU.mult),
            reads=[(tag, "mv")], writes=[(tag, "rv")])
    mk.emit("V", lambda e: e.tensor_add(vvec[:, :], vvec[:, :], rvec[:, :]),
            reads=[(tag, "vv"), (tag, "rv")], writes=[(tag, "vv")])
    mk.emit("S", lambda e: e.activation(vvec[:, :], vvec[:, :], AF.Ln, bias=consts["epscol"][:, 0:1]),
            reads=[(tag, "vv"), ("c", "epsc")], writes=[(tag, "vv")])
    mk.emit("S", lambda e: e.activation(rvec[:, :], vvec[:, :], AF.Exp, scale=-0.5),
            reads=[(tag, "vv")], writes=[(tag, "rv")])
    for tt in range(NTT):
        mk.emit("Yq", (lambda tt=tt: lambda e: e.dma_start(
            out=rjb[:, 0:1024].rearrange("p (j f) -> p j f", f=128),
            in_=xT_in[:, tt * 128:(tt + 1) * 128].rearrange("(j p) f -> p j f", p=128)))(),
            reads=[(tag, "rjb")], writes=[(tag, "rjbw")])

        def rtt(e, tt=tt):
            last = None
            for j in range(NJ):
                last = e.transpose(mk.pool[:, 2048 + j * 128: 2048 + j * 128 + 128],
                                   rjb[:, j * 128:(j + 1) * 128], eyef[:])
            return last
        mk.emit("T", rtt, reads=[(tag, "rjbw"), ("c", "eyef")],
                writes=[("pb", 4), ("pb", 5), ("pb", 6), ("pb", 7)])
        at = ftm[:, tt * 1024:(tt + 1) * 1024]
        mk.emit("V", (lambda at=at, tt=tt: lambda e: e.tensor_scalar(
            lnf[:, 0:1024], at, mvec[:, tt:tt + 1], rvec[:, tt:tt + 1],
            ALU.subtract, ALU.mult))(),
            reads=[("ftm", tt), (tag, "mv"), (tag, "rv")],
            writes=[(tag, "lnf")])
        mk.emit("V", (lambda tt=tt: lambda e: e.tensor_add(
            outb[:, 0:1024], lnf[:, 0:1024], mk.pool[:, 2048:3072]))(),
            reads=[(tag, "lnf"), ("pb", 4), ("pb", 5), ("pb", 6), ("pb", 7), (tag, "outb")],
            writes=[(tag, "outbw"), (tag, "rjb")])
        mk.emit("Yq", (lambda tt=tt: lambda e: e.dma_start(
            out=out_tm[tt * 128:(tt + 1) * 128, :], in_=outb[:, 0:1024]))(),
            reads=[(tag, "outbw")], writes=[(tag, "outb"), (tag, "st", tt)])
    mk.wait_only("Y", [(tag, "st", tt) for tt in range(NTT)])
    return mk



def emit_phase_A0(mk, consts, eyef, io):
    x_in = io["x"]
    vT_in, uTb_in = io["vT3"], io["uTb3"]
    qT_out, kTf_out, vf_out, xT_out = io["qTs"], io["kTfs"], io["vfs"], io["xTs"]

    vTs = mk.asbuf("vTs", [128, 3 * NJ * 32], BF16)
    uTbs = mk.asbuf("uTbs", [33, 3 * NJ * 128], BF16)
    mk.emit("Gq", lambda e: e.dma_start(out=vTs[:, :].rearrange("p (x j r) -> p x j r", x=3, j=NJ),
                                        in_=vT_in.rearrange("x j p r -> p x j r")),
            writes=[("c", "vTs")])
    mk.emit("Gq", lambda e: e.dma_start(out=uTbs[:, :].rearrange("p (x j f) -> p x j f", x=3, j=NJ),
                                        in_=uTb_in.rearrange("x j p f -> p x j f")),
            writes=[("c", "uTbs")])

    res = mk.asbuf("res_a0", [128, NJ * TOK], F32)
    xln = mk.asbuf("xln_a0", [128, NJ * TOK], BF16)
    ln_tmp = alloc_ln_tmp(mk)
    taug = mk.asbuf("taug", [33, 2 * 2048], BF16)
    mk.emit("G", lambda e: e.memset(taug[:, :], 1.0), writes=[("taug", 0), ("taug", 1)])
    qbuf = mk.asbuf("qbuf_a0", [128, 2 * TOK], BF16)
    kbuf = mk.asbuf("kbuf_a0", [128, 2 * TOK], BF16)
    vbuf = mk.asbuf("vbuf_a0", [128, 2 * TOK], BF16)
    bufs = {0: qbuf, 1: kbuf, 2: vbuf}
    xtm0 = mk.asbuf("xtm_a0", [128, 2 * 1024], F32)

    for Q in range(4):
        rk = lambda j, c, Q=Q: ("res", Q, j, c)
        emit_loadx_fm(mk, x_in[Q * TOK:(Q + 1) * TOK, :], res, rk, eyef, "lx%d" % Q, xtm=xtm0)
        emit_ln_fm(mk, consts, ln_tmp, "lnA%d" % Q, res, rk,
                   dst_fn=lambda j, c: (xln[:, j * TOK + c * 512: j * TOK + c * 512 + 512],
                                        ("xlnA", Q, j, c)))
        if Q == 0:
            for j in range(NJ):
                mk.emit("Yq", (lambda j=j, rk=rk: lambda e: e.dma_start(
                    out=xT_out[j * 128:(j + 1) * 128, :], in_=res[:, j * TOK:(j + 1) * TOK]))(),
                    reads=[rk(j, c) for c in range(NC4)], writes=[("resst", j)])
        for h in range(NJ):
            hpar = h % 2
            projs = (0, 1, 2) if Q == 0 else (1, 2)
            for p in projs:
                tpar = (h * 3 + p) % 2
                xkeys = [("xlnA", Q, h, c) for c in range(NC4)]

                def mm1(e, h=h, p=p):
                    last = None
                    for c in range(NC4):
                        last = e.matmul(
                            mk.pool[0:32, c * 512:c * 512 + 512],
                            vTs[:, (p * NJ + h) * 32:(p * NJ + h) * 32 + 32],
                            xln[:, h * TOK + c * 512: h * TOK + c * 512 + 512],
                            start=True, stop=True)
                    return last
                mk.emit("T", mm1, reads=xkeys + [("c", "vTs")] +
                        [("pb", i) for i in range(4)],
                        writes=[("pb", i) for i in range(4)])
                mk.emit("V", (lambda tpar=tpar: lambda e: e.tensor_copy(
                    taug[0:32, tpar * 2048:(tpar + 1) * 2048],
                    mk.pool[0:32, 0:2048]))(),
                    reads=[("pb", i) for i in range(4)], writes=[("taug", tpar)])
                if p < 2:
                    def mm2(e, h=h, p=p, tpar=tpar):
                        last = None
                        for c in range(NC4):
                            last = e.matmul(
                                mk.pool[:, 2048 + c * 512: 2048 + c * 512 + 512],
                                uTbs[0:33, (p * NJ + h) * 128:(p * NJ + h) * 128 + 128],
                                taug[0:33, tpar * 2048 + c * 512: tpar * 2048 + c * 512 + 512],
                                start=True, stop=True)
                        return last
                    mk.emit("T", mm2, reads=[("taug", tpar), ("c", "uTbs")] +
                            [("pb", 4 + i) for i in range(4)],
                            writes=[("pb", 4 + i) for i in range(4)])
                    mk.emit("S", (lambda p=p, hpar=hpar: lambda e: e.copy(
                        bufs[p][:, hpar * TOK:(hpar + 1) * TOK],
                        mk.pool[:, 2048:4096]))(),
                        reads=[("pb", 4 + i) for i in range(4)],
                        writes=[("ob", p, hpar, c) for c in range(NC4)])
                else:
                    def vmm(e, h=h, tpar=tpar):
                        last = None
                        for tt in range(NTT):
                            last = e.matmul(
                                mk.pool[:, 2048 + tt * 128: 2048 + tt * 128 + 128],
                                taug[0:33, tpar * 2048 + tt * 128: tpar * 2048 + tt * 128 + 128],
                                uTbs[0:33, (2 * NJ + h) * 128:(2 * NJ + h) * 128 + 128],
                                start=True, stop=True)
                        return last
                    mk.emit("T", vmm, reads=[("taug", tpar), ("c", "uTbs")] +
                            [("pb", 4 + i) for i in range(4)],
                            writes=[("pb", 4 + i) for i in range(4)])
                    mk.emit("S", (lambda hpar=hpar: lambda e: e.copy(
                        bufs[2][:, hpar * TOK:(hpar + 1) * TOK],
                        mk.pool[:, 2048:4096]))(),
                        reads=[("pb", 4 + i) for i in range(4)],
                        writes=[("ob", 2, hpar, c) for c in range(NC4)])
            for p in projs:
                if p == 0:
                    mk.emit("Yq", (lambda h=h, hpar=hpar: lambda e: e.dma_start(
                        out=qT_out[h * 128:(h + 1) * 128, :],
                        in_=bufs[0][:, hpar * TOK:(hpar + 1) * TOK]))(),
                        reads=[("ob", 0, hpar, c) for c in range(NC4)], writes=[("obst", 0, Q, h)])
                elif p == 1:
                    mk.emit("Yq", (lambda h=h, hpar=hpar, Q=Q: lambda e: e.dma_start(
                        out=kTf_out[h * 128:(h + 1) * 128, Q * TOK:(Q + 1) * TOK],
                        in_=bufs[1][:, hpar * TOK:(hpar + 1) * TOK]))(),
                        reads=[("ob", 1, hpar, c) for c in range(NC4)], writes=[("obst", 1, Q, h)])
                else:
                    mk.emit("Yq", (lambda h=h, hpar=hpar, Q=Q: lambda e: e.dma_start(
                        out=vf_out[Q * TOK:(Q + 1) * TOK, :].rearrange("(tt p) d -> p tt d", p=128)[:, :, h * 128:(h + 1) * 128],
                        in_=bufs[2][:, hpar * TOK:(hpar + 1) * TOK].rearrange("p (tt f) -> p tt f", f=128)))(),
                        reads=[("ob", 2, hpar, c) for c in range(NC4)], writes=[("obst", 2, Q, h)])
    mk.wait_only("Y", [("obst", p, Q, h) for p in range(3) for Q in range(4) for h in range(NJ)
                       if (p != 0 or Q == 0)] + [("resst", j) for j in range(NJ)])



def build_M1():
    mk = MK()
    io_0 = {
        "x": mk.dram("x", [S, D], F32, kind="ExternalInput").ap(),
        "vT3": mk.dram("vT3", [3, NJ, 128, 32], F32, kind="ExternalInput").ap(),
        "uTb3": mk.dram("uTb3", [3, NJ, 33, 128], F32, kind="ExternalInput").ap(),
        "qTs": mk.dram("qTs", [D, TOK], BF16).ap(),
        "kTfs": mk.dram("kTfs", [D, S], BF16).ap(),
        "vfs": mk.dram("vfs", [S, D], BF16).ap(),
        "xTs": mk.dram("xTs", [D, TOK], F32).ap(),
    }
    io_a = {
        "xT": io_0["xTs"],
        "qT": io_0["qTs"],
        "kTf": io_0["kTfs"],
        "vf": io_0["vfs"],
        "x1_out": mk.dram("x1s", [D, TOK], F32).ap(),
    }
    x2s = mk.dram("x2s", [D, TOK], F32).ap()
    io_b = {
        "xT": io_a["x1_out"],
        "gwT": mk.dram("gwT", [D, NE], F32, kind="ExternalInput").ap(),
        "gb": mk.dram("gb", [NE, 1], F32, kind="ExternalInput").ap(),
        "w1": mk.dram("w1", [NE, D, 2048], F32, kind="ExternalInput").ap(),
        "b1r": mk.dram("b1r", [128, NE * NHC], F32, kind="ExternalInput").ap(),
        "w2": mk.dram("w2", [NE, 2048, D], F32, kind="ExternalInput").ap(),
        "b2": mk.dram("b2", [NE, D], F32, kind="ExternalInput").ap(),
        "x2_out": x2s,
        "x2_out2": mk.dram("x2T", [D, TOK], F32, kind="ExternalOutput").ap(),
    }
    io_c = {
        "xT": x2s,
        "wqkv": mk.dram("wqkv", [D, 3 * D], F32, kind="ExternalInput").ap(),
        "cos2": mk.dram("cos2", [128, TOK], F32, kind="ExternalInput").ap(),
        "sin2": mk.dram("sin2", [128, TOK], F32, kind="ExternalInput").ap(),
        "prot": mk.dram("prot", [128, 128], F32, kind="ExternalInput").ap(),
        "qT": mk.dram("qT1", [D, TOK], BF16, kind="ExternalOutput").ap(),
        "kT": mk.dram("kT1", [D, TOK], BF16, kind="ExternalOutput").ap(),
        "v": mk.dram("v1", [TOK, D], BF16, kind="ExternalOutput").ap(),
    }
    eyef_in = mk.dram("eyef", [128, 128], F32, kind="ExternalInput").ap()
    mk.psum_pool()
    consts = emit_consts(mk)
    eyef = mk.sbuf("eyef_s", [128, 128], F32)
    mk.emit("Yq", lambda e: e.dma_start(out=eyef[:], in_=eyef_in[:, :]), writes=[("c", "eyef")])
    mk.init_arena()
    emit_phase_A0(mk, consts, eyef, io_0)
    mk.arena_reset()
    emit_phase_A2(mk, consts, io_a)
    mk.arena_reset()
    emit_phase_B(mk, consts, eyef, io_b)
    mk.arena_reset()
    emit_phase_C1(mk, consts, eyef, io_c)
    return mk




def build_M2():
    mk = MK()
    x3s = mk.dram("x3s", [D, TOK], F32).ap()
    io_c = {
        "xT": mk.dram("xT", [D, TOK], F32, kind="ExternalInput").ap(),
        "qT": mk.dram("qT", [D, TOK], BF16, kind="ExternalInput").ap(),
        "kTf": mk.dram("kTf", [D, S], BF16, kind="ExternalInput").ap(),
        "vf": mk.dram("vf", [S, D], BF16, kind="ExternalInput").ap(),
        "wout": mk.dram("wout", [D, D], F32, kind="ExternalInput").ap(),
        "x3_out": x3s,
        "x3_out2": mk.dram("x3T", [D, TOK], F32, kind="ExternalOutput").ap(),
    }
    io_d = {
        "xT": x3s,
        "nwT3": mk.dram("nwT3", [3, D, 15], F32, kind="ExternalInput").ap(),
        "nb3": mk.dram("nb3", [15, 3], F32, kind="ExternalInput").ap(),
        "w1f": mk.dram("w1f", [3, NLP, D, 128], F32, kind="ExternalInput").ap(),
        "b1f": mk.dram("b1f", [3, 128, NLP], F32, kind="ExternalInput").ap(),
        "w2f": mk.dram("w2f", [3, NLP, 128, D], F32, kind="ExternalInput").ap(),
        "b2f": mk.dram("b2f", [3, 16, D], F32, kind="ExternalInput").ap(),
        "selm": mk.dram("selm", [16, 16 * 64], F32, kind="ExternalInput").ap(),
        "qT": mk.dram("qT1", [D, TOK], BF16, kind="ExternalOutput").ap(),
        "kT": mk.dram("kT1", [D, TOK], BF16, kind="ExternalOutput").ap(),
        "v": mk.dram("v1", [TOK, D], BF16, kind="ExternalOutput").ap(),
    }
    eyef_in = mk.dram("eyef", [128, 128], F32, kind="ExternalInput").ap()
    mk.psum_pool()
    consts = emit_consts(mk)
    eyef = mk.sbuf("eyef_s", [128, 128], F32)
    mk.emit("Yq", lambda e: e.dma_start(out=eyef[:], in_=eyef_in[:, :]), writes=[("c", "eyef")])
    mk.init_arena()
    emit_phase_C2(mk, consts, eyef, io_c)
    mk.arena_reset()
    emit_phase_D1(mk, consts, eyef, io_d)
    return mk



_NC_CACHE = {}


def _get_nc(name, builder):
    if name not in _NC_CACHE:
        _NC_CACHE[name] = builder().finalize()
    return _NC_CACHE[name]


def phaseA(inp):
    x = np.asarray(inp["L0_attn_x"], np.float32)
    eye = np.eye(128, dtype=np.float32)
    vT3 = np.stack([np.asarray(inp["L0_attn_V" + p], np.float32).transpose(0, 2, 1) for p in "qkv"])
    uTb3 = np.stack([
        np.concatenate([np.asarray(inp["L0_attn_U" + p], np.float32).transpose(0, 2, 1),
                        np.asarray(inp["L0_attn_b" + p], np.float32)[:, None, :]], axis=1)
        for p in "qkv"])
    nc1 = _get_nc("A1", build_A1)
    in_maps = []
    for c in range(NCORES):
        b, q0 = c // 4, (c % 4) * TOK
        in_maps.append({"x": np.ascontiguousarray(x[b, q0:q0 + TOK]),
                        "vT3": vT3, "uTb3": uTb3, "eyef": eye})
    r1 = run_bass_kernel_spmd(nc1, in_maps, core_ids=list(range(NCORES))).results
    kTf = [np.ascontiguousarray(np.concatenate([r1[4 * b + i]["kT"] for i in range(4)], axis=1)) for b in range(B)]
    vf = [np.ascontiguousarray(np.concatenate([r1[4 * b + i]["v"] for i in range(4)], axis=0)) for b in range(B)]
    nc2 = _get_nc("A2", build_A2)
    in_maps2 = []
    for c in range(NCORES):
        b = c // 4
        in_maps2.append({"xT": r1[c]["xT"], "qT": r1[c]["qT"], "kTf": kTf[b], "vf": vf[b]})
    r2 = run_bass_kernel_spmd(nc2, in_maps2, core_ids=list(range(NCORES))).results
    return [r2[c]["x1T"] for c in range(NCORES)]


def phaseB(inp, x1):
    eye = np.eye(128, dtype=np.float32)
    gwT = np.ascontiguousarray(np.asarray(inp["L0_ffn_gate_w"], np.float32).T)
    gb = np.asarray(inp["L0_ffn_gate_b"], np.float32).reshape(NE, 1)
    w1 = np.ascontiguousarray(np.asarray(inp["L0_ffn_w1"], np.float32))
    b1r = np.ascontiguousarray(np.asarray(inp["L0_ffn_b1"], np.float32)
                               .reshape(NE, NHC, 128).transpose(2, 0, 1).reshape(128, NE * NHC))
    w2 = np.ascontiguousarray(np.asarray(inp["L0_ffn_w2"], np.float32))
    b2 = np.ascontiguousarray(np.asarray(inp["L0_ffn_b2"], np.float32))
    ncb = _get_nc("B", build_B_full)
    in_maps = [{"xT": x1[c], "gwT": gwT, "gb": gb, "w1": w1, "b1r": b1r,
                "w2": w2, "b2": b2, "eyef": eye} for c in range(NCORES)]
    r = run_bass_kernel_spmd(ncb, in_maps, core_ids=list(range(NCORES))).results
    return [r[c]["x2T"] for c in range(NCORES)]


def _rope_tables():
    inv = 1.0 / (10000.0 ** (np.arange(0, 64, 2, dtype=np.float32) / 64.0))
    t = np.arange(S, dtype=np.float32)
    fr = np.outer(t, inv)
    emb = np.concatenate([fr, fr], axis=-1)
    return np.cos(emb).T.astype(np.float32), np.sin(emb).T.astype(np.float32)


def _prot():
    P = np.zeros((128, 128), np.float32)
    for base in (0, 64):
        for i in range(32):
            P[base + 32 + i, base + i] = -1.0
            P[base + i, base + 32 + i] = 1.0
    return P


def phaseC(inp, x2):
    eye = np.eye(128, dtype=np.float32)
    wqkv = np.ascontiguousarray(np.asarray(inp["L1_attn_Wqkv"], np.float32))
    wout = np.ascontiguousarray(np.asarray(inp["L1_attn_Wout"], np.float32))
    cosT, sinT = _rope_tables()
    prot = _prot()
    nc1 = _get_nc("C1", build_C1)
    in_maps = []
    for c in range(NCORES):
        q0 = (c % 4) * TOK
        cos2 = np.ascontiguousarray(np.concatenate([cosT[:, q0:q0 + TOK]] * 2, axis=0))
        sin2 = np.ascontiguousarray(np.concatenate([sinT[:, q0:q0 + TOK]] * 2, axis=0))
        in_maps.append({"xT": x2[c], "wqkv": wqkv, "cos2": cos2, "sin2": sin2,
                        "prot": prot, "eyef": eye})
    r1 = run_bass_kernel_spmd(nc1, in_maps, core_ids=list(range(NCORES))).results
    kTf = [np.ascontiguousarray(np.concatenate([r1[4 * b + i]["kT"] for i in range(4)], axis=1)) for b in range(B)]
    vf = [np.ascontiguousarray(np.concatenate([r1[4 * b + i]["v"] for i in range(4)], axis=0)) for b in range(B)]
    nc2 = _get_nc("C2", build_C2)
    in_maps2 = []
    for c in range(NCORES):
        b = c // 4
        in_maps2.append({"xT": x2[c], "qT": r1[c]["qT"], "kTf": kTf[b], "vf": vf[b],
                         "wout": wout, "eyef": eye})
    r2 = run_bass_kernel_spmd(nc2, in_maps2, core_ids=list(range(NCORES))).results
    return [r2[c]["x3T"] for c in range(NCORES)]


def phaseD(inp, x3):
    eye = np.eye(128, dtype=np.float32)
    f32 = np.float32
    nwT3 = np.stack([np.ascontiguousarray(np.asarray(inp["L1_ffn_%s_node_w" % p], f32).T) for p in "qkv"])
    nb3 = np.stack([np.asarray(inp["L1_ffn_%s_node_b" % p], f32) for p in "qkv"], axis=1)
    w1f = np.stack([np.asarray(inp["L1_ffn_%s_w1" % p], f32).reshape(NLP, 2, D, 64)
                    .transpose(0, 2, 1, 3).reshape(NLP, D, 128) for p in "qkv"])
    b1f = np.stack([np.asarray(inp["L1_ffn_%s_b1" % p], f32).reshape(NLP, 128).T for p in "qkv"])
    w2f = np.stack([np.asarray(inp["L1_ffn_%s_w2" % p], f32).reshape(NLP, 128, D) for p in "qkv"])
    b2f = np.stack([np.asarray(inp["L1_ffn_%s_b2" % p], f32) for p in "qkv"])
    selm = np.zeros((16, 16 * 64), f32)
    for l in range(16):
        selm[l, l * 64:(l + 1) * 64] = 1.0
    nc1 = _get_nc("D1", build_D1)
    in_maps = [{"xT": x3[c], "nwT3": nwT3, "nb3": np.ascontiguousarray(nb3),
                "w1f": np.ascontiguousarray(w1f), "b1f": np.ascontiguousarray(b1f),
                "w2f": np.ascontiguousarray(w2f), "b2f": np.ascontiguousarray(b2f),
                "selm": selm, "eyef": eye} for c in range(NCORES)]
    r1 = run_bass_kernel_spmd(nc1, in_maps, core_ids=list(range(NCORES))).results
    kTf = [np.ascontiguousarray(np.concatenate([r1[4 * b + i]["kT"] for i in range(4)], axis=1)) for b in range(B)]
    vf = [np.ascontiguousarray(np.concatenate([r1[4 * b + i]["v"] for i in range(4)], axis=0)) for b in range(B)]
    nc2 = _get_nc("D2", build_D2)
    in_maps2 = []
    for c in range(NCORES):
        b = c // 4
        in_maps2.append({"xT": x3[c], "qT": r1[c]["qT"], "kTf": kTf[b], "vf": vf[b], "eyef": eye})
    r2 = run_bass_kernel_spmd(nc2, in_maps2, core_ids=list(range(NCORES))).results
    return [r2[c]["xout"] for c in range(NCORES)]


_ALL_INPUT_NAMES = (
    "L0_attn_x", "L0_attn_Uq", "L0_attn_Vq", "L0_attn_bq", "L0_attn_Uk", "L0_attn_Vk",
    "L0_attn_bk", "L0_attn_Uv", "L0_attn_Vv", "L0_attn_bv", "L0_ffn_gate_w",
    "L0_ffn_gate_b", "L0_ffn_w1", "L0_ffn_b1", "L0_ffn_w2", "L0_ffn_b2",
    "L1_attn_Wqkv", "L1_attn_Wout",
    "L1_ffn_q_node_w", "L1_ffn_q_node_b", "L1_ffn_q_w1", "L1_ffn_q_b1", "L1_ffn_q_w2",
    "L1_ffn_q_b2", "L1_ffn_k_node_w", "L1_ffn_k_node_b", "L1_ffn_k_w1", "L1_ffn_k_b1",
    "L1_ffn_k_w2", "L1_ffn_k_b2", "L1_ffn_v_node_w", "L1_ffn_v_node_b", "L1_ffn_v_w1",
    "L1_ffn_v_b1", "L1_ffn_v_w2", "L1_ffn_v_b2",
)


def run_M1(inp):
    x = np.asarray(inp["L0_attn_x"], np.float32)
    vT3 = np.stack([np.asarray(inp["L0_attn_V" + p], np.float32).transpose(0, 2, 1) for p in "qkv"])
    uTb3 = np.stack([
        np.concatenate([np.asarray(inp["L0_attn_U" + p], np.float32).transpose(0, 2, 1),
                        np.asarray(inp["L0_attn_b" + p], np.float32)[:, None, :]], axis=1)
        for p in "qkv"])
    eye = np.eye(128, dtype=np.float32)
    gwT = np.ascontiguousarray(np.asarray(inp["L0_ffn_gate_w"], np.float32).T)
    gb = np.asarray(inp["L0_ffn_gate_b"], np.float32).reshape(NE, 1)
    w1 = np.ascontiguousarray(np.asarray(inp["L0_ffn_w1"], np.float32))
    b1r = np.ascontiguousarray(np.asarray(inp["L0_ffn_b1"], np.float32)
                               .reshape(NE, NHC, 128).transpose(2, 0, 1).reshape(128, NE * NHC))
    w2 = np.ascontiguousarray(np.asarray(inp["L0_ffn_w2"], np.float32))
    b2 = np.ascontiguousarray(np.asarray(inp["L0_ffn_b2"], np.float32))
    wqkv = np.ascontiguousarray(np.asarray(inp["L1_attn_Wqkv"], np.float32))
    cosT, sinT = _rope_tables()
    prot = _prot()
    ncm = _get_nc("M1", build_M1)
    in_maps = []
    for c in range(NCORES):
        b, q0 = c // 4, (c % 4) * TOK
        cos2 = np.ascontiguousarray(np.concatenate([cosT[:, q0:q0 + TOK]] * 2, axis=0))
        sin2 = np.ascontiguousarray(np.concatenate([sinT[:, q0:q0 + TOK]] * 2, axis=0))
        x_rot = np.ascontiguousarray(np.concatenate([x[b, q0:], x[b, :q0]], axis=0))
        in_maps.append({"x": x_rot, "vT3": vT3, "uTb3": uTb3,
                        "gwT": gwT, "gb": gb, "w1": w1, "b1r": b1r, "w2": w2, "b2": b2,
                        "wqkv": wqkv, "cos2": cos2, "sin2": sin2, "prot": prot, "eyef": eye})
    return run_bass_kernel_spmd(ncm, in_maps, core_ids=list(range(NCORES))).results


def kernel_merged(inputs):
    inp = inputs
    rm = run_M1(inp)
    x2 = [rm[c]["x2T"] for c in range(NCORES)]
    r1b = [{"qT": rm[c]["qT1"], "kT": rm[c]["kT1"], "v": rm[c]["v1"]} for c in range(NCORES)]
    rm2 = run_M2(inp, x2, r1b)
    x3 = [rm2[c]["x3T"] for c in range(NCORES)]
    r2b = [{"qT": rm2[c]["qT1"], "kT": rm2[c]["kT1"], "v": rm2[c]["v1"]} for c in range(NCORES)]
    x4 = run_D2(inp, x3, r2b)
    out = np.zeros((B, S, D), np.float32)
    for c in range(NCORES):
        b, q0 = c // 4, (c % 4) * TOK
        out[b, q0:q0 + TOK] = x4[c]
    return out


def run_M2(inp, x2, r1):
    eye = np.eye(128, dtype=np.float32)
    f32 = np.float32
    wout = np.ascontiguousarray(np.asarray(inp["L1_attn_Wout"], f32))
    kTf = [np.ascontiguousarray(np.concatenate([r1[4 * b + i]["kT"] for i in range(4)], axis=1)) for b in range(B)]
    vf = [np.ascontiguousarray(np.concatenate([r1[4 * b + i]["v"] for i in range(4)], axis=0)) for b in range(B)]
    nwT3 = np.stack([np.ascontiguousarray(np.asarray(inp["L1_ffn_%s_node_w" % p], f32).T) for p in "qkv"])
    nb3 = np.ascontiguousarray(np.stack([np.asarray(inp["L1_ffn_%s_node_b" % p], f32) for p in "qkv"], axis=1))
    w1f = np.ascontiguousarray(np.stack([np.asarray(inp["L1_ffn_%s_w1" % p], f32).reshape(NLP, 2, D, 64)
                                         .transpose(0, 2, 1, 3).reshape(NLP, D, 128) for p in "qkv"]))
    b1f = np.ascontiguousarray(np.stack([np.asarray(inp["L1_ffn_%s_b1" % p], f32).reshape(NLP, 128).T for p in "qkv"]))
    w2f = np.ascontiguousarray(np.stack([np.asarray(inp["L1_ffn_%s_w2" % p], f32).reshape(NLP, 128, D) for p in "qkv"]))
    b2f = np.ascontiguousarray(np.stack([np.asarray(inp["L1_ffn_%s_b2" % p], f32) for p in "qkv"]))
    selm = np.zeros((16, 16 * 64), f32)
    for l in range(16):
        selm[l, l * 64:(l + 1) * 64] = 1.0
    ncm = _get_nc("M2", build_M2)
    in_maps = []
    for c in range(NCORES):
        b = c // 4
        in_maps.append({"xT": x2[c], "qT": r1[c]["qT"], "kTf": kTf[b], "vf": vf[b], "wout": wout,
                        "nwT3": nwT3, "nb3": nb3, "w1f": w1f, "b1f": b1f, "w2f": w2f, "b2f": b2f,
                        "selm": selm, "eyef": eye})
    return run_bass_kernel_spmd(ncm, in_maps, core_ids=list(range(NCORES))).results


def run_D2(inp, x3, r1):
    eye = np.eye(128, dtype=np.float32)
    kTf = [np.ascontiguousarray(np.concatenate([r1[4 * b + i]["kT"] for i in range(4)], axis=1)) for b in range(B)]
    vf = [np.ascontiguousarray(np.concatenate([r1[4 * b + i]["v"] for i in range(4)], axis=0)) for b in range(B)]
    nc2 = _get_nc("D2", build_D2)
    in_maps2 = []
    for c in range(NCORES):
        b = c // 4
        in_maps2.append({"xT": x3[c], "qT": r1[c]["qT"], "kTf": kTf[b], "vf": vf[b], "eyef": eye})
    r2 = run_bass_kernel_spmd(nc2, in_maps2, core_ids=list(range(NCORES))).results
    return [r2[c]["xout"] for c in range(NCORES)]


def phaseC2_only(inp, x2, r1):
    eye = np.eye(128, dtype=np.float32)
    wout = np.ascontiguousarray(np.asarray(inp["L1_attn_Wout"], np.float32))
    kTf = [np.ascontiguousarray(np.concatenate([r1[4 * b + i]["kT"] for i in range(4)], axis=1)) for b in range(B)]
    vf = [np.ascontiguousarray(np.concatenate([r1[4 * b + i]["v"] for i in range(4)], axis=0)) for b in range(B)]
    nc2 = _get_nc("C2", build_C2)
    in_maps2 = []
    for c in range(NCORES):
        b = c // 4
        in_maps2.append({"xT": x2[c], "qT": r1[c]["qT"], "kTf": kTf[b], "vf": vf[b],
                         "wout": wout, "eyef": eye})
    r2 = run_bass_kernel_spmd(nc2, in_maps2, core_ids=list(range(NCORES))).results
    return [r2[c]["x3T"] for c in range(NCORES)]


def kernel(**inputs):
    assert all(n in inputs for n in _ALL_INPUT_NAMES)
    return kernel_merged(inputs)


def kernel_old(**inputs):
    assert all(n in inputs for n in _ALL_INPUT_NAMES)
    x1 = phaseA(inputs)
    x2 = phaseB(inputs, x1)
    x3 = phaseC(inputs, x2)
    x4 = phaseD(inputs, x3)
    out = np.zeros((B, S, D), np.float32)
    for c in range(NCORES):
        b, q0 = c // 4, (c % 4) * TOK
        out[b, q0:q0 + TOK] = x4[c]
    return out
